# version 07
# baseline (speedup 1.0000x reference)
import math
import numpy as np
from contextlib import ExitStack

import concourse.bass as bass
import concourse.mybir as mybir
from concourse.ap import AP
from concourse.bass_utils import run_bass_kernel_spmd
from concourse.aot_env import allow_ici_all_to_all
import ml_dtypes

F32 = mybir.dt.float32
BF16 = mybir.dt.bfloat16
AF = mybir.ActivationFunctionType
ALU = mybir.AluOpType
AX = mybir.AxisListType

NCORES = 8
B, S, D = 16, 2048, 1024
BC = 2
H1, DH1 = 16, 64
H2, DH2 = 8, 128
RP = 192
NPOS = 2 * S - 1
WIN = S + 127
SL = S // NCORES
TS = B
NT1 = S // 128
NT2 = (TS * SL) // 128
EPS = 1e-5
SC2 = DH2 ** -0.5

ENGS = ("pe", "dve", "act", "pool", "sp")
SEMS = ("Lg", "Lw", "Lr", "Lp0", "Lp1", "Lo", "Lx0", "Lx1", "Ls", "St0", "St1",
        "Wh", "Bt0", "Bt1", "A2", "pe", "dve", "act", "pool", "cc", "bar")


def _positions_T():
    d = np.arange(-S + 1, S, dtype=np.float64)
    n = RP // 6
    absd = np.abs(d)[:, None]
    max_range = math.log(S) / math.log(2.0)
    half_life = 2.0 ** np.linspace(3.0, max_range, n)
    feat_exp = np.exp(-math.log(2.0) / half_life[None, :] * absd)
    center_widths = 2.0 ** np.arange(1, n + 1, dtype=np.float64) - 1.0
    feat_cm = (center_widths[None, :] > absd).astype(np.float64)
    stddev = S / (2.0 * n)
    start_mean = S / n
    mean = np.linspace(start_mean, float(S), n)[None, :]
    conc = (mean / stddev) ** 2
    rate = mean / stddev ** 2
    with np.errstate(divide="ignore", invalid="ignore"):
        logd = np.where(absd > 0, np.log(np.maximum(absd, 1e-300)), 0.0)
        log_unnorm = (conc - 1.0) * logd - rate * absd
    log_unnorm = np.where(absd > 0, log_unnorm, -np.inf)
    lgam = np.array([math.lgamma(c) for c in conc[0]])[None, :]
    log_norm = lgam - conc * np.log(rate)
    probs = np.exp(log_unnorm - log_norm) + 1e-08
    feat_gamma = probs / np.max(probs, axis=-1, keepdims=True)
    emb = np.concatenate([feat_exp, feat_cm, feat_gamma], axis=-1)
    emb = np.concatenate([emb, np.sign(d)[:, None] * emb], axis=-1)
    return np.ascontiguousarray(emb.T.astype(np.float32))


class MK:
    def __init__(self, dev=None):
        self.dev = dev or {}
        self.nc = bass.Bass("TRN2", target_bir_lowering=False)
        self.q = {e: [] for e in ENGS}
        self.cnt = {s: 0 for s in SEMS}
        self.handles = {}
        self.dr = {}
        self.sb = {}
        self.ps = {}
        self.st = ExitStack()
        self.tr_guard = 0

    def on(self, eng, fn=None, wait=(), inc=None, incn=1):
        wait = list(wait)
        if fn is not None and eng in ("act", "dve"):
            assert inc in (None, eng)
            wait.append((eng, self.cnt[eng]))
            inc = eng
            incn = 1
        if inc is not None:
            self.cnt[inc] += incn
        tgt = self.cnt[inc] if inc is not None else None
        DMAS = ("Lg", "Lw", "Lr", "Lp0", "Lp1", "Lo", "Lx0", "Lx1", "Ls", "St0",
                "St1", "Wh", "Bt0", "Bt1", "A2")
        wait = tuple((s, self.cnt[s] if s in DMAS else v) for (s, v) in wait if v)
        h = self.handles

        def thunk(e, _fn=fn, _wait=wait, _inc=inc, _incn=incn):
            for s, v in _wait:
                e.wait_ge(h[s], v)
            inst = _fn(e) if _fn is not None else None
            if _inc is not None:
                if inst is None:
                    e.sem_inc(h[_inc], _incn)
                else:
                    inst.then_inc(h[_inc], _incn)
        self.q[eng].append(thunk)
        return tgt

    def dma(self, eng, sem, out, in_, wait=(), slow=False):
        return self.on(eng, lambda e, o=out, i=in_, s=slow: e.dma_start(
            out=o, in_=i, allow_slow_non_contiguous=s), wait=wait, inc=sem, incn=16)

    def barrier(self):
        for e in ENGS:
            self.on(e, None, inc="bar")
        tgt = self.cnt["bar"]
        for e in ENGS:
            self.on(e, None, wait=[("bar", tgt)])
        return tgt

    def din(self, name, shape, dtype=F32):
        self.dr[name] = self.nc.dram_tensor(name, shape, dtype, kind="ExternalInput")
        return self.dr[name]

    def dout(self, name, shape, dtype=F32):
        self.dr[name] = self.nc.dram_tensor(name, shape, dtype, kind="ExternalOutput")
        return self.dr[name]

    def dint(self, name, shape, dtype=F32):
        self.dr[name] = self.nc.dram_tensor(name, shape, dtype)
        return self.dr[name]

    def SB(self, name, shape, dtype, stack=None):
        t = (stack or self.st).enter_context(self.nc.sbuf_tensor(name, shape, dtype))
        self.sb[name] = t
        return t

    def PSm(self, name, shape, dtype=F32, stack=None):
        t = (stack or self.st).enter_context(self.nc.psum_tensor(name, shape, dtype))
        self.ps[name] = t
        return t

    def trblk(self, src, dst, wait=()):
        tr = self.ps["tr"]
        eye = self.sb["s_eye"]
        self.on("pe", lambda e, s=src, t=tr[:, 0:128], y=eye[:, :]: e.transpose(t, s, y),
                wait=list(wait) + [("dve", self.tr_guard)], inc="pe")
        pe0 = self.cnt["pe"]
        self.on("dve", lambda e, d=dst, t=tr[:, 0:128]: e.tensor_copy(d, t),
                wait=[("pe", pe0)], inc="dve")
        self.tr_guard = self.cnt["dve"]
        return self.tr_guard

    def declare(self):
        din, dout, dint = self.din, self.dout, self.dint
        din("x0", [BC * S, D])
        for nm in ("aWq", "aWk", "aWv", "aWo", "bwq", "bwk", "bwv", "bwout",
                   "fwf", "fuf", "fwi", "fui", "fwc", "fuc", "fwo", "fuo"):
            din(nm, [D, D])
        for nm in ("abq", "abk", "abv", "abo", "bbout", "fbf", "fbi", "fbc",
                   "fbo", "finc"):
            din(nm, [D])
        din("adecay", [H1, 2]); din("amix", [H1, 2])
        din("bwrel", [RP, D]); din("breb", [H2, DH2]); din("brpb", [H2, DH2])
        din("eye", [128, 128], BF16)
        din("posT", [RP, NPOS])
        din("absd", [128, 128])
        dout("y", [TS * SL, D])
        dint("a2a_in", [NCORES, BC, SL, D])
        dint("a2a_out", [NCORES, BC, SL, D])
        dint("win_hbm", [127 + S * WIN + WIN], BF16)
        for tname, tshape in self.dev.get("taps", {}).items():
            dout(tname, tshape)
        if self.dev.get("devffn"):
            din("devffn", [NCORES, BC, SL, D])

    def alloc_global(self):
        SB = self.SB
        SB("s_eye", [128, 128], BF16)
        SB("sq", [128, D], F32)
        SB("lnt", [128, D], BF16)
        SB("stats", [128, 16], F32)
        SB("onesb", [128, 512], BF16)
        SB("onesc", [128, 1], BF16)
        SB("repsb", [128, 512], F32)
        SB("scr", [128, 512], F32)
        SB("scr2", [128, 512], F32)

        SB("amixs2", [16, 2], F32)
        for nm in ("abq_t", "abk_t", "abv_t", "fbf_t", "fbi_t", "fbc_t", "finc_t"):
            SB(nm, [128, 8], F32)
        SB("breb_t", [128, H2], F32)
        SB("brpb_t", [128, H2], F32)
        SB("abo_bf", [1, D], BF16)
        SB("bbout_bf", [1, D], BF16)
        SB("fbo_bf", [1, D], BF16)
        SB("adp", [H1, 2], F32)
        SB("amixs", [H1, 2], F32)
        SB("alogT", [2, H1], F32)
        SB("alogTn", [2, H1], F32)
        SB("alogT2047", [2, H1], F32)
        SB("alogT127", [2, H1], F32)
        SB("amixT", [2, H1], F32)
        SB("alogF", [1, 2 * H1], F32)
        SB("amixF", [1, 2 * H1], F32)
        SB("alogFb", [1, 2 * H1], BF16)
        SB("amixFb", [1, 2 * H1], BF16)
        SB("s_absd", [128, 128], F32)
        SB("epsc", [128, 1], F32)
        SB("nb30", [128, 1], F32)

    def setup(self):
        sb, dr, dma, on = self.sb, self.dr, self.dma, self.on

        def col_ap(v):
            return AP(v.ap().tensor, 0, [[1, 128], [128, 8]])
        dma("pool", "Lg", sb["s_eye"][:, :], dr["eye"].ap())
        dma("pool", "Lg", sb["s_absd"][:, :], dr["absd"].ap())
        for nm, src in (("abq_t", "abq"), ("abk_t", "abk"), ("abv_t", "abv"),
                        ("fbf_t", "fbf"), ("fbi_t", "fbi"), ("fbc_t", "fbc"),
                        ("finc_t", "finc")):
            dma("pool", "Lg", sb[nm][:, :], col_ap(dr[src]), slow=True)
        for nm, src in (("abo_bf", "abo"), ("bbout_bf", "bbout"), ("fbo_bf", "fbo")):
            dma("pool", "Lg", sb[nm][0:1, :], AP(dr[src].ap().tensor, 0, [[D, 1], [1, D]]))
        dma("pool", "Lg", sb["breb_t"][:, :],
            AP(dr["breb"].ap().tensor, 0, [[1, DH2], [DH2, H2]]), slow=True)
        dma("pool", "Lg", sb["brpb_t"][:, :],
            AP(dr["brpb"].ap().tensor, 0, [[1, DH2], [DH2, H2]]), slow=True)
        dma("pool", "Lg", sb["adp"][:, :], dr["adecay"].ap())
        dma("pool", "Lg", sb["amixs"][:, :], dr["amix"].ap())
        lg = self.cnt["Lg"]
        on("dve", lambda e, o=sb["onesb"][:, :]: e.memset(o, 1.0), inc="dve")
        on("dve", lambda e, o=sb["onesc"][:, :]: e.memset(o, 1.0), inc="dve")
        on("dve", lambda e, o=sb["epsc"][:, :]: e.memset(o, EPS), inc="dve")
        on("dve", lambda e, o=sb["nb30"][:, :]: e.memset(o, -60.0), inc="dve")
        on("dve", lambda e, o=sb["adp"][:, :]: e.tensor_scalar(
            out=o, in0=o, scalar1=0.01, scalar2=0.9999, op0=ALU.max, op1=ALU.min),
            wait=[("Lg", lg)], inc="dve")
        dv = self.cnt["dve"]
        on("act", lambda e, o=sb["stats"][0:H1, 8:10], i=sb["adp"][:, :]: e.activation(
            out=o, in_=i, func=AF.Ln), wait=[("dve", dv)], inc="act")
        ln_ac = self.cnt["act"]
        on("dve", lambda e, o=sb["stats"][0:H1, 0:1], i=sb["amixs"][:, 0:1],
           j=sb["amixs"][:, 1:2]: e.tensor_tensor(out=o, in0=i, in1=j, op=ALU.subtract),
           wait=[("Lg", lg)], inc="dve")
        dvd = self.cnt["dve"]
        on("act", lambda e, o=sb["amixs2"][:, 0:1], i=sb["stats"][0:H1, 0:1]:
           e.activation(out=o, in_=i, func=AF.Sigmoid), wait=[("dve", dvd)], inc="act")
        on("act", lambda e, o=sb["amixs2"][:, 1:2], i=sb["stats"][0:H1, 0:1]:
           e.activation(out=o, in_=i, func=AF.Sigmoid, scale=-1.0), inc="act")
        ac2 = self.cnt["act"]
        on("dve", None, wait=[("act", ac2)], inc="dve")
        dv = self.cnt["dve"]
        for h in range(H1):
            for n in range(2):
                dma("pool", "Lg", sb["alogT"][n:n + 1, h:h + 1],
                    sb["stats"][h:h + 1, 8 + n:9 + n], wait=[("act", ln_ac), ("dve", dv)],
                    slow=True)
                dma("pool", "Lg", sb["amixT"][n:n + 1, h:h + 1],
                    sb["amixs2"][h:h + 1, n:n + 1], wait=[("dve", dv)], slow=True)
                dma("pool", "Lg", sb["alogF"][0:1, 2 * h + n:2 * h + n + 1],
                    sb["stats"][h:h + 1, 8 + n:9 + n], wait=[("act", ln_ac)], slow=True)
                dma("pool", "Lg", sb["amixF"][0:1, 2 * h + n:2 * h + n + 1],
                    sb["amixs2"][h:h + 1, n:n + 1], wait=[("dve", dv)], slow=True)
        lg = self.cnt["Lg"]
        on("dve", lambda e, o=sb["alogTn"][:, :], i=sb["alogT"][:, :]: e.tensor_scalar(
            out=o, in0=i, scalar1=-1.0, scalar2=None, op0=ALU.mult),
            wait=[("Lg", lg)], inc="dve")
        on("dve", lambda e, o=sb["alogT2047"][:, :], i=sb["alogT"][:, :]: e.tensor_scalar(
            out=o, in0=i, scalar1=2047.0, scalar2=None, op0=ALU.mult), inc="dve")
        on("dve", lambda e, o=sb["alogT127"][:, :], i=sb["alogT"][:, :]: e.tensor_scalar(
            out=o, in0=i, scalar1=127.0, scalar2=None, op0=ALU.mult), inc="dve")
        on("dve", lambda e, o=sb["alogFb"][:, :], i=sb["alogF"][:, :]: e.tensor_copy(o, i),
           wait=[("Lg", self.cnt["Lg"])], inc="dve")
        on("dve", lambda e, o=sb["amixFb"][:, :], i=sb["amixF"][:, :]: e.tensor_copy(o, i),
           inc="dve")
        self.on("pool", None, wait=[("Lg", self.cnt["Lg"])])
        if self.dev.get("dbg0"):
            dv = self.cnt["dve"]
            for nm, apx in (("t_amixs", sb["amixs"][:, :]), ("t_stats", sb["stats"][:, 0:16]),
                            ("t_amixs2", sb["amixs2"][:, :]),
                            ("t_alogF", sb["alogF"][:, :]), ("t_amixF", sb["amixF"][:, :]),
                            ("t_alogT", sb["alogT"][:, :]), ("t_amixT", sb["amixT"][:, :]),
                            ("t_adp", sb["adp"][:, :])):
                if nm in self.dr:
                    self.dma("pool", "A2", self.dr[nm].ap(), apx,
                             wait=[("dve", dv), ("Lg", self.cnt["Lg"])])
            self.on("sp", None, wait=[("A2", self.cnt["A2"])])
        self.barrier()

    def ln_tile(self, xt, lnxT_col_fn, wait):
        sb, on = self.sb, self.on
        c1 = 1.0 / D
        stats = sb["stats"]
        wait = list(wait) + [("dve", getattr(self, "ln_guard", 0))]
        on("dve", lambda e, o=stats[:, 0:1], i=xt: e.tensor_reduce(
            out=o, in_=i, axis=AX.X, op=ALU.add), wait=wait, inc="dve")
        on("act", lambda e, o=sb["sq"][:, :], i=xt: e.activation(
            out=o, in_=i, func=AF.Square), wait=wait, inc="act")
        ac0 = self.cnt["act"]
        on("dve", lambda e, o=stats[:, 1:2], i=sb["sq"][:, :]: e.tensor_reduce(
            out=o, in_=i, axis=AX.X, op=ALU.add), wait=[("act", ac0)], inc="dve")
        dv0 = self.cnt["dve"]
        on("dve", lambda e, o=stats[:, 2:3], i=stats[:, 0:1]: e.tensor_scalar(
            out=o, in0=i, scalar1=c1, scalar2=None, op0=ALU.mult),
            wait=[("dve", dv0), ("act", ac0)], inc="dve")
        on("dve", lambda e, o=stats[:, 3:4], i=stats[:, 2:3]: e.tensor_tensor(
            out=o, in0=i, in1=i, op=ALU.mult), inc="dve")
        on("dve", lambda e, o=stats[:, 4:5], i=stats[:, 1:2], j=stats[:, 3:4]:
            e.scalar_tensor_tensor(out=o, in0=i, scalar=c1, in1=j,
                                   op0=ALU.mult, op1=ALU.subtract),
            wait=[("act", getattr(self, "ln_act_guard", 0))], inc="dve")
        dv1 = self.cnt["dve"]
        self.ln_guard = dv1
        on("act", lambda e, o=stats[:, 5:6], i=stats[:, 4:5], b=sb["epsc"][:, 0:1]:
            e.activation(out=o, in_=i, func=AF.Sqrt, bias=b), wait=[("dve", dv1)], inc="act")
        self.ln_act_guard = self.cnt["act"]
        ac1 = self.cnt["act"]
        on("dve", lambda e, o=stats[:, 6:7], i=stats[:, 5:6]: e.reciprocal(out=o, in_=i),
           wait=[("act", ac1)], inc="dve")
        on("dve", lambda e, o=stats[:, 7:8], i=stats[:, 2:3], j=stats[:, 6:7]:
            e.tensor_tensor(out=o, in0=i, in1=j, op=ALU.mult), inc="dve")
        on("dve", lambda e, o=stats[:, 7:8], i=stats[:, 7:8]: e.tensor_scalar(
            out=o, in0=i, scalar1=-1.0, scalar2=None, op0=ALU.mult), inc="dve")
        dvr = self.cnt["dve"]
        lnt = sb["lnt"]
        on("act", lambda e, o=lnt[:, :], i=xt, r=stats[:, 6:7], b=stats[:, 7:8]:
            e.activation(out=o, in_=i, func=AF.Identity, scale=r, bias=b),
            wait=[("dve", dvr)], inc="act")
        ac2 = self.cnt["act"]
        on("dve", None, wait=[("act", ac2)], inc="dve")
        dv2 = self.cnt["dve"]
        for ft in range(8):
            self.trblk(lnt[:, ft * 128:(ft + 1) * 128], lnxT_col_fn(ft),
                       wait=[("dve", dv2)])
        return self.cnt["dve"]

    def attn_a(self, src="x0", export="a2a"):
        sb, dr, ps, on, dma = self.sb, self.dr, self.ps, self.on, self.dma
        stk = ExitStack()
        SB = lambda n, s, d: self.SB(n, s, d, stk)
        x = SB("x", [128, NT1 * D], F32)
        lnxT = SB("lnxT", [128, 8 * S], BF16)
        outT = SB("outT", [128, 8 * S], BF16)
        iota = SB("iota", [2, S], mybir.dt.uint16)
        tasc = SB("tasc", [2, S], BF16)
        tdesc = SB("tdesc", [2, S], BF16)
        mlo = SB("mlo", [2, 128], BF16)
        mup = SB("mup", [2, 128], BF16)
        diag = SB("diag", [128, H1 * 128], BF16)
        rden = SB("rden", [128, 512], BF16)
        wstg = SB("wstg", [128, D], BF16)
        wT = SB("wT", [128, D], BF16)
        bufA = SB("bufA", [128, S], BF16)
        bufK = SB("bufK", [128, S], BF16)
        bufV = SB("bufV", [128, 2112], BF16)
        pt = SB("pt", [128, 1024], BF16)
        msk = SB("msk", [128, 1024], F32)
        SB("l128", [128, 2], F32)
        SB("e128", [128, 128], F32)
        SB("tt2", [2, 128], F32)
        SB("m128", [128, 2], F32)
        SB("dtmp", [128, 128], F32)
        SB("dtmp2", [128, 128], F32)
        woT = SB("woT", [128, 8 * D], BF16)
        pstk = ExitStack()
        psc = self.PSm("psc", [128, 1024], F32, pstk)
        ppv = self.PSm("ppv", [128, S], F32, pstk)
        paux = self.PSm("paux", [128, 512], F32, pstk)

        on("pool", lambda e, o=iota[:, :]: e.iota(
            o, [[1, S]], channel_multiplier=0, allow_small_or_imprecise_dtypes=True),
            inc="pool")
        pl0 = self.cnt["pool"]
        on("dve", lambda e, o=bufV[:, :]: e.memset(o, 1.0))
        on("dve", lambda e, o=psc[:, 0:1024]: e.memset(o, 0.0))
        for h in range(H1):
            on("pe", lambda e, o=paux[:, 0:2], l=sb["onesb"][0:1, 0:128],
               r=sb["alogFb"][0:1, 2 * h:2 * h + 2]: e.matmul(o, l, r, start=True, stop=True),
               wait=[("Lg", self.cnt["Lg"]), ("dve", self.cnt["dve"]),
                     ("act", self.cnt["act"])], inc="pe")
            pe0 = self.cnt["pe"]
            on("dve", lambda e, o=sb["l128"][:, :], i=paux[:, 0:2]: e.tensor_copy(o, i),
               wait=[("pe", pe0)], inc="dve")
            dvl = self.cnt["dve"]
            on("pe", lambda e, o=paux[:, 0:2], l=sb["onesb"][0:1, 0:128],
               r=sb["amixFb"][0:1, 2 * h:2 * h + 2]: e.matmul(o, l, r, start=True, stop=True),
               wait=[("dve", dvl)], inc="pe")
            pe0 = self.cnt["pe"]
            on("dve", lambda e, o=sb["m128"][:, :], i=paux[:, 0:2]: e.tensor_copy(o, i),
               wait=[("pe", pe0)], inc="dve")
            dvm = self.cnt["dve"]
            for n in range(2):
                o = sb["e128"][:, :] if n == 0 else sb["dtmp2"][:, :]
                on("act", lambda e, o=o, i=sb["s_absd"][:, :], s=sb["l128"][:, n:n + 1]:
                   e.activation(out=o, in_=i, func=AF.Exp, scale=s),
                   wait=[("dve", dvl)], inc="act")
            a2 = self.cnt["act"]
            dgs = diag[:, h * 128:(h + 1) * 128]
            on("act", lambda e, o=sb["dtmp"][:, :], i=sb["dtmp2"][:, :],
               s=sb["m128"][:, 1:2]: e.activation(
                   out=o, in_=i, func=AF.Identity, scale=s),
               wait=[("dve", dvm)], inc="act")
            on("act", lambda e, o=sb["dtmp2"][:, :], i=sb["e128"][:, :],
               s=sb["m128"][:, 0:1]: e.activation(
                   out=o, in_=i, func=AF.Identity, scale=s), inc="act")
            a3 = self.cnt["act"]
            on("dve", lambda e, o=dgs, i=sb["dtmp"][:, :], j=sb["dtmp2"][:, :]:
               e.tensor_tensor(out=o, in0=i, in1=j, op=ALU.add),
               wait=[("act", a3)], inc="dve")
        diag_done = self.cnt["dve"]
        woT_done = None

        prev_b = (0, 0)
        for b in range(BC):
            impw = [("dve", prev_b[0]), ("A2", prev_b[1]), ("cc", self.cnt["cc"])]
            if src == "x0":
                src_ap = AP(dr["x0"].ap().tensor, b * S * D,
                            [[D, 128], [128 * D, NT1], [1, D]])
                dst_ap = AP(x[:, :].tensor, 0, [[NT1 * D, 128], [D, NT1], [1, D]])
                xi = dma("sp", "Ls", dst_ap, src_ap, wait=impw)
            else:
                for th in range(2):
                    src_ap = AP(dr["a2a_out"].ap().tensor, b * SL * D + th * 128 * D,
                                [[D, 128], [BC * SL * D, NCORES], [1, D]])
                    dst_ap = AP(x[:, :].tensor, th * D,
                                [[NT1 * D, 128], [2 * D, NCORES], [1, D]])
                    xi = dma("sp", "Ls", dst_ap, src_ap, wait=impw if th == 0 else ())
            for tt in range(NT1):
                self.ln_tile(x[:, tt * D:(tt + 1) * D],
                             lambda ft, tt=tt: lnxT[:, ft * S + tt * 128: ft * S + (tt + 1) * 128],
                             [("Ls", xi)])
            ln_dve = self.cnt["dve"]

            for pair in range(H1 // 2):
                r0 = pair * 128
                projs = (("aWv", "abv_t", bufA, True), ("aWq", "abq_t", bufA, False),
                         ("aWk", "abk_t", bufK, False))
                for wname, bname, tgt, is_v in projs:
                    lw = dma("pool", "Lw", wstg[:, :], dr[wname].ap()[r0:r0 + 128, :],
                             wait=[("pe", self.cnt["pe"])])
                    for ec in range(8):
                        self.trblk(wstg[:, ec * 128:(ec + 1) * 128],
                                   wT[:, ec * 128:(ec + 1) * 128],
                                   wait=[("Lw", lw)] if ec == 0 else ())
                    tdone = self.cnt["dve"]
                    for qc in range(4):
                        for ec in range(8):
                            on("pe", lambda e, o=psc[:, 0:512],
                               l=wT[:, ec * 128:(ec + 1) * 128],
                               r=lnxT[:, ec * S + qc * 512: ec * S + qc * 512 + 512],
                               s=(ec == 0), t=(ec == 7): e.matmul(o, l, r, start=s, stop=t),
                               wait=[("dve", tdone), ("dve", ln_dve)] if ec == 0 else (),
                               inc="pe")
                        pe0 = self.cnt["pe"]
                        on("act", lambda e, o=tgt[:, qc * 512:(qc + 1) * 512],
                           i=psc[:, 0:512], bi=sb[bname][:, pair:pair + 1]:
                           e.activation(out=o, in_=i, func=AF.Identity, bias=bi),
                           wait=[("pe", pe0)], inc="act")
                        on("pe", None, wait=[("act", self.cnt["act"])])
                    if is_v:
                        acv = self.cnt["act"]
                        tr = ps["tr"]
                        for kt in range(NT1):
                            on("pe", lambda e, s=bufA[:, kt * 128:(kt + 1) * 128],
                               t=tr[:, 0:128], y=sb["s_eye"][:, :]: e.transpose(t, s, y),
                               wait=([("act", acv)] if kt == 0 else []) +
                                    [("dve", self.tr_guard)], inc="pe")
                            pe0 = self.cnt["pe"]
                            on("dve", lambda e, o=bufV[:, kt * 129: kt * 129 + 64],
                               t=tr[:, 0:64]: e.tensor_copy(o, t),
                               wait=[("pe", pe0)], inc="dve")
                            on("dve", lambda e, o=bufV[:, kt * 129 + 65: kt * 129 + 129],
                               t=tr[:, 64:128]: e.tensor_copy(o, t), inc="dve")
                            self.tr_guard = self.cnt["dve"]
                proj_act = self.cnt["act"]

                for hi in range(2):
                    h = 2 * pair + hi
                    ho = 64 * hi
                    pe_now = self.cnt["pe"]
                    on("act", lambda e, o=tasc[:, :], i=iota[:, :],
                       s=sb["alogT"][:, h:h + 1]: e.activation(
                           out=o, in_=i, func=AF.Exp, scale=s),
                       wait=[("pool", pl0), ("pe", pe_now)], inc="act")
                    on("act", lambda e, o=tdesc[:, :], i=iota[:, :],
                       s=sb["alogTn"][:, h:h + 1], bi=sb["alogT2047"][:, h:h + 1]:
                       e.activation(out=o, in_=i, func=AF.Exp, scale=s, bias=bi),
                       inc="act")
                    on("act", lambda e, o=sb["tt2"][:, :], i=iota[:, 0:128],
                       s=sb["alogTn"][:, h:h + 1], bi=sb["alogT127"][:, h:h + 1]:
                       e.activation(out=o, in_=i, func=AF.Exp, scale=s, bias=bi),
                       inc="act")
                    on("act", lambda e, o=mlo[:, :], i=sb["tt2"][:, :],
                       s=sb["amixT"][:, h:h + 1]: e.activation(
                           out=o, in_=i, func=AF.Identity, scale=s), inc="act")
                    on("act", lambda e, o=sb["tt2"][:, :], i=iota[:, 0:128],
                       s=sb["alogT"][:, h:h + 1]: e.activation(
                           out=o, in_=i, func=AF.Exp, scale=s), inc="act")
                    on("act", lambda e, o=mup[:, :], i=sb["tt2"][:, :],
                       s=sb["amixT"][:, h:h + 1]: e.activation(
                           out=o, in_=i, func=AF.Identity, scale=s), inc="act")
                    tab_act = self.cnt["act"]
                    on("dve", None, wait=[("act", tab_act)], inc="dve")
                    tab_done = self.cnt["dve"]

                    mult_done = 0
                    mult_hist = [0, 0]
                    exp_done = proj_act
                    exp_hist = [proj_act, proj_act]
                    mevac_done = 0
                    mevac_hist = [proj_act, proj_act]
                    ptf = {0: 0, 1: 0}
                    ft = h // 2
                    for qc in range(4):
                        q0c, q1c = qc * 512, qc * 512 + 512
                        pend = None
                        for kt in range(NT1):
                            k0 = kt * 128
                            mb = kt % 2
                            bm = psc[:, mb * 512:(mb + 1) * 512]
                            bs = psc[:, (1 - mb) * 512:(2 - mb) * 512]
                            scrb = sb["scr"][:, :] if mb == 0 else sb["scr2"][:, :]
                            msl = msk[:, mb * 512:(mb + 1) * 512]
                            mwait = [("dve", tab_done), ("act", mevac_hist[mb]),
                                     ("dve", mult_hist[mb])]
                            ju = min(k0, q1c)
                            if ju > q0c:
                                ln_ = ju - q0c
                                on("pe", lambda e, o=bm[:, 0:ln_], l=mup[:, :],
                                   r=tdesc[:, q0c + 2047 - k0: q0c + 2047 - k0 + ln_]:
                                   e.matmul(o, l, r, start=True, stop=True),
                                   wait=mwait, inc="pe")
                            jl = max(k0 + 128, q0c)
                            if q1c > jl:
                                ln_ = q1c - jl
                                off = jl - q0c
                                on("pe", lambda e, o=bm[:, off:off + ln_], l=mlo[:, :],
                                   r=tasc[:, jl - k0 - 127: jl - k0 - 127 + ln_]:
                                   e.matmul(o, l, r, start=True, stop=True),
                                   wait=mwait, inc="pe")
                            pe_m = self.cnt["pe"]
                            on("pe", lambda e, o=bs,
                               l=bufK[ho:ho + 64, kt * 128:(kt + 1) * 128],
                               r=bufA[ho:ho + 64, qc * 512:(qc + 1) * 512]:
                               e.matmul(o, l, r, start=True, stop=True),
                               wait=[("act", mevac_hist[1 - mb]),
                                     ("dve", mult_hist[1 - mb])], inc="pe")
                            pe_s = self.cnt["pe"]
                            if pend is not None:
                                pend()
                            on("act", lambda e, o=msl, i=bm: e.activation(
                                out=o, in_=i, func=AF.Copy),
                                wait=[("pe", pe_m), ("dve", mult_hist[mb])], inc="act")
                            mevac_done = mevac_hist[mb] = self.cnt["act"]
                            dg_done = 0
                            if q0c <= k0 < q1c:
                                off = k0 - q0c
                                on("dve", lambda e, o=msl[:, off:off + 128],
                                   i=diag[:, h * 128:(h + 1) * 128]: e.tensor_copy(o, i),
                                   wait=[("act", mevac_done), ("dve", diag_done)],
                                   inc="dve")
                                dg_done = self.cnt["dve"]
                            on("dve", lambda e, o=scrb, m=msl,
                               i=bs: e.tensor_tensor(
                                out=o, in0=m, in1=i, op=ALU.mult),
                                wait=[("pe", pe_s), ("act", mevac_done),
                                      ("dve", dg_done), ("act", exp_hist[mb])],
                                inc="dve")
                            mult_done = mult_hist[1 - mb] = self.cnt["dve"]
                            pb = kt % 2
                            on("act", lambda e, o=pt[:, pb * 512:(pb + 1) * 512],
                               i=scrb: e.activation(
                                   out=o, in_=i, func=AF.Exp, scale=0.125),
                                wait=[("dve", mult_done), ("pe", ptf[pb])], inc="act")
                            exp_done = exp_hist[mb] = self.cnt["act"]

                            def mk_pend(kt=kt, pb=pb, exp_done=exp_done, qc=qc,
                                        ho=ho, hi=hi):
                                def pend_():
                                    if hi == 0:
                                        on("pe", lambda e,
                                           o=ppv[0:65, qc * 512:(qc + 1) * 512],
                                           l=bufV[:, kt * 129: kt * 129 + 65],
                                           r=pt[:, pb * 512:(pb + 1) * 512],
                                           s=(kt == 0), t=(kt == NT1 - 1):
                                           e.matmul(o, l, r, start=s, stop=t),
                                           wait=[("act", exp_done)], inc="pe")
                                    else:
                                        on("pe", lambda e, o=paux[0:1, 0:512],
                                           l=sb["onesc"][:, 0:1],
                                           r=pt[:, pb * 512:(pb + 1) * 512],
                                           s=(kt == 0), t=(kt == NT1 - 1):
                                           e.matmul(o, l, r, start=s, stop=t),
                                           wait=[("act", exp_done)], inc="pe")
                                        on("pe", lambda e,
                                           o=ppv[64:128, qc * 512:(qc + 1) * 512],
                                           l=bufV[:, kt * 129 + 65: kt * 129 + 129],
                                           r=pt[:, pb * 512:(pb + 1) * 512],
                                           s=(kt == 0), t=(kt == NT1 - 1), tp=(0, 64):
                                           e.matmul(o, l, r, start=s, stop=t,
                                                    tile_position=tp), inc="pe")
                                    ptf[pb] = self.cnt["pe"]
                                return pend_
                            pend = mk_pend()
                        pend()
                        pend = None
                        pv_done = self.cnt["pe"]
                        drow = 64 if hi == 0 else 0
                        dsrc = (ppv[64:65, qc * 512:(qc + 1) * 512] if hi == 0
                                else paux[0:1, 0:512])
                        on("dve", lambda e, o=rden[drow:drow + 1, 0:512], i=dsrc:
                           e.reciprocal(out=o, in_=i), wait=[("pe", pv_done)], inc="dve")
                        dv0 = self.cnt["dve"]
                        on("pe", lambda e, o=paux[:, 0:512],
                           l=sb["onesb"][drow:drow + 1, 0:128],
                           r=rden[drow:drow + 1, 0:512]:
                           e.matmul(o, l, r, start=True, stop=True),
                           wait=[("dve", dv0), ("act", mevac_done)], inc="pe")
                        pe0 = self.cnt["pe"]
                        on("dve", lambda e, o=sb["repsb"][:, 0:512], i=paux[:, 0:512]:
                           e.tensor_copy(o, i), wait=[("pe", pe0)], inc="dve")
                        dv1 = self.cnt["dve"]
                        on("dve", lambda e,
                           o=outT[ho:ho + 64, ft * S + qc * 512: ft * S + (qc + 1) * 512],
                           i=ppv[ho:ho + 64, qc * 512:(qc + 1) * 512],
                           r=sb["repsb"][ho:ho + 64, 0:512]: e.tensor_tensor(
                               out=o, in0=i, in1=r, op=ALU.mult),
                           wait=[("dve", dv1)], inc="dve")
                        on("pe", None, wait=[("dve", self.cnt["dve"])])
                    on("act", None, wait=[("dve", self.cnt["dve"])])
                    if self.dev.get("dbg3") and pair == self.dev.get("dbg3_pair", 0) and hi == 1:
                        self.dma("pool", "A2", dr["t_outT_all"].ap(), outT[:, :],
                                 wait=[("dve", self.cnt["dve"])])
                        if "t_rden2" in dr:
                            self.dma("pool", "A2", dr["t_rden2"].ap(), rden[:, :],
                                     wait=[("dve", self.cnt["dve"])])
                        if "t_repsb" in dr:
                            self.dma("pool", "A2", dr["t_repsb"].ap(), sb["repsb"][:, :],
                                     wait=[("dve", self.cnt["dve"])])
                        self.on("sp", None, wait=[("A2", self.cnt["A2"])])
                        self.barrier()
                        pstk.close()
                        stk.close()
                        return
                    if self.dev.get("dbg1"):
                        dv = self.cnt["dve"]
                        for nm, apx in (("t_tasc", tasc[:, :]), ("t_tdesc", tdesc[:, :]),
                                        ("t_mlo", mlo[:, :]), ("t_mup", mup[:, :]),
                                        ("t_msk", msk[:, :]), ("t_pt", pt[:, :]),
                                        ("t_rden", rden[:, :]),
                                        ("t_outT", outT[:, 0:S]),
                                        ("t_lnxT", lnxT[:, 0:S]),
                                        ("t_bufA", bufA[:, :]), ("t_bufK", bufK[:, :]),
                                        ("t_bufV", bufV[:, :]),
                                        ("t_diag", diag[:, 0:256])):
                            if nm in dr:
                                self.dma("pool", "A2", dr[nm].ap(), apx,
                                         wait=[("dve", dv)])
                        self.on("sp", None, wait=[("A2", self.cnt["A2"])])
                        self.barrier()
                        pstk.close()
                        stk.close()
                        return
            heads_dve = self.cnt["dve"]
            if self.dev.get("dbg2"):
                self.dma("pool", "A2", dr["t_outT_all"].ap(), outT[:, :],
                         wait=[("dve", heads_dve)])
                self.dma("pool", "A2", dr["t_rden2"].ap(), rden[:, :],
                         wait=[("dve", heads_dve)])
                self.on("sp", None, wait=[("A2", self.cnt["A2"])])
                self.barrier()
                pstk.close()
                stk.close()
                return

            if woT_done is None:
                for fot in range(8):
                    lw = dma("pool", "Lw", wstg[:, :],
                             dr["aWo"].ap()[fot * 128:(fot + 1) * 128, :],
                             wait=[("pe", self.cnt["pe"])])
                    for ec in range(8):
                        self.trblk(wstg[:, ec * 128:(ec + 1) * 128],
                                   woT[:, ec * D + fot * 128: ec * D + fot * 128 + 128],
                                   wait=[("Lw", lw)] if ec == 0 else ())
                woT_done = self.cnt["dve"]
            for st2 in range(NT1):
                for foc in range(2):
                    for fic in range(8):
                        on("pe", lambda e, o=psc[:, 0:512],
                           l=outT[:, fic * S + st2 * 128: fic * S + st2 * 128 + 128],
                           r=woT[:, fic * D + foc * 512: fic * D + foc * 512 + 512],
                           s=(fic == 0): e.matmul(o, l, r, start=s, stop=False),
                           wait=[("dve", woT_done), ("dve", heads_dve)] if fic == 0 else (),
                           inc="pe")
                    on("pe", lambda e, o=psc[:, 0:512], l=sb["onesb"][0:1, 0:128],
                       r=sb["abo_bf"][0:1, foc * 512:(foc + 1) * 512]:
                       e.matmul(o, l, r, start=False, stop=True, tile_position=(0, 0)),
                       inc="pe")
                    pe0 = self.cnt["pe"]
                    on("dve", lambda e,
                       o=x[:, st2 * D + foc * 512: st2 * D + foc * 512 + 512],
                       i=psc[:, 0:512]: e.tensor_tensor(out=o, in0=o, in1=i, op=ALU.add),
                       wait=[("pe", pe0)], inc="dve")
                    on("pe", None, wait=[("dve", self.cnt["dve"])])
            res_done = self.cnt["dve"]
            if export == "a2a":
                for th in range(2):
                    eap = AP(dr["a2a_in"].ap().tensor, b * SL * D + th * 128 * D,
                             [[D, 128], [BC * SL * D, NCORES], [1, D]])
                    sap = AP(x[:, :].tensor, th * D,
                             [[NT1 * D, 128], [2 * D, NCORES], [1, D]])
                    ex = self.dma("pool", "A2", eap, sap, wait=[("dve", res_done)])
            else:
                eap = AP(dr[export].ap().tensor, b * S * D, [[D, 128], [128 * D, NT1], [1, D]])
                sap = AP(x[:, :].tensor, 0, [[NT1 * D, 128], [D, NT1], [1, D]])
                ex = self.dma("pool", "A2", eap, sap, wait=[("dve", res_done)])
            prev_b = (res_done, ex)
        self.on("sp", None, wait=[("A2", self.cnt["A2"])])
        self.barrier()
        pstk.close()
        stk.close()

    def a2a(self):
        cw = [("A2", self.cnt["A2"]), ("St0", 1), ("St1", 1)]
        ins = [self.dr["a2a_in"].ap().opt()]
        outs = [self.dr["a2a_out"].ap().opt()]

        def fn(e, ins=ins, outs=outs):
            with allow_ici_all_to_all():
                return e.collective_compute(
                    "AllToAll", ALU.bypass, replica_groups=[list(range(NCORES))],
                    ins=ins, outs=outs)
        self.on("pool", fn, wait=cw, inc="cc")

    def ffn(self, dst="a2a_in", src="a2a_out"):
        sb, dr, ps, on, dma = self.sb, self.dr, self.ps, self.on, self.dma
        stk = ExitStack()
        SB = lambda n, s, d: self.SB(n, s, d, stk)
        uniq = "f%d" % self.cnt["cc"]
        NTOK = NT2 * 128
        lnxT = SB("lnxT" + uniq, [128, 8 * NTOK], BF16)
        cst = SB("c" + uniq, [128, 8 * SL], F32)
        cbf = SB("cbf" + uniq, [128, 8 * SL], BF16)
        xt = [SB("xt0" + uniq, [128, D], F32), SB("xt1" + uniq, [128, D], F32)]
        gf = SB("gf" + uniq, [128, SL], F32)
        gi = SB("gi" + uniq, [128, SL], F32)
        gc = SB("gc" + uniq, [128, SL], F32)
        gtmp = SB("gtmp" + uniq, [128, SL], F32)
        wstg2 = SB("wstg2" + uniq, [128, D], BF16)
        pstk = ExitStack()
        pg = {"f": self.PSm("pgf" + uniq, [128, 512], F32, pstk),
              "i": self.PSm("pgi" + uniq, [128, 512], F32, pstk),
              "c": self.PSm("pgc" + uniq, [128, 512], F32, pstk)}
        py = self.PSm("py" + uniq, [128, 512], F32, pstk)

        def tile_ap(dram, tt):
            return AP(dram.ap().tensor,
                      (tt // 4) * (BC * SL * D) + ((tt // 2) % 2) * (SL * D)
                      + (tt % 2) * 128 * D,
                      [[D, 128], [1, D]])

        ccw = [("cc", self.cnt["cc"])]
        wstk = ExitStack()
        mats = {}
        for nm in ("fwf", "fuf", "fwi", "fui", "fwc", "fuc"):
            mats[nm] = self.SB(nm + uniq, [128, 8 * D], BF16, wstk)
            for fot in range(8):
                lw = dma("pool", "Lw", wstg2[:, :],
                         dr[nm].ap()[fot * 128:(fot + 1) * 128, :],
                         wait=[("dve", self.cnt["dve"])])
                for ec in range(8):
                    self.trblk(wstg2[:, ec * 128:(ec + 1) * 128],
                               mats[nm][:, ec * D + fot * 128: ec * D + fot * 128 + 128],
                               wait=[("Lw", lw)] if ec == 0 else ())
        mats_done = self.cnt["dve"]
        xld = []
        for tt in range(NT2):
            xb = xt[tt % 2]
            w = list(ccw)
            if tt >= 2:
                w.append(("dve", xld[tt - 2]))
            ld = dma("sp", "Lx%d" % (tt % 2), xb[:, :], tile_ap(dr[src], tt), wait=w)
            dv = self.ln_tile(xb[:, :],
                              lambda ft, tt=tt: lnxT[:, ft * NTOK + tt * 128: ft * NTOK + (tt + 1) * 128],
                              [("Lx%d" % (tt % 2), ld)])
            xld.append(dv)
        ln_dve = self.cnt["dve"]
        for fot in range(8):
            on("act", lambda e, o=cst[:, fot * SL:(fot + 1) * SL],
               i=sb["onesb"][:, 0:SL], b=sb["finc_t"][:, fot:fot + 1]: e.activation(
                   out=o, in_=i, func=AF.Identity, scale=0.0, bias=b), inc="act")
            ac9 = self.cnt["act"]
            on("dve", lambda e, o=cbf[:, fot * SL:(fot + 1) * SL],
               i=cst[:, fot * SL:(fot + 1) * SL]: e.tensor_copy(o, i),
               wait=[("act", ac9)], inc="dve")
        cbf_done = self.cnt["dve"]
        gates = (("fwf", "fuf", "f", gf, AF.Sigmoid, "fbf_t"),
                 ("fwi", "fui", "i", gi, AF.Sigmoid, "fbi_t"),
                 ("fwc", "fuc", "c", gc, AF.Tanh, "fbc_t"))
        for t in range(TS):
            gate_ev = getattr(self, "_gate_ev", None)
            if gate_ev is None or t == 0:
                gate_ev = self._gate_ev = {"f": 0, "i": 0, "c": 0}
            for fot in range(8):
                for wn, un, pn, gb, fn_, bn in gates:
                    pgn = pg[pn]
                    for fic in range(8):
                        on("pe", lambda e, o=pgn[:, 0:SL],
                           l=mats[wn][:, fic * D + fot * 128: fic * D + fot * 128 + 128],
                           r=lnxT[:, fic * NTOK + t * SL: fic * NTOK + (t + 1) * SL],
                           s=(fic == 0): e.matmul(o, l, r, start=s, stop=False),
                           wait=[("dve", mats_done), ("dve", ln_dve),
                                 ("dve", cbf_done), ("act", gate_ev[pn])]
                           if fic == 0 else (), inc="pe")
                    for fic in range(8):
                        on("pe", lambda e, o=pgn[:, 0:SL],
                           l=mats[un][:, fic * D + fot * 128: fic * D + fot * 128 + 128],
                           r=cbf[:, fic * SL:(fic + 1) * SL],
                           t_=(fic == 7): e.matmul(o, l, r, start=False, stop=t_),
                           inc="pe")
                    pe0 = self.cnt["pe"]
                    on("act", lambda e, o=gb[:, :], i=pgn[:, 0:SL], fn_=fn_,
                       bi=sb[bn][:, fot:fot + 1]: e.activation(
                           out=o, in_=i, func=fn_, bias=bi),
                       wait=[("pe", pe0), ("dve", self.cnt["dve"])], inc="act")
                    gate_ev[pn] = self.cnt["act"]
                ev = self.cnt["act"]
                cs = cst[:, fot * SL:(fot + 1) * SL]
                on("dve", lambda e, o=gtmp[:, :], i=gi[:, :], j=gc[:, :]:
                   e.tensor_tensor(out=o, in0=i, in1=j, op=ALU.mult),
                   wait=[("act", ev)], inc="dve")
                on("dve", lambda e, o=cs, i=gf[:, :]: e.tensor_tensor(
                    out=o, in0=o, in1=i, op=ALU.mult), inc="dve")
                on("dve", lambda e, o=cs, i=gtmp[:, :]: e.tensor_tensor(
                    out=o, in0=o, in1=i, op=ALU.add), inc="dve")
            step_pe = self.cnt["pe"]
            for fot in range(8):
                on("dve", lambda e, o=cbf[:, fot * SL:(fot + 1) * SL],
                   i=cst[:, fot * SL:(fot + 1) * SL]: e.tensor_copy(o, i),
                   wait=[("pe", step_pe)] if fot == 0 else (), inc="dve")
            cbf_done = self.cnt["dve"]
        scan_pe = self.cnt["pe"]
        wstk.close()
        fstk = ExitStack()
        woT = self.SB("fwoT" + uniq, [128, 8 * D], BF16, fstk)
        uoT = self.SB("fuoT" + uniq, [128, 8 * D], BF16, fstk)
        cpart = self.SB("cpart" + uniq, [128, 2 * D], F32, fstk)
        for nm, tgt in (("fwo", woT), ("fuo", uoT)):
            for fot in range(8):
                lw = dma("pool", "Lw", wstg2[:, :],
                         dr[nm].ap()[fot * 128:(fot + 1) * 128, :],
                         wait=[("pe", scan_pe), ("dve", self.cnt["dve"])])
                for ec in range(8):
                    self.trblk(wstg2[:, ec * 128:(ec + 1) * 128],
                               tgt[:, ec * D + fot * 128: ec * D + fot * 128 + 128],
                               wait=[("Lw", lw)] if ec == 0 else ())
        wo_done = self.cnt["dve"]
        if self.dev.get("dbgf") == self.cnt["cc"]:
            self.dma("pool", "A2", dr["t_flnxT"].ap(), lnxT[:, 0:NTOK],
                     wait=[("dve", wo_done)])
            self.dma("pool", "A2", dr["t_fcbf"].ap(), cbf[:, :],
                     wait=[("dve", wo_done)])
        for stl in range(2):
            for foc in range(2):
                for fmc in range(8):
                    on("pe", lambda e, o=py[:, 0:512],
                       l=cbf[:, fmc * SL + stl * 128: fmc * SL + stl * 128 + 128],
                       r=uoT[:, fmc * D + foc * 512: fmc * D + foc * 512 + 512],
                       s=(fmc == 0), t_=(fmc == 7): e.matmul(o, l, r, start=s, stop=t_),
                       wait=[("dve", wo_done), ("dve", cbf_done),
                             ("act", self.cnt["act"])] if fmc == 0 else (),
                       inc="pe")
                pe0 = self.cnt["pe"]
                on("act", lambda e, o=cpart[:, stl * D + foc * 512: stl * D + foc * 512 + 512],
                   i=py[:, 0:512]: e.activation(out=o, in_=i, func=AF.Copy),
                   wait=[("pe", pe0)], inc="act")
        cpart_done = self.cnt["act"]
        pass
        xold = {}
        for tt in range(2):
            xold[tt] = dma("sp", "Lx%d" % (tt % 2), xt[tt % 2][:, :], tile_ap(dr[src], tt),
                           wait=[("dve", ln_dve)])
        for tt in range(NT2):
            xb = xt[tt % 2]
            stl = tt % 2
            for foc in range(2):
                for fic in range(8):
                    on("pe", lambda e, o=py[:, 0:512],
                       l=lnxT[:, fic * NTOK + tt * 128: fic * NTOK + tt * 128 + 128],
                       r=woT[:, fic * D + foc * 512: fic * D + foc * 512 + 512],
                       s=(fic == 0): e.matmul(o, l, r, start=s, stop=False),
                       wait=[("act", cpart_done)] if fic == 0 else (), inc="pe")
                on("pe", lambda e, o=py[:, 0:512], l=sb["onesb"][0:1, 0:128],
                   r=sb["fbo_bf"][0:1, foc * 512:(foc + 1) * 512]:
                   e.matmul(o, l, r, start=False, stop=True, tile_position=(0, 0)),
                   inc="pe")
                pe0 = self.cnt["pe"]
                on("dve", lambda e, o=sb["scr"][:, :], j=py[:, 0:512],
                   i=cpart[:, stl * D + foc * 512: stl * D + foc * 512 + 512]:
                   e.tensor_tensor(out=o, in0=i, in1=j, op=ALU.add),
                   wait=[("pe", pe0), ("act", cpart_done),
                         ("act", self.cnt["act"])], inc="dve")
                dv0 = self.cnt["dve"]
                on("act", lambda e, o=sb["scr"][:, :]: e.activation(
                    out=o, in_=o, func=AF.Tanh), wait=[("dve", dv0)], inc="act")
                ac0 = self.cnt["act"]
                on("dve", lambda e, o=xb[:, foc * 512:(foc + 1) * 512],
                   i=sb["scr"][:, :]: e.tensor_tensor(out=o, in0=o, in1=i, op=ALU.add),
                   wait=[("act", ac0), ("Lx%d" % (tt % 2), xold[tt])], inc="dve")
                on("pe", None, wait=[("dve", self.cnt["dve"])])
            res_done = self.cnt["dve"]
            stsem = "St%d" % (tt % 2)
            if dst == "a2a_in":
                st_ap = tile_ap(dr["a2a_in"], tt)
            else:
                st_ap = dr["y"].ap()[tt * 128:(tt + 1) * 128, :]
            self.dma("pool", stsem, st_ap, xb[:, :], wait=[("dve", res_done)])
            nxt = tt + 2
            if nxt < NT2:
                xold[nxt] = dma("sp", "Lx%d" % (nxt % 2), xb[:, :], tile_ap(dr[src], nxt),
                                wait=[(stsem, 1)])
        self.on("sp", None, wait=[("St0", 1), ("St1", 1)])
        self.barrier()
        pstk.close()
        fstk.close()
        stk.close()

    def attn_b(self, src="a2a_out", export="a2a"):
        sb, dr, ps, on, dma = self.sb, self.dr, self.ps, self.on, self.dma
        stk = ExitStack()
        SB = lambda n, s, d: self.SB(n, s, d, stk)
        x = SB("bx", [128, NT1 * D], F32)
        lnxT = SB("blnxT", [128, 8 * S], BF16)
        outT = SB("boutT", [128, 8 * S], BF16)
        relkT = SB("relkT", [128, NPOS + 1], BF16)
        qreb = SB("qreb", [128, S], BF16)
        qrpb = SB("qrpb", [128, S], BF16)
        kT = SB("bkT", [128, S], BF16)
        vv = SB("bvv", [128, S], BF16)
        winsb = SB("winsb", [128, WIN], BF16)
        bandT = SB("bandT", [128, 1024], BF16)
        pt = SB("bpt", [128, 1024], BF16)
        rden = SB("brden", [128, 512], BF16)
        wstg = SB("bwstg", [128, D], BF16)
        wrelstg = SB("wrelstg", [128, 256], BF16)
        pstg0 = SB("pstg0", [128, 1024], BF16)
        pstg1 = SB("pstg1", [64, 1024], BF16)
        woutb = SB("woutb", [128, 8 * D], BF16)
        band32 = SB("band32", [128, 512], F32)
        pstk = ExitStack()
        psc = self.PSm("bpsc", [128, 1024], F32, pstk)
        pwin = self.PSm("pwin", [128, 512], F32, pstk)
        ppv = self.PSm("bppv", [128, S], F32, pstk)
        wout_ld = None

        prev_b = (0, 0)
        for b in range(BC):
            impw = [("dve", prev_b[0]), ("A2", prev_b[1]), ("cc", self.cnt["cc"])]
            if src == "a2a_out":
                for th in range(2):
                    src_ap = AP(dr["a2a_out"].ap().tensor, b * SL * D + th * 128 * D,
                                [[D, 128], [BC * SL * D, NCORES], [1, D]])
                    dst_ap = AP(x[:, :].tensor, th * D,
                                [[NT1 * D, 128], [2 * D, NCORES], [1, D]])
                    xi = dma("sp", "Ls", dst_ap, src_ap, wait=impw if th == 0 else ())
            else:
                src_ap = AP(dr[src].ap().tensor, b * S * D,
                            [[D, 128], [128 * D, NT1], [1, D]])
                dst_ap = AP(x[:, :].tensor, 0, [[NT1 * D, 128], [D, NT1], [1, D]])
                xi = dma("sp", "Ls", dst_ap, src_ap, wait=impw)
            for tt in range(NT1):
                self.ln_tile(x[:, tt * D:(tt + 1) * D],
                             lambda ft, tt=tt: lnxT[:, ft * S + tt * 128: ft * S + (tt + 1) * 128],
                             [("Ls", xi)])
            ln_dve = self.cnt["dve"]
            if self.dev.get("dbg8"):
                self.dma("pool", "A2", dr["t_vv"].ap(), lnxT[:, 0:S],
                         wait=[("dve", ln_dve)])
                if "t_x" in dr:
                    self.dma("pool", "A2", dr["t_x"].ap(), x[:, :],
                             wait=[("dve", ln_dve)])
                self.on("sp", None, wait=[("A2", self.cnt["A2"])])
                self.barrier()
                pstk.close()
                stk.close()
                return

            for h in range(H2):
                hc = h * 128
                for wname in ("bwq", "bwk", "bwv"):
                    lw = dma("pool", "Lw",
                             AP(wstg[:, :].tensor, 0, [[D, 128], [128, 8], [1, 128]]),
                             AP(dr[wname].ap().tensor, hc, [[D, 128], [128 * D, 8], [1, 128]]),
                             wait=[("pe", self.cnt["pe"])])
                    for qc in range(4):
                        for ec in range(8):
                            on("pe", lambda e, o=psc[:, 0:512],
                               l=wstg[:, ec * 128:(ec + 1) * 128],
                               r=lnxT[:, ec * S + qc * 512: ec * S + qc * 512 + 512],
                               s=(ec == 0), t=(ec == 7)
                               : e.matmul(o, l, r, start=s, stop=t),
                               wait=[("Lw", lw), ("dve", ln_dve)] if ec == 0 else (),
                               inc="pe")
                        pe0 = self.cnt["pe"]
                        if wname == "bwq":
                            on("act", lambda e, o=qreb[:, qc * 512:(qc + 1) * 512],
                               i=psc[:, 0:512], bi=sb["breb_t"][:, h:h + 1]:
                               e.activation(out=o, in_=i, func=AF.Identity, scale=SC2, bias=bi),
                               wait=[("pe", pe0)], inc="act")
                            on("act", lambda e, o=qrpb[:, qc * 512:(qc + 1) * 512],
                               i=psc[:, 0:512], bi=sb["brpb_t"][:, h:h + 1]:
                               e.activation(out=o, in_=i, func=AF.Identity, scale=SC2, bias=bi),
                               inc="act")
                        elif wname == "bwk":
                            on("act", lambda e, o=kT[:, qc * 512:(qc + 1) * 512],
                               i=psc[:, 0:512]: e.activation(out=o, in_=i, func=AF.Copy),
                               wait=[("pe", pe0)], inc="act")
                        else:
                            pbh = 0 if qc % 2 == 0 else 512
                            on("act", lambda e, o=pt[:, pbh:pbh + 512],
                               i=psc[:, 0:512]: e.activation(out=o, in_=i, func=AF.Copy),
                               wait=[("pe", pe0), ("dve", self.cnt["dve"])], inc="act")
                        on("pe", None, wait=[("act", self.cnt["act"])])
                        if wname == "bwv":
                            pbh = 0 if qc % 2 == 0 else 512
                            acv = self.cnt["act"]
                            for blk in range(4):
                                self.trblk(pt[:, pbh + blk * 128: pbh + (blk + 1) * 128],
                                           vv[:, (qc * 4 + blk) * 128: (qc * 4 + blk + 1) * 128],
                                           wait=[("act", acv)] if blk == 0 else ())
                vv_done = self.cnt["dve"]
                if self.dev.get("dbg6"):
                    self.dma("pool", "A2", dr["t_qreb"].ap(), qreb[:, :],
                             wait=[("dve", vv_done), ("act", self.cnt["act"])])
                    self.dma("pool", "A2", dr["t_vv"].ap(), vv[:, :],
                             wait=[("dve", vv_done)])
                    self.on("sp", None, wait=[("A2", self.cnt["A2"])])
                    self.barrier()
                    pstk.close()
                    stk.close()
                    return
                lw1 = dma("pool", "Lr", wrelstg[:, 0:128],
                          AP(dr["bwrel"].ap().tensor, hc, [[D, 128], [1, 128]]),
                          wait=[("pe", self.cnt["pe"])])
                lw2 = dma("pool", "Lr", wrelstg[0:64, 128:256],
                          AP(dr["bwrel"].ap().tensor, 128 * D + hc, [[D, 64], [1, 128]]))
                relk_ac = 0
                for c in range(8):
                    cn = min(512, NPOS - c * 512)
                    pb = c % 2
                    l1 = dma("pool", "Lp%d" % pb, pstg0[:, pb * 512: pb * 512 + cn],
                             AP(dr["posT"].ap().tensor, c * 512, [[NPOS, 128], [1, cn]]),
                             wait=[("act", relk_ac)])
                    l2 = dma("pool", "Lp%d" % pb, pstg1[0:64, pb * 512: pb * 512 + cn],
                             AP(dr["posT"].ap().tensor, 128 * NPOS + c * 512,
                                [[NPOS, 64], [1, cn]]))
                    on("pe", lambda e, o=psc[:, 0:cn], l=wrelstg[:, 0:128],
                       r=pstg0[:, pb * 512: pb * 512 + cn]: e.matmul(
                           o, l, r, start=True, stop=False),
                       wait=[("Lp%d" % pb, l2), ("Lr", lw2), ("act", relk_ac)], inc="pe")
                    on("pe", lambda e, o=psc[:, 0:cn], l=wrelstg[0:64, 128:256],
                       r=pstg1[0:64, pb * 512: pb * 512 + cn]: e.matmul(
                           o, l, r, start=False, stop=True), inc="pe")
                    pe0 = self.cnt["pe"]
                    on("act", lambda e, o=relkT[:, c * 512: c * 512 + cn],
                       i=psc[:, 0:cn]: e.activation(out=o, in_=i, func=AF.Copy),
                       wait=[("pe", pe0)], inc="act")
                    relk_ac = self.cnt["act"]
                if self.dev.get("dbg7"):
                    self.dma("pool", "A2", dr["t_relkT"].ap(), relkT[:, :],
                             wait=[("act", relk_ac)])
                    self.on("sp", None, wait=[("A2", self.cnt["A2"])])
                    self.barrier()
                    pstk.close()
                    stk.close()
                    return
                wev = relk_ac
                win_dve = self.cnt["dve"]
                for qt in range(NT1):
                    u0 = 1920 - qt * 128
                    for uc in range(5):
                        cn = min(512, WIN - uc * 512)
                        on("pe", lambda e, o=pwin[:, 0:cn],
                           l=qrpb[:, qt * 128:(qt + 1) * 128],
                           r=relkT[:, u0 + uc * 512: u0 + uc * 512 + cn]:
                           e.matmul(o, l, r, start=True, stop=True),
                           wait=[("act", wev), ("dve", win_dve)], inc="pe")
                        pe0 = self.cnt["pe"]
                        on("act", lambda e, o=winsb[:, uc * 512: uc * 512 + cn],
                           i=pwin[:, 0:cn]: e.activation(out=o, in_=i, func=AF.Copy),
                           wait=[("pe", pe0), ("Wh", 1)], inc="act")
                        wev = self.cnt["act"]
                    self.dma("sp", "Wh",
                             AP(dr["win_hbm"].ap().tensor, qt * 128 * WIN,
                                [[WIN + 1, 128], [1, WIN]]),
                             winsb[:, :], wait=[("act", wev), ("Bt0", 1), ("Bt1", 1)])
                wh_all = self.cnt["Wh"]
                if self.dev.get("dbg5"):
                    self.dma("pool", "A2", dr["t_winsb"].ap(), winsb[:, :],
                             wait=[("Wh", wh_all)])
                    self.dma("pool", "A2", dr["t_relkT"].ap(), relkT[:, :],
                             wait=[("Wh", wh_all)])
                    if self.dev.get("dbg5") == 2:
                        self.on("sp", lambda e, o=bandT[:, 0:128],
                                i=AP(dr["win_hbm"].ap().tensor, 127, [[WIN - 1, 128], [1, 128]]):
                                e.dma_start_transpose(out=o, in_=i),
                                wait=[("Wh", wh_all)], inc="Bt", incn=16)
                        self.dma("pool", "A2", dr["t_bandT"].ap(), bandT[:, :],
                                 wait=[("Bt", self.cnt["Bt"])])
                    self.on("sp", None, wait=[("A2", self.cnt["A2"])])
                    self.barrier()
                    pstk.close()
                    stk.close()
                    return
                exp_done = self.cnt["act"]
                ptf = {0: 0, 1: 0}

                def emit_band(qc, kt, bb, wv):
                    sname = "Bt%d" % bb
                    ap_in = AP(dr["win_hbm"].ap().tensor,
                               127 + qc * 512 * WIN + kt * 128,
                               [[WIN, 512], [1, 128]])
                    self.on("sp", lambda e, o=bandT[:, bb * 512:(bb + 1) * 512],
                            i=ap_in: e.dma_start_transpose(out=o, in_=i),
                            wait=wv, inc=sname, incn=16)
                    return (sname, self.cnt[sname])
                for qc in range(4):
                    btd = {}
                    for kt in range(2):
                        btd[kt] = emit_band(qc, kt, kt % 2,
                                            [("Wh", wh_all), ("dve", self.cnt["dve"])])
                    exp_hist = [exp_done, exp_done]
                    mult_hist = [0, 0]
                    pend = None
                    for kt in range(NT1):
                        bb = kt % 2
                        pb = kt % 2
                        pscb = psc[:, bb * 512:(bb + 1) * 512]
                        scrb = sb["scr"][:, :] if bb == 0 else sb["scr2"][:, :]
                        on("pe", lambda e, o=pscb,
                           l=kT[:, kt * 128:(kt + 1) * 128],
                           r=qreb[:, qc * 512:(qc + 1) * 512]:
                           e.matmul(o, l, r, start=True, stop=True),
                           wait=[("dve", mult_hist[bb])], inc="pe")
                        pe0 = self.cnt["pe"]
                        if pend is not None:
                            pend()
                        on("dve", lambda e, o=sb["band32"][:, 0:512],
                           m=bandT[:, bb * 512:(bb + 1) * 512]:
                           e.tensor_copy(o, m),
                           wait=[btd[kt], ("dve", self.cnt["dve"])], inc="dve")
                        on("dve", lambda e, o=scrb,
                           m=sb["band32"][:, 0:512],
                           i=pscb: e.tensor_tensor(out=o, in0=m, in1=i, op=ALU.add),
                           wait=[("pe", pe0), ("act", exp_hist[bb])], inc="dve")
                        dv0 = mult_hist[bb] = self.cnt["dve"]
                        on("act", lambda e, o=pt[:, pb * 512:(pb + 1) * 512],
                           i=scrb, b_=sb["nb30"][:, 0:1]: e.activation(
                               out=o, in_=i, func=AF.Exp, bias=b_),
                           wait=[("dve", dv0), ("pe", ptf[pb])], inc="act")
                        exp_done = exp_hist[bb] = self.cnt["act"]

                        def mk_pend(kt=kt, pb=pb, exp_done=exp_done, qc=qc):
                            def pend_():
                                on("pe", lambda e, o=pwin[0:1, 0:512],
                                   l=sb["onesc"][:, 0:1],
                                   r=pt[:, pb * 512:(pb + 1) * 512],
                                   s=(kt == 0), t=(kt == NT1 - 1):
                                   e.matmul(o, l, r, start=s, stop=t),
                                   wait=[("act", exp_done)], inc="pe")
                                on("pe", lambda e, o=ppv[:, qc * 512:(qc + 1) * 512],
                                   l=vv[:, kt * 128:(kt + 1) * 128],
                                   r=pt[:, pb * 512:(pb + 1) * 512],
                                   s=(kt == 0), t=(kt == NT1 - 1):
                                   e.matmul(o, l, r, start=s, stop=t),
                                   wait=[("dve", vv_done)] if kt == 0 else (), inc="pe")
                                ptf[pb] = self.cnt["pe"]
                            return pend_
                        pend = mk_pend()
                        nkt = kt + 2
                        if nkt < NT1:
                            btd[nkt] = emit_band(qc, nkt, nkt % 2,
                                                 [("Wh", wh_all), ("dve", self.cnt["dve"])])
                    pend()
                    pend = None
                    pv_done = self.cnt["pe"]
                    on("dve", lambda e, o=rden[0:1, 0:512], i=pwin[0:1, 0:512]:
                       e.reciprocal(out=o, in_=i), wait=[("pe", pv_done)], inc="dve")
                    dv0 = self.cnt["dve"]
                    on("pe", lambda e, o=pwin[:, 0:512],
                       l=sb["onesb"][0:1, 0:128], r=rden[0:1, 0:512]:
                       e.matmul(o, l, r, start=True, stop=True),
                       wait=[("dve", dv0), ("act", self.cnt["act"])], inc="pe")
                    pe0 = self.cnt["pe"]
                    on("dve", lambda e, o=sb["repsb"][:, 0:512], i=pwin[:, 0:512]:
                       e.tensor_copy(o, i), wait=[("pe", pe0)], inc="dve")
                    dv1 = self.cnt["dve"]
                    on("dve", lambda e,
                       o=outT[:, h * S + qc * 512: h * S + (qc + 1) * 512],
                       i=ppv[:, qc * 512:(qc + 1) * 512],
                       r=sb["repsb"][:, 0:512]: e.tensor_tensor(
                           out=o, in0=i, in1=r, op=ALU.mult),
                       wait=[("dve", dv1)], inc="dve")
                    on("pe", None, wait=[("dve", self.cnt["dve"])])
                on("act", None, wait=[("dve", self.cnt["dve"])])
            heads_dve = self.cnt["dve"]
            if wout_ld is None:
                for fic in range(8):
                    dma("pool", "Lo", woutb[:, fic * D:(fic + 1) * D],
                        dr["bwout"].ap()[fic * 128:(fic + 1) * 128, :])
                wout_ld = self.cnt["Lo"]
            for st2 in range(NT1):
                for foc in range(2):
                    for fic in range(8):
                        on("pe", lambda e, o=psc[:, 0:512],
                           l=outT[:, fic * S + st2 * 128: fic * S + st2 * 128 + 128],
                           r=woutb[:, fic * D + foc * 512: fic * D + foc * 512 + 512],
                           s=(fic == 0): e.matmul(o, l, r, start=s, stop=False),
                           wait=[("Lo", wout_ld), ("dve", heads_dve)] if fic == 0 else (),
                           inc="pe")
                    on("pe", lambda e, o=psc[:, 0:512], l=sb["onesb"][0:1, 0:128],
                       r=sb["bbout_bf"][0:1, foc * 512:(foc + 1) * 512]:
                       e.matmul(o, l, r, start=False, stop=True, tile_position=(0, 0)),
                       inc="pe")
                    pe0 = self.cnt["pe"]
                    on("dve", lambda e,
                       o=x[:, st2 * D + foc * 512: st2 * D + foc * 512 + 512],
                       i=psc[:, 0:512]: e.tensor_tensor(out=o, in0=o, in1=i, op=ALU.add),
                       wait=[("pe", pe0)], inc="dve")
                    on("pe", None, wait=[("dve", self.cnt["dve"])])
            res_done = self.cnt["dve"]
            if export == "a2a":
                for th in range(2):
                    eap = AP(dr["a2a_in"].ap().tensor, b * SL * D + th * 128 * D,
                             [[D, 128], [BC * SL * D, NCORES], [1, D]])
                    sap = AP(x[:, :].tensor, th * D,
                             [[NT1 * D, 128], [2 * D, NCORES], [1, D]])
                    ex = self.dma("pool", "A2", eap, sap, wait=[("dve", res_done)])
            else:
                eap = AP(dr[export].ap().tensor, b * S * D, [[D, 128], [128 * D, NT1], [1, D]])
                sap = AP(x[:, :].tensor, 0, [[NT1 * D, 128], [D, NT1], [1, D]])
                ex = self.dma("pool", "A2", eap, sap, wait=[("dve", res_done)])
            prev_b = (res_done, ex)
        self.on("sp", None, wait=[("A2", self.cnt["A2"])])
        self.barrier()
        pstk.close()
        stk.close()

    def emit(self):
        dev = self.dev
        self.setup()
        phases = dev.get("phases",
                         ("attn_a", "a2a", "ffn1", "a2a", "attn_b", "a2a", "ffn2"))
        for ph in phases:
            if ph == "attn_a":
                self.attn_a(src=dev.get("attn_a_src", "x0"),
                            export=dev.get("attn_a_export", "a2a"))
            elif ph == "a2a":
                self.a2a()
            elif ph == "ffn1":
                self.ffn(dst=dev.get("ffn1_dst", "a2a_in"), src=dev.get("ffn1_src", "a2a_out"))
            elif ph == "ffn2":
                self.ffn(dst="y")
            elif ph == "attn_b":
                self.attn_b(src=dev.get("attn_b_src", "a2a_out"),
                            export=dev.get("attn_b_export", "a2a"))
        self.on("sp", None, wait=[(s0, 1) for s0 in
                                  ("Lg", "Lw", "Lr", "Lp0", "Lp1", "Lo", "Lx0",
                                   "Lx1", "Ls", "St0", "St1", "Wh", "Bt0", "Bt1",
                                   "A2")])

    def build(self):
        self.declare()
        with self.st:
            self.alloc_global()
            self.PSm("tr", [128, 512], BF16)
            semstk = ExitStack()
            with semstk:
                for name in SEMS:
                    self.handles[name] = semstk.enter_context(self.nc.semaphore(name))
                self.emit()
                nc = self.nc
                with nc.allow_low_precision(reason="bf16 operand staging is intentional"), \
                     nc.Block() as blk:
                    @blk.tensor
                    def _(e):
                        for t in self.q["pe"]:
                            t(e)

                    @blk.vector
                    def _(e):
                        for t in self.q["dve"]:
                            t(e)

                    @blk.scalar
                    def _(e):
                        for t in self.q["act"]:
                            t(e)

                    @blk.gpsimd
                    def _(e):
                        for t in self.q["pool"]:
                            t(e)

                    @blk.sync
                    def _(e):
                        for t in self.q["sp"]:
                            t(e)
        return self.nc


_CONSTS = None


def _consts():
    global _CONSTS
    if _CONSTS is None:
        eye = np.eye(128, dtype=ml_dtypes.bfloat16)
        posT = _positions_T()
        absd = np.abs(np.arange(128, dtype=np.float32)[:, None]
                      - np.arange(128, dtype=np.float32)[None, :])
        _CONSTS = dict(eye=eye, posT=posT, absd=absd)
    return _CONSTS


def _in_maps(inputs, extra=None):
    c = _consts()
    x = np.ascontiguousarray(np.asarray(inputs["L0_attn_a_x"], dtype=np.float32))
    base = dict(
        aWq=inputs["L0_attn_a_Wq"], aWk=inputs["L0_attn_a_Wk"],
        aWv=inputs["L0_attn_a_Wv"], aWo=inputs["L0_attn_a_Wo"],
        abq=inputs["L0_attn_a_bq"], abk=inputs["L0_attn_a_bk"],
        abv=inputs["L0_attn_a_bv"], abo=inputs["L0_attn_a_bo"],
        adecay=inputs["L0_attn_a_decay_params"], amix=inputs["L0_attn_a_mix_weights"],
        bwq=inputs["L0_attn_b_w_qs"], bwk=inputs["L0_attn_b_w_ks"],
        bwv=inputs["L0_attn_b_w_vs"], bwrel=inputs["L0_attn_b_w_rel_pos_k"],
        breb=np.asarray(inputs["L0_attn_b_rel_embed_bias"]).reshape(H2, DH2),
        brpb=np.asarray(inputs["L0_attn_b_rel_pos_bias"]).reshape(H2, DH2),
        bwout=inputs["L0_attn_b_w_out"], bbout=inputs["L0_attn_b_b_out"],
        fwf=inputs["L0_ffn_wf"], fuf=inputs["L0_ffn_uf"], fbf=inputs["L0_ffn_bf"],
        fwi=inputs["L0_ffn_wi"], fui=inputs["L0_ffn_ui"], fbi=inputs["L0_ffn_bi"],
        fwc=inputs["L0_ffn_wc"], fuc=inputs["L0_ffn_uc"], fbc=inputs["L0_ffn_bc"],
        fwo=inputs["L0_ffn_wo"], fuo=inputs["L0_ffn_uo"], fbo=inputs["L0_ffn_bo"],
        finc=inputs["L0_ffn_in_c"],
        posT=c["posT"], absd=c["absd"],
    )
    base = {k: np.ascontiguousarray(np.asarray(v, dtype=np.float32))
            for k, v in base.items()}
    base["eye"] = c["eye"]
    maps = []
    for core in range(NCORES):
        m = dict(base)
        m["x0"] = np.ascontiguousarray(x[2 * core:2 * core + 2].reshape(BC * S, D))
        if extra:
            m.update(extra(core))
        maps.append(m)
    return maps


def kernel(**inputs):
    nc = MK().build()
    maps = _in_maps(inputs)
    res = run_bass_kernel_spmd(nc, maps, core_ids=list(range(NCORES)))
    out = np.empty((B, S, D), dtype=np.float32)
    for core in range(NCORES):
        yv = res.results[core]["y"].reshape(TS, SL, D)
        out[:, core * SL:(core + 1) * SL, :] = yv
    return out


# version 08
# speedup vs baseline: 1.0083x; 1.0083x over previous
import math
import numpy as np
from contextlib import ExitStack

import concourse.bass as bass
import concourse.mybir as mybir
from concourse.ap import AP
from concourse.bass_utils import run_bass_kernel_spmd
from concourse.aot_env import allow_ici_all_to_all
import ml_dtypes

F32 = mybir.dt.float32
BF16 = mybir.dt.bfloat16
AF = mybir.ActivationFunctionType
ALU = mybir.AluOpType
AX = mybir.AxisListType

NCORES = 8
B, S, D = 16, 2048, 1024
BC = 2
H1, DH1 = 16, 64
H2, DH2 = 8, 128
RP = 192
NPOS = 2 * S - 1
WIN = S + 127
SL = S // NCORES
TS = B
NT1 = S // 128
NT2 = (TS * SL) // 128
EPS = 1e-5
SC2 = DH2 ** -0.5

ENGS = ("pe", "dve", "act", "pool", "sp")
SEMS = ("Lg", "Lw", "LwB", "Lr", "Lp0", "Lp1", "Lo", "Lx0", "Lx1", "Ls", "St0", "St1",
        "Wh", "Bt0", "Bt1", "A2", "pe", "dve", "act", "pool", "cc", "bar")


def _positions_T():
    d = np.arange(-S + 1, S, dtype=np.float64)
    n = RP // 6
    absd = np.abs(d)[:, None]
    max_range = math.log(S) / math.log(2.0)
    half_life = 2.0 ** np.linspace(3.0, max_range, n)
    feat_exp = np.exp(-math.log(2.0) / half_life[None, :] * absd)
    center_widths = 2.0 ** np.arange(1, n + 1, dtype=np.float64) - 1.0
    feat_cm = (center_widths[None, :] > absd).astype(np.float64)
    stddev = S / (2.0 * n)
    start_mean = S / n
    mean = np.linspace(start_mean, float(S), n)[None, :]
    conc = (mean / stddev) ** 2
    rate = mean / stddev ** 2
    with np.errstate(divide="ignore", invalid="ignore"):
        logd = np.where(absd > 0, np.log(np.maximum(absd, 1e-300)), 0.0)
        log_unnorm = (conc - 1.0) * logd - rate * absd
    log_unnorm = np.where(absd > 0, log_unnorm, -np.inf)
    lgam = np.array([math.lgamma(c) for c in conc[0]])[None, :]
    log_norm = lgam - conc * np.log(rate)
    probs = np.exp(log_unnorm - log_norm) + 1e-08
    feat_gamma = probs / np.max(probs, axis=-1, keepdims=True)
    emb = np.concatenate([feat_exp, feat_cm, feat_gamma], axis=-1)
    emb = np.concatenate([emb, np.sign(d)[:, None] * emb], axis=-1)
    return np.ascontiguousarray(emb.T.astype(np.float32))


class MK:
    def __init__(self, dev=None):
        self.dev = dev or {}
        self.nc = bass.Bass("TRN2", target_bir_lowering=False)
        self.q = {e: [] for e in ENGS}
        self.cnt = {s: 0 for s in SEMS}
        self.handles = {}
        self.dr = {}
        self.sb = {}
        self.ps = {}
        self.st = ExitStack()
        self.tr_guard = 0

    def on(self, eng, fn=None, wait=(), inc=None, incn=1):
        wait = list(wait)
        if fn is not None and eng in ("act", "dve"):
            assert inc in (None, eng)
            wait.append((eng, self.cnt[eng]))
            inc = eng
            incn = 1
        if inc is not None:
            self.cnt[inc] += incn
        tgt = self.cnt[inc] if inc is not None else None
        DMAS = ("Lg", "Lw", "LwB", "Lr", "Lp0", "Lp1", "Lo", "Lx0", "Lx1", "Ls", "St0",
                "St1", "Wh", "Bt0", "Bt1", "A2")
        wait = tuple((s, self.cnt[s] if s in DMAS else v) for (s, v) in wait if v)
        h = self.handles

        def thunk(e, _fn=fn, _wait=wait, _inc=inc, _incn=incn):
            for s, v in _wait:
                e.wait_ge(h[s], v)
            inst = _fn(e) if _fn is not None else None
            if _inc is not None:
                if inst is None:
                    e.sem_inc(h[_inc], _incn)
                else:
                    inst.then_inc(h[_inc], _incn)
        self.q[eng].append(thunk)
        return tgt

    def dma(self, eng, sem, out, in_, wait=(), slow=False):
        return self.on(eng, lambda e, o=out, i=in_, s=slow: e.dma_start(
            out=o, in_=i, allow_slow_non_contiguous=s), wait=wait, inc=sem, incn=16)

    def barrier(self):
        for e in ENGS:
            self.on(e, None, inc="bar")
        tgt = self.cnt["bar"]
        for e in ENGS:
            self.on(e, None, wait=[("bar", tgt)])
        return tgt

    def din(self, name, shape, dtype=F32):
        self.dr[name] = self.nc.dram_tensor(name, shape, dtype, kind="ExternalInput")
        return self.dr[name]

    def dout(self, name, shape, dtype=F32):
        self.dr[name] = self.nc.dram_tensor(name, shape, dtype, kind="ExternalOutput")
        return self.dr[name]

    def dint(self, name, shape, dtype=F32):
        self.dr[name] = self.nc.dram_tensor(name, shape, dtype)
        return self.dr[name]

    def SB(self, name, shape, dtype, stack=None):
        t = (stack or self.st).enter_context(self.nc.sbuf_tensor(name, shape, dtype))
        self.sb[name] = t
        return t

    def PSm(self, name, shape, dtype=F32, stack=None):
        t = (stack or self.st).enter_context(self.nc.psum_tensor(name, shape, dtype))
        self.ps[name] = t
        return t

    def trblk(self, src, dst, wait=()):
        tr = self.ps["tr"]
        eye = self.sb["s_eye"]
        self.on("pe", lambda e, s=src, t=tr[:, 0:128], y=eye[:, :]: e.transpose(t, s, y),
                wait=list(wait) + [("dve", self.tr_guard)], inc="pe")
        pe0 = self.cnt["pe"]
        self.on("dve", lambda e, d=dst, t=tr[:, 0:128]: e.tensor_copy(d, t),
                wait=[("pe", pe0)], inc="dve")
        self.tr_guard = self.cnt["dve"]
        return self.tr_guard

    def declare(self):
        din, dout, dint = self.din, self.dout, self.dint
        din("x0", [BC * S, D])
        for nm in ("aWq", "aWk", "aWv", "aWo", "bwq", "bwk", "bwv", "bwout",
                   "fwf", "fuf", "fwi", "fui", "fwc", "fuc", "fwo", "fuo"):
            din(nm, [D, D])
        for nm in ("abq", "abk", "abv", "abo", "bbout", "fbf", "fbi", "fbc",
                   "fbo", "finc"):
            din(nm, [D])
        din("adecay", [H1, 2]); din("amix", [H1, 2])
        din("bwrel", [RP, D]); din("breb", [H2, DH2]); din("brpb", [H2, DH2])
        din("eye", [128, 128], BF16)
        din("posT", [RP, NPOS])
        din("absd", [128, 128])
        dout("y", [TS * SL, D])
        dint("a2a_in", [NCORES, BC, SL, D])
        dint("a2a_out", [NCORES, BC, SL, D])
        dint("win_hbm", [127 + S * WIN + WIN], BF16)
        for tname, tshape in self.dev.get("taps", {}).items():
            dout(tname, tshape)
        if self.dev.get("devffn"):
            din("devffn", [NCORES, BC, SL, D])

    def alloc_global(self):
        SB = self.SB
        SB("s_eye", [128, 128], BF16)
        SB("lnt", [128, D], BF16)
        SB("stats", [128, 16], F32)
        SB("onesb", [128, 512], BF16)
        SB("onesc", [128, 1], BF16)
        SB("repsb", [128, 512], F32)
        SB("scr", [128, 512], F32)
        SB("scr2", [128, 512], F32)

        SB("amixs2", [16, 2], F32)
        for nm in ("abq_t", "abk_t", "abv_t", "fbf_t", "fbi_t", "fbc_t", "finc_t"):
            SB(nm, [128, 8], F32)
        SB("breb_t", [128, H2], F32)
        SB("brpb_t", [128, H2], F32)
        SB("abo_bf", [1, D], BF16)
        SB("bbout_bf", [1, D], BF16)
        SB("fbo_bf", [1, D], BF16)
        SB("adp", [H1, 2], F32)
        SB("amixs", [H1, 2], F32)
        SB("alogT", [2, H1], F32)
        SB("alogTn", [2, H1], F32)
        SB("alogT2047", [2, H1], F32)
        SB("alogT127", [2, H1], F32)
        SB("amixT", [2, H1], F32)
        SB("alogF", [1, 2 * H1], F32)
        SB("amixF", [1, 2 * H1], F32)
        SB("alogFb", [1, 2 * H1], BF16)
        SB("amixFb", [1, 2 * H1], BF16)
        SB("s_absd", [128, 128], F32)
        SB("epsc", [128, 1], F32)
        SB("nb30", [128, 1], F32)

    def setup(self):
        sb, dr, dma, on = self.sb, self.dr, self.dma, self.on

        def col_ap(v):
            return AP(v.ap().tensor, 0, [[1, 128], [128, 8]])
        dma("pool", "Lg", sb["s_eye"][:, :], dr["eye"].ap())
        dma("pool", "Lg", sb["s_absd"][:, :], dr["absd"].ap())
        for nm, src in (("abq_t", "abq"), ("abk_t", "abk"), ("abv_t", "abv"),
                        ("fbf_t", "fbf"), ("fbi_t", "fbi"), ("fbc_t", "fbc"),
                        ("finc_t", "finc")):
            dma("pool", "Lg", sb[nm][:, :], col_ap(dr[src]), slow=True)
        for nm, src in (("abo_bf", "abo"), ("bbout_bf", "bbout"), ("fbo_bf", "fbo")):
            dma("pool", "Lg", sb[nm][0:1, :], AP(dr[src].ap().tensor, 0, [[D, 1], [1, D]]))
        dma("pool", "Lg", sb["breb_t"][:, :],
            AP(dr["breb"].ap().tensor, 0, [[1, DH2], [DH2, H2]]), slow=True)
        dma("pool", "Lg", sb["brpb_t"][:, :],
            AP(dr["brpb"].ap().tensor, 0, [[1, DH2], [DH2, H2]]), slow=True)
        dma("pool", "Lg", sb["adp"][:, :], dr["adecay"].ap())
        dma("pool", "Lg", sb["amixs"][:, :], dr["amix"].ap())
        lg = self.cnt["Lg"]
        on("dve", lambda e, o=sb["onesb"][:, :]: e.memset(o, 1.0), inc="dve")
        on("dve", lambda e, o=sb["onesc"][:, :]: e.memset(o, 1.0), inc="dve")
        on("dve", lambda e, o=sb["epsc"][:, :]: e.memset(o, EPS), inc="dve")
        on("dve", lambda e, o=sb["nb30"][:, :]: e.memset(o, -60.0), inc="dve")
        on("dve", lambda e, o=sb["adp"][:, :]: e.tensor_scalar(
            out=o, in0=o, scalar1=0.01, scalar2=0.9999, op0=ALU.max, op1=ALU.min),
            wait=[("Lg", lg)], inc="dve")
        dv = self.cnt["dve"]
        on("act", lambda e, o=sb["stats"][0:H1, 8:10], i=sb["adp"][:, :]: e.activation(
            out=o, in_=i, func=AF.Ln), wait=[("dve", dv)], inc="act")
        ln_ac = self.cnt["act"]
        on("dve", lambda e, o=sb["stats"][0:H1, 0:1], i=sb["amixs"][:, 0:1],
           j=sb["amixs"][:, 1:2]: e.tensor_tensor(out=o, in0=i, in1=j, op=ALU.subtract),
           wait=[("Lg", lg)], inc="dve")
        dvd = self.cnt["dve"]
        on("act", lambda e, o=sb["amixs2"][:, 0:1], i=sb["stats"][0:H1, 0:1]:
           e.activation(out=o, in_=i, func=AF.Sigmoid), wait=[("dve", dvd)], inc="act")
        on("act", lambda e, o=sb["amixs2"][:, 1:2], i=sb["stats"][0:H1, 0:1]:
           e.activation(out=o, in_=i, func=AF.Sigmoid, scale=-1.0), inc="act")
        ac2 = self.cnt["act"]
        on("dve", None, wait=[("act", ac2)], inc="dve")
        dv = self.cnt["dve"]
        for h in range(H1):
            for n in range(2):
                dma("pool", "Lg", sb["alogT"][n:n + 1, h:h + 1],
                    sb["stats"][h:h + 1, 8 + n:9 + n], wait=[("act", ln_ac), ("dve", dv)],
                    slow=True)
                dma("pool", "Lg", sb["amixT"][n:n + 1, h:h + 1],
                    sb["amixs2"][h:h + 1, n:n + 1], wait=[("dve", dv)], slow=True)
                dma("pool", "Lg", sb["alogF"][0:1, 2 * h + n:2 * h + n + 1],
                    sb["stats"][h:h + 1, 8 + n:9 + n], wait=[("act", ln_ac)], slow=True)
                dma("pool", "Lg", sb["amixF"][0:1, 2 * h + n:2 * h + n + 1],
                    sb["amixs2"][h:h + 1, n:n + 1], wait=[("dve", dv)], slow=True)
        lg = self.cnt["Lg"]
        on("dve", lambda e, o=sb["alogTn"][:, :], i=sb["alogT"][:, :]: e.tensor_scalar(
            out=o, in0=i, scalar1=-1.0, scalar2=None, op0=ALU.mult),
            wait=[("Lg", lg)], inc="dve")
        on("dve", lambda e, o=sb["alogT2047"][:, :], i=sb["alogT"][:, :]: e.tensor_scalar(
            out=o, in0=i, scalar1=2047.0, scalar2=None, op0=ALU.mult), inc="dve")
        on("dve", lambda e, o=sb["alogT127"][:, :], i=sb["alogT"][:, :]: e.tensor_scalar(
            out=o, in0=i, scalar1=127.0, scalar2=None, op0=ALU.mult), inc="dve")
        on("dve", lambda e, o=sb["alogFb"][:, :], i=sb["alogF"][:, :]: e.tensor_copy(o, i),
           wait=[("Lg", self.cnt["Lg"])], inc="dve")
        on("dve", lambda e, o=sb["amixFb"][:, :], i=sb["amixF"][:, :]: e.tensor_copy(o, i),
           inc="dve")
        self.on("pool", None, wait=[("Lg", self.cnt["Lg"])])
        if self.dev.get("dbg0"):
            dv = self.cnt["dve"]
            for nm, apx in (("t_amixs", sb["amixs"][:, :]), ("t_stats", sb["stats"][:, 0:16]),
                            ("t_amixs2", sb["amixs2"][:, :]),
                            ("t_alogF", sb["alogF"][:, :]), ("t_amixF", sb["amixF"][:, :]),
                            ("t_alogT", sb["alogT"][:, :]), ("t_amixT", sb["amixT"][:, :]),
                            ("t_adp", sb["adp"][:, :])):
                if nm in self.dr:
                    self.dma("pool", "A2", self.dr[nm].ap(), apx,
                             wait=[("dve", dv), ("Lg", self.cnt["Lg"])])
            self.on("sp", None, wait=[("A2", self.cnt["A2"])])
        self.barrier()

    def ln_tile(self, xt, lnxT_col_fn, wait):
        sb, on = self.sb, self.on
        c1 = 1.0 / D
        stats = sb["stats"]
        wait = list(wait) + [("dve", getattr(self, "ln_guard", 0))]
        on("dve", lambda e, o=stats[:, 0:1], i=xt: e.tensor_reduce(
            out=o, in_=i, axis=AX.X, op=ALU.add), wait=wait, inc="dve")
        sqb = sb[self.ln_sq]
        on("act", lambda e, o=sqb[:, :], i=xt: e.activation(
            out=o, in_=i, func=AF.Square), wait=wait, inc="act")
        ac0 = self.cnt["act"]
        on("dve", lambda e, o=stats[:, 1:2], i=sqb[:, :]: e.tensor_reduce(
            out=o, in_=i, axis=AX.X, op=ALU.add), wait=[("act", ac0)], inc="dve")
        dv0 = self.cnt["dve"]
        on("dve", lambda e, o=stats[:, 2:3], i=stats[:, 0:1]: e.tensor_scalar(
            out=o, in0=i, scalar1=c1, scalar2=None, op0=ALU.mult),
            wait=[("dve", dv0), ("act", ac0)], inc="dve")
        on("dve", lambda e, o=stats[:, 3:4], i=stats[:, 2:3]: e.tensor_tensor(
            out=o, in0=i, in1=i, op=ALU.mult), inc="dve")
        on("dve", lambda e, o=stats[:, 4:5], i=stats[:, 1:2], j=stats[:, 3:4]:
            e.scalar_tensor_tensor(out=o, in0=i, scalar=c1, in1=j,
                                   op0=ALU.mult, op1=ALU.subtract),
            wait=[("act", getattr(self, "ln_act_guard", 0))], inc="dve")
        dv1 = self.cnt["dve"]
        self.ln_guard = dv1
        on("act", lambda e, o=stats[:, 5:6], i=stats[:, 4:5], b=sb["epsc"][:, 0:1]:
            e.activation(out=o, in_=i, func=AF.Sqrt, bias=b), wait=[("dve", dv1)], inc="act")
        self.ln_act_guard = self.cnt["act"]
        ac1 = self.cnt["act"]
        on("dve", lambda e, o=stats[:, 6:7], i=stats[:, 5:6]: e.reciprocal(out=o, in_=i),
           wait=[("act", ac1)], inc="dve")
        on("dve", lambda e, o=stats[:, 7:8], i=stats[:, 2:3], j=stats[:, 6:7]:
            e.tensor_tensor(out=o, in0=i, in1=j, op=ALU.mult), inc="dve")
        on("dve", lambda e, o=stats[:, 7:8], i=stats[:, 7:8]: e.tensor_scalar(
            out=o, in0=i, scalar1=-1.0, scalar2=None, op0=ALU.mult), inc="dve")
        dvr = self.cnt["dve"]
        lnt = sb["lnt"]
        on("act", lambda e, o=lnt[:, :], i=xt, r=stats[:, 6:7], b=stats[:, 7:8]:
            e.activation(out=o, in_=i, func=AF.Identity, scale=r, bias=b),
            wait=[("dve", dvr)], inc="act")
        ac2 = self.cnt["act"]
        on("dve", None, wait=[("act", ac2)], inc="dve")
        dv2 = self.cnt["dve"]
        for ft in range(8):
            self.trblk(lnt[:, ft * 128:(ft + 1) * 128], lnxT_col_fn(ft),
                       wait=[("dve", dv2)])
        return self.cnt["dve"]

    def attn_a(self, src="x0", export="a2a"):
        sb, dr, ps, on, dma = self.sb, self.dr, self.ps, self.on, self.dma
        stk = ExitStack()
        SB = lambda n, s, d: self.SB(n, s, d, stk)
        x = SB("x", [128, NT1 * D], F32)
        lnxT = SB("lnxT", [128, 8 * S], BF16)
        outT = SB("outT", [128, 8 * S], BF16)
        iota = SB("iota", [2, S], mybir.dt.uint16)
        tasc = SB("tasc", [2, S], BF16)
        tdesc = SB("tdesc", [2, S], BF16)
        mlo = SB("mlo", [2, 128], BF16)
        mup = SB("mup", [2, 128], BF16)
        diag = SB("diag", [128, H1 * 128], BF16)
        rden = SB("rden", [128, 512], BF16)
        wstg = SB("wstg", [128, D], BF16)
        wstgB = SB("wstgB", [128, D], BF16)
        wsl = [wstg, wstgB]
        wpe = [self.cnt["pe"], self.cnt["pe"]]
        wix = [0]
        wT = SB("wT", [128, D], BF16)
        bufA = SB("bufA", [128, S], BF16)
        bufK = SB("bufK", [128, S], BF16)
        bufV = SB("bufV", [128, 2112], BF16)
        pt = SB("pt", [128, 1024], BF16)
        msk = SB("msk", [128, 1024], F32)
        self.ln_sq = "msk"
        SB("l128", [128, 2], F32)
        SB("e128", [128, 128], F32)
        SB("tt2", [2, 128], F32)
        SB("m128", [128, 2], F32)
        SB("dtmp", [128, 128], F32)
        SB("dtmp2", [128, 128], F32)
        woT = SB("woT", [128, 8 * D], BF16)
        pstk = ExitStack()
        psc = self.PSm("psc", [128, 1024], F32, pstk)
        ppv = self.PSm("ppv", [128, S], F32, pstk)
        paux = self.PSm("paux", [128, 512], F32, pstk)

        on("pool", lambda e, o=iota[:, :]: e.iota(
            o, [[1, S]], channel_multiplier=0, allow_small_or_imprecise_dtypes=True),
            inc="pool")
        pl0 = self.cnt["pool"]
        on("dve", lambda e, o=bufV[:, :]: e.memset(o, 1.0))
        on("dve", lambda e, o=psc[:, 0:1024]: e.memset(o, 0.0))
        for h in range(H1):
            on("pe", lambda e, o=paux[:, 0:2], l=sb["onesb"][0:1, 0:128],
               r=sb["alogFb"][0:1, 2 * h:2 * h + 2]: e.matmul(o, l, r, start=True, stop=True),
               wait=[("Lg", self.cnt["Lg"]), ("dve", self.cnt["dve"]),
                     ("act", self.cnt["act"])], inc="pe")
            pe0 = self.cnt["pe"]
            on("dve", lambda e, o=sb["l128"][:, :], i=paux[:, 0:2]: e.tensor_copy(o, i),
               wait=[("pe", pe0)], inc="dve")
            dvl = self.cnt["dve"]
            on("pe", lambda e, o=paux[:, 0:2], l=sb["onesb"][0:1, 0:128],
               r=sb["amixFb"][0:1, 2 * h:2 * h + 2]: e.matmul(o, l, r, start=True, stop=True),
               wait=[("dve", dvl)], inc="pe")
            pe0 = self.cnt["pe"]
            on("dve", lambda e, o=sb["m128"][:, :], i=paux[:, 0:2]: e.tensor_copy(o, i),
               wait=[("pe", pe0)], inc="dve")
            dvm = self.cnt["dve"]
            for n in range(2):
                o = sb["e128"][:, :] if n == 0 else sb["dtmp2"][:, :]
                on("act", lambda e, o=o, i=sb["s_absd"][:, :], s=sb["l128"][:, n:n + 1]:
                   e.activation(out=o, in_=i, func=AF.Exp, scale=s),
                   wait=[("dve", dvl)], inc="act")
            a2 = self.cnt["act"]
            dgs = diag[:, h * 128:(h + 1) * 128]
            on("act", lambda e, o=sb["dtmp"][:, :], i=sb["dtmp2"][:, :],
               s=sb["m128"][:, 1:2]: e.activation(
                   out=o, in_=i, func=AF.Identity, scale=s),
               wait=[("dve", dvm)], inc="act")
            on("act", lambda e, o=sb["dtmp2"][:, :], i=sb["e128"][:, :],
               s=sb["m128"][:, 0:1]: e.activation(
                   out=o, in_=i, func=AF.Identity, scale=s), inc="act")
            a3 = self.cnt["act"]
            on("dve", lambda e, o=dgs, i=sb["dtmp"][:, :], j=sb["dtmp2"][:, :]:
               e.tensor_tensor(out=o, in0=i, in1=j, op=ALU.add),
               wait=[("act", a3)], inc="dve")
        diag_done = self.cnt["dve"]
        woT_done = None

        prev_b = (0, 0)
        for b in range(BC):
            impw = [("dve", prev_b[0]), ("A2", prev_b[1]), ("cc", self.cnt["cc"])]
            if src == "x0":
                src_ap = AP(dr["x0"].ap().tensor, b * S * D,
                            [[D, 128], [128 * D, NT1], [1, D]])
                dst_ap = AP(x[:, :].tensor, 0, [[NT1 * D, 128], [D, NT1], [1, D]])
                xi = dma("sp", "Ls", dst_ap, src_ap, wait=impw)
            else:
                for th in range(2):
                    src_ap = AP(dr["a2a_out"].ap().tensor, b * SL * D + th * 128 * D,
                                [[D, 128], [BC * SL * D, NCORES], [1, D]])
                    dst_ap = AP(x[:, :].tensor, th * D,
                                [[NT1 * D, 128], [2 * D, NCORES], [1, D]])
                    xi = dma("sp", "Ls", dst_ap, src_ap, wait=impw if th == 0 else ())
            for tt in range(NT1):
                self.ln_tile(x[:, tt * D:(tt + 1) * D],
                             lambda ft, tt=tt: lnxT[:, ft * S + tt * 128: ft * S + (tt + 1) * 128],
                             [("Ls", xi)])
            ln_dve = self.cnt["dve"]

            for pair in range(H1 // 2):
                r0 = pair * 128
                projs = (("aWv", "abv_t", bufA, True), ("aWq", "abq_t", bufA, False),
                         ("aWk", "abk_t", bufK, False))
                for wname, bname, tgt, is_v in projs:
                    sl = wix[0] % 2
                    wix[0] += 1
                    stg = wsl[sl]
                    wsem = ("Lw", "LwB")[sl]
                    lw = dma("pool", wsem, stg[:, :], dr[wname].ap()[r0:r0 + 128, :],
                             wait=[("pe", wpe[sl])])
                    for ec in range(8):
                        self.trblk(stg[:, ec * 128:(ec + 1) * 128],
                                   wT[:, ec * 128:(ec + 1) * 128],
                                   wait=[(wsem, lw)] if ec == 0 else ())
                    wpe[sl] = self.cnt["pe"]
                    tdone = self.cnt["dve"]
                    for qc in range(4):
                        for ec in range(8):
                            on("pe", lambda e, o=psc[:, 0:512],
                               l=wT[:, ec * 128:(ec + 1) * 128],
                               r=lnxT[:, ec * S + qc * 512: ec * S + qc * 512 + 512],
                               s=(ec == 0), t=(ec == 7): e.matmul(o, l, r, start=s, stop=t),
                               wait=[("dve", tdone), ("dve", ln_dve)] if ec == 0 else (),
                               inc="pe")
                        pe0 = self.cnt["pe"]
                        on("act", lambda e, o=tgt[:, qc * 512:(qc + 1) * 512],
                           i=psc[:, 0:512], bi=sb[bname][:, pair:pair + 1]:
                           e.activation(out=o, in_=i, func=AF.Identity, bias=bi),
                           wait=[("pe", pe0)], inc="act")
                        on("pe", None, wait=[("act", self.cnt["act"])])
                    if is_v:
                        acv = self.cnt["act"]
                        tr = ps["tr"]
                        for kt in range(NT1):
                            on("pe", lambda e, s=bufA[:, kt * 128:(kt + 1) * 128],
                               t=tr[:, 0:128], y=sb["s_eye"][:, :]: e.transpose(t, s, y),
                               wait=([("act", acv)] if kt == 0 else []) +
                                    [("dve", self.tr_guard)], inc="pe")
                            pe0 = self.cnt["pe"]
                            on("dve", lambda e, o=bufV[:, kt * 129: kt * 129 + 64],
                               t=tr[:, 0:64]: e.tensor_copy(o, t),
                               wait=[("pe", pe0)], inc="dve")
                            on("dve", lambda e, o=bufV[:, kt * 129 + 65: kt * 129 + 129],
                               t=tr[:, 64:128]: e.tensor_copy(o, t), inc="dve")
                            self.tr_guard = self.cnt["dve"]
                proj_act = self.cnt["act"]

                for hi in range(2):
                    h = 2 * pair + hi
                    ho = 64 * hi
                    pe_now = self.cnt["pe"]
                    on("act", lambda e, o=tasc[:, :], i=iota[:, :],
                       s=sb["alogT"][:, h:h + 1]: e.activation(
                           out=o, in_=i, func=AF.Exp, scale=s),
                       wait=[("pool", pl0), ("pe", pe_now)], inc="act")
                    on("act", lambda e, o=tdesc[:, :], i=iota[:, :],
                       s=sb["alogTn"][:, h:h + 1], bi=sb["alogT2047"][:, h:h + 1]:
                       e.activation(out=o, in_=i, func=AF.Exp, scale=s, bias=bi),
                       inc="act")
                    on("act", lambda e, o=sb["tt2"][:, :], i=iota[:, 0:128],
                       s=sb["alogTn"][:, h:h + 1], bi=sb["alogT127"][:, h:h + 1]:
                       e.activation(out=o, in_=i, func=AF.Exp, scale=s, bias=bi),
                       inc="act")
                    on("act", lambda e, o=mlo[:, :], i=sb["tt2"][:, :],
                       s=sb["amixT"][:, h:h + 1]: e.activation(
                           out=o, in_=i, func=AF.Identity, scale=s), inc="act")
                    on("act", lambda e, o=sb["tt2"][:, :], i=iota[:, 0:128],
                       s=sb["alogT"][:, h:h + 1]: e.activation(
                           out=o, in_=i, func=AF.Exp, scale=s), inc="act")
                    on("act", lambda e, o=mup[:, :], i=sb["tt2"][:, :],
                       s=sb["amixT"][:, h:h + 1]: e.activation(
                           out=o, in_=i, func=AF.Identity, scale=s), inc="act")
                    tab_act = self.cnt["act"]
                    on("dve", None, wait=[("act", tab_act)], inc="dve")
                    tab_done = self.cnt["dve"]

                    mult_done = 0
                    mult_hist = [0, 0]
                    exp_done = proj_act
                    exp_hist = [proj_act, proj_act]
                    mevac_done = 0
                    mevac_hist = [proj_act, proj_act]
                    ptf = {0: 0, 1: 0}
                    ft = h // 2
                    for qc in range(4):
                        q0c, q1c = qc * 512, qc * 512 + 512
                        pend = None
                        for kt in range(NT1):
                            k0 = kt * 128
                            mb = kt % 2
                            bm = psc[:, mb * 512:(mb + 1) * 512]
                            bs = psc[:, (1 - mb) * 512:(2 - mb) * 512]
                            scrb = sb["scr"][:, :] if mb == 0 else sb["scr2"][:, :]
                            msl = msk[:, mb * 512:(mb + 1) * 512]
                            mwait = [("dve", tab_done), ("act", mevac_hist[mb]),
                                     ("dve", mult_hist[mb])]
                            ju = min(k0, q1c)
                            if ju > q0c:
                                ln_ = ju - q0c
                                on("pe", lambda e, o=bm[:, 0:ln_], l=mup[:, :],
                                   r=tdesc[:, q0c + 2047 - k0: q0c + 2047 - k0 + ln_]:
                                   e.matmul(o, l, r, start=True, stop=True),
                                   wait=mwait, inc="pe")
                            jl = max(k0 + 128, q0c)
                            if q1c > jl:
                                ln_ = q1c - jl
                                off = jl - q0c
                                on("pe", lambda e, o=bm[:, off:off + ln_], l=mlo[:, :],
                                   r=tasc[:, jl - k0 - 127: jl - k0 - 127 + ln_]:
                                   e.matmul(o, l, r, start=True, stop=True),
                                   wait=mwait, inc="pe")
                            pe_m = self.cnt["pe"]
                            on("pe", lambda e, o=bs,
                               l=bufK[ho:ho + 64, kt * 128:(kt + 1) * 128],
                               r=bufA[ho:ho + 64, qc * 512:(qc + 1) * 512]:
                               e.matmul(o, l, r, start=True, stop=True),
                               wait=[("act", mevac_hist[1 - mb]),
                                     ("dve", mult_hist[1 - mb])], inc="pe")
                            pe_s = self.cnt["pe"]
                            if pend is not None:
                                pend()
                            on("act", lambda e, o=msl, i=bm: e.activation(
                                out=o, in_=i, func=AF.Copy),
                                wait=[("pe", pe_m), ("dve", mult_hist[mb])], inc="act")
                            mevac_done = mevac_hist[mb] = self.cnt["act"]
                            dg_done = 0
                            if q0c <= k0 < q1c:
                                off = k0 - q0c
                                on("dve", lambda e, o=msl[:, off:off + 128],
                                   i=diag[:, h * 128:(h + 1) * 128]: e.tensor_copy(o, i),
                                   wait=[("act", mevac_done), ("dve", diag_done)],
                                   inc="dve")
                                dg_done = self.cnt["dve"]
                            on("dve", lambda e, o=scrb, m=msl,
                               i=bs: e.tensor_tensor(
                                out=o, in0=m, in1=i, op=ALU.mult),
                                wait=[("pe", pe_s), ("act", mevac_done),
                                      ("dve", dg_done), ("act", exp_hist[mb])],
                                inc="dve")
                            mult_done = mult_hist[1 - mb] = self.cnt["dve"]
                            pb = kt % 2
                            on("act", lambda e, o=pt[:, pb * 512:(pb + 1) * 512],
                               i=scrb: e.activation(
                                   out=o, in_=i, func=AF.Exp, scale=0.125),
                                wait=[("dve", mult_done), ("pe", ptf[pb])], inc="act")
                            exp_done = exp_hist[mb] = self.cnt["act"]

                            def mk_pend(kt=kt, pb=pb, exp_done=exp_done, qc=qc,
                                        ho=ho, hi=hi):
                                def pend_():
                                    if hi == 0:
                                        on("pe", lambda e,
                                           o=ppv[0:65, qc * 512:(qc + 1) * 512],
                                           l=bufV[:, kt * 129: kt * 129 + 65],
                                           r=pt[:, pb * 512:(pb + 1) * 512],
                                           s=(kt == 0), t=(kt == NT1 - 1):
                                           e.matmul(o, l, r, start=s, stop=t),
                                           wait=[("act", exp_done)], inc="pe")
                                    else:
                                        on("pe", lambda e, o=paux[0:1, 0:512],
                                           l=sb["onesc"][:, 0:1],
                                           r=pt[:, pb * 512:(pb + 1) * 512],
                                           s=(kt == 0), t=(kt == NT1 - 1):
                                           e.matmul(o, l, r, start=s, stop=t),
                                           wait=[("act", exp_done)], inc="pe")
                                        on("pe", lambda e,
                                           o=ppv[64:128, qc * 512:(qc + 1) * 512],
                                           l=bufV[:, kt * 129 + 65: kt * 129 + 129],
                                           r=pt[:, pb * 512:(pb + 1) * 512],
                                           s=(kt == 0), t=(kt == NT1 - 1), tp=(0, 64):
                                           e.matmul(o, l, r, start=s, stop=t,
                                                    tile_position=tp), inc="pe")
                                    ptf[pb] = self.cnt["pe"]
                                return pend_
                            pend = mk_pend()
                        pend()
                        pend = None
                        pv_done = self.cnt["pe"]
                        drow = 64 if hi == 0 else 0
                        dsrc = (ppv[64:65, qc * 512:(qc + 1) * 512] if hi == 0
                                else paux[0:1, 0:512])
                        on("dve", lambda e, o=rden[drow:drow + 1, 0:512], i=dsrc:
                           e.reciprocal(out=o, in_=i), wait=[("pe", pv_done)], inc="dve")
                        dv0 = self.cnt["dve"]
                        on("pe", lambda e, o=paux[:, 0:512],
                           l=sb["onesb"][drow:drow + 1, 0:128],
                           r=rden[drow:drow + 1, 0:512]:
                           e.matmul(o, l, r, start=True, stop=True),
                           wait=[("dve", dv0), ("act", mevac_done)], inc="pe")
                        pe0 = self.cnt["pe"]
                        on("dve", lambda e, o=sb["repsb"][:, 0:512], i=paux[:, 0:512]:
                           e.tensor_copy(o, i), wait=[("pe", pe0)], inc="dve")
                        dv1 = self.cnt["dve"]
                        on("dve", lambda e,
                           o=outT[ho:ho + 64, ft * S + qc * 512: ft * S + (qc + 1) * 512],
                           i=ppv[ho:ho + 64, qc * 512:(qc + 1) * 512],
                           r=sb["repsb"][ho:ho + 64, 0:512]: e.tensor_tensor(
                               out=o, in0=i, in1=r, op=ALU.mult),
                           wait=[("dve", dv1)], inc="dve")
                        on("pe", None, wait=[("dve", self.cnt["dve"])])
                    on("act", None, wait=[("dve", self.cnt["dve"])])
                    if self.dev.get("dbg3") and pair == self.dev.get("dbg3_pair", 0) and hi == 1:
                        self.dma("pool", "A2", dr["t_outT_all"].ap(), outT[:, :],
                                 wait=[("dve", self.cnt["dve"])])
                        if "t_rden2" in dr:
                            self.dma("pool", "A2", dr["t_rden2"].ap(), rden[:, :],
                                     wait=[("dve", self.cnt["dve"])])
                        if "t_repsb" in dr:
                            self.dma("pool", "A2", dr["t_repsb"].ap(), sb["repsb"][:, :],
                                     wait=[("dve", self.cnt["dve"])])
                        self.on("sp", None, wait=[("A2", self.cnt["A2"])])
                        self.barrier()
                        pstk.close()
                        stk.close()
                        return
                    if self.dev.get("dbg1"):
                        dv = self.cnt["dve"]
                        for nm, apx in (("t_tasc", tasc[:, :]), ("t_tdesc", tdesc[:, :]),
                                        ("t_mlo", mlo[:, :]), ("t_mup", mup[:, :]),
                                        ("t_msk", msk[:, :]), ("t_pt", pt[:, :]),
                                        ("t_rden", rden[:, :]),
                                        ("t_outT", outT[:, 0:S]),
                                        ("t_lnxT", lnxT[:, 0:S]),
                                        ("t_bufA", bufA[:, :]), ("t_bufK", bufK[:, :]),
                                        ("t_bufV", bufV[:, :]),
                                        ("t_diag", diag[:, 0:256])):
                            if nm in dr:
                                self.dma("pool", "A2", dr[nm].ap(), apx,
                                         wait=[("dve", dv)])
                        self.on("sp", None, wait=[("A2", self.cnt["A2"])])
                        self.barrier()
                        pstk.close()
                        stk.close()
                        return
            heads_dve = self.cnt["dve"]
            if self.dev.get("dbg2"):
                self.dma("pool", "A2", dr["t_outT_all"].ap(), outT[:, :],
                         wait=[("dve", heads_dve)])
                self.dma("pool", "A2", dr["t_rden2"].ap(), rden[:, :],
                         wait=[("dve", heads_dve)])
                self.on("sp", None, wait=[("A2", self.cnt["A2"])])
                self.barrier()
                pstk.close()
                stk.close()
                return

            if woT_done is None:
                for fot in range(8):
                    sl = wix[0] % 2
                    wix[0] += 1
                    stg = wsl[sl]
                    wsem = ("Lw", "LwB")[sl]
                    lw = dma("pool", wsem, stg[:, :],
                             dr["aWo"].ap()[fot * 128:(fot + 1) * 128, :],
                             wait=[("pe", wpe[sl])])
                    for ec in range(8):
                        self.trblk(stg[:, ec * 128:(ec + 1) * 128],
                                   woT[:, ec * D + fot * 128: ec * D + fot * 128 + 128],
                                   wait=[(wsem, lw)] if ec == 0 else ())
                    wpe[sl] = self.cnt["pe"]
                woT_done = self.cnt["dve"]
            for st2 in range(NT1):
                for foc in range(2):
                    for fic in range(8):
                        on("pe", lambda e, o=psc[:, 0:512],
                           l=outT[:, fic * S + st2 * 128: fic * S + st2 * 128 + 128],
                           r=woT[:, fic * D + foc * 512: fic * D + foc * 512 + 512],
                           s=(fic == 0): e.matmul(o, l, r, start=s, stop=False),
                           wait=[("dve", woT_done), ("dve", heads_dve)] if fic == 0 else (),
                           inc="pe")
                    on("pe", lambda e, o=psc[:, 0:512], l=sb["onesb"][0:1, 0:128],
                       r=sb["abo_bf"][0:1, foc * 512:(foc + 1) * 512]:
                       e.matmul(o, l, r, start=False, stop=True, tile_position=(0, 0)),
                       inc="pe")
                    pe0 = self.cnt["pe"]
                    on("dve", lambda e,
                       o=x[:, st2 * D + foc * 512: st2 * D + foc * 512 + 512],
                       i=psc[:, 0:512]: e.tensor_tensor(out=o, in0=o, in1=i, op=ALU.add),
                       wait=[("pe", pe0)], inc="dve")
                    on("pe", None, wait=[("dve", self.cnt["dve"])])
            res_done = self.cnt["dve"]
            if export == "a2a":
                for th in range(2):
                    eap = AP(dr["a2a_in"].ap().tensor, b * SL * D + th * 128 * D,
                             [[D, 128], [BC * SL * D, NCORES], [1, D]])
                    sap = AP(x[:, :].tensor, th * D,
                             [[NT1 * D, 128], [2 * D, NCORES], [1, D]])
                    ex = self.dma("pool", "A2", eap, sap, wait=[("dve", res_done)])
            else:
                eap = AP(dr[export].ap().tensor, b * S * D, [[D, 128], [128 * D, NT1], [1, D]])
                sap = AP(x[:, :].tensor, 0, [[NT1 * D, 128], [D, NT1], [1, D]])
                ex = self.dma("pool", "A2", eap, sap, wait=[("dve", res_done)])
            prev_b = (res_done, ex)
        self.on("sp", None, wait=[("A2", self.cnt["A2"])])
        self.barrier()
        pstk.close()
        stk.close()

    def a2a(self):
        cw = [("A2", self.cnt["A2"]), ("St0", 1), ("St1", 1)]
        ins = [self.dr["a2a_in"].ap().opt()]
        outs = [self.dr["a2a_out"].ap().opt()]

        def fn(e, ins=ins, outs=outs):
            with allow_ici_all_to_all():
                return e.collective_compute(
                    "AllToAll", ALU.bypass, replica_groups=[list(range(NCORES))],
                    ins=ins, outs=outs)
        self.on("pool", fn, wait=cw, inc="cc")

    def ffn(self, dst="a2a_in", src="a2a_out"):
        sb, dr, ps, on, dma = self.sb, self.dr, self.ps, self.on, self.dma
        stk = ExitStack()
        SB = lambda n, s, d: self.SB(n, s, d, stk)
        uniq = "f%d" % self.cnt["cc"]
        NTOK = NT2 * 128
        lnxT = SB("lnxT" + uniq, [128, 8 * NTOK], BF16)
        cst = SB("c" + uniq, [128, 8 * SL], F32)
        SB("sq" + uniq, [128, D], F32)
        cbf = SB("cbf" + uniq, [128, 8 * SL], BF16)
        xt = [SB("xt0" + uniq, [128, D], F32), SB("xt1" + uniq, [128, D], F32)]
        gf = SB("gf" + uniq, [128, SL], F32)
        gi = SB("gi" + uniq, [128, SL], F32)
        gc = SB("gc" + uniq, [128, SL], F32)
        gtmp = SB("gtmp" + uniq, [128, SL], F32)
        wstg2 = SB("wstg2" + uniq, [128, D], BF16)
        pstk = ExitStack()
        pg = {"f": self.PSm("pgf" + uniq, [128, 512], F32, pstk),
              "i": self.PSm("pgi" + uniq, [128, 512], F32, pstk),
              "c": self.PSm("pgc" + uniq, [128, 512], F32, pstk)}
        py = self.PSm("py" + uniq, [128, 512], F32, pstk)

        def tile_ap(dram, tt):
            return AP(dram.ap().tensor,
                      (tt // 4) * (BC * SL * D) + ((tt // 2) % 2) * (SL * D)
                      + (tt % 2) * 128 * D,
                      [[D, 128], [1, D]])

        ccw = [("cc", self.cnt["cc"])]
        wstk = ExitStack()
        mats = {}
        for nm in ("fwf", "fuf", "fwi", "fui", "fwc", "fuc"):
            mats[nm] = self.SB(nm + uniq, [128, 8 * D], BF16, wstk)
            for fot in range(8):
                lw = dma("pool", "Lw", wstg2[:, :],
                         dr[nm].ap()[fot * 128:(fot + 1) * 128, :],
                         wait=[("dve", self.cnt["dve"])])
                for ec in range(8):
                    self.trblk(wstg2[:, ec * 128:(ec + 1) * 128],
                               mats[nm][:, ec * D + fot * 128: ec * D + fot * 128 + 128],
                               wait=[("Lw", lw)] if ec == 0 else ())
        mats_done = self.cnt["dve"]
        xld = []
        for tt in range(NT2):
            xb = xt[tt % 2]
            w = list(ccw)
            if tt >= 2:
                w.append(("dve", xld[tt - 2]))
            ld = dma("sp", "Lx%d" % (tt % 2), xb[:, :], tile_ap(dr[src], tt), wait=w)
            self.ln_sq = "sq" + uniq
            dv = self.ln_tile(xb[:, :],
                              lambda ft, tt=tt: lnxT[:, ft * NTOK + tt * 128: ft * NTOK + (tt + 1) * 128],
                              [("Lx%d" % (tt % 2), ld)])
            xld.append(dv)
        ln_dve = self.cnt["dve"]
        for fot in range(8):
            on("act", lambda e, o=cst[:, fot * SL:(fot + 1) * SL],
               i=sb["onesb"][:, 0:SL], b=sb["finc_t"][:, fot:fot + 1]: e.activation(
                   out=o, in_=i, func=AF.Identity, scale=0.0, bias=b), inc="act")
            ac9 = self.cnt["act"]
            on("dve", lambda e, o=cbf[:, fot * SL:(fot + 1) * SL],
               i=cst[:, fot * SL:(fot + 1) * SL]: e.tensor_copy(o, i),
               wait=[("act", ac9)], inc="dve")
        cbf_done = self.cnt["dve"]
        gates = (("fwf", "fuf", "f", gf, AF.Sigmoid, "fbf_t"),
                 ("fwi", "fui", "i", gi, AF.Sigmoid, "fbi_t"),
                 ("fwc", "fuc", "c", gc, AF.Tanh, "fbc_t"))
        for t in range(TS):
            gate_ev = getattr(self, "_gate_ev", None)
            if gate_ev is None or t == 0:
                gate_ev = self._gate_ev = {"f": 0, "i": 0, "c": 0}
            for fot in range(8):
                for wn, un, pn, gb, fn_, bn in gates:
                    pgn = pg[pn]
                    for fic in range(8):
                        on("pe", lambda e, o=pgn[:, 0:SL],
                           l=mats[wn][:, fic * D + fot * 128: fic * D + fot * 128 + 128],
                           r=lnxT[:, fic * NTOK + t * SL: fic * NTOK + (t + 1) * SL],
                           s=(fic == 0): e.matmul(o, l, r, start=s, stop=False),
                           wait=[("dve", mats_done), ("dve", ln_dve),
                                 ("dve", cbf_done), ("act", gate_ev[pn])]
                           if fic == 0 else (), inc="pe")
                    for fic in range(8):
                        on("pe", lambda e, o=pgn[:, 0:SL],
                           l=mats[un][:, fic * D + fot * 128: fic * D + fot * 128 + 128],
                           r=cbf[:, fic * SL:(fic + 1) * SL],
                           t_=(fic == 7): e.matmul(o, l, r, start=False, stop=t_),
                           inc="pe")
                    pe0 = self.cnt["pe"]
                    on("act", lambda e, o=gb[:, :], i=pgn[:, 0:SL], fn_=fn_,
                       bi=sb[bn][:, fot:fot + 1]: e.activation(
                           out=o, in_=i, func=fn_, bias=bi),
                       wait=[("pe", pe0), ("dve", self.cnt["dve"])], inc="act")
                    gate_ev[pn] = self.cnt["act"]
                ev = self.cnt["act"]
                cs = cst[:, fot * SL:(fot + 1) * SL]
                on("dve", lambda e, o=gtmp[:, :], i=gi[:, :], j=gc[:, :]:
                   e.tensor_tensor(out=o, in0=i, in1=j, op=ALU.mult),
                   wait=[("act", ev)], inc="dve")
                on("dve", lambda e, o=cs, i=gf[:, :]: e.tensor_tensor(
                    out=o, in0=o, in1=i, op=ALU.mult), inc="dve")
                on("dve", lambda e, o=cs, i=gtmp[:, :]: e.tensor_tensor(
                    out=o, in0=o, in1=i, op=ALU.add), inc="dve")
            step_pe = self.cnt["pe"]
            for fot in range(8):
                on("dve", lambda e, o=cbf[:, fot * SL:(fot + 1) * SL],
                   i=cst[:, fot * SL:(fot + 1) * SL]: e.tensor_copy(o, i),
                   wait=[("pe", step_pe)] if fot == 0 else (), inc="dve")
            cbf_done = self.cnt["dve"]
        scan_pe = self.cnt["pe"]
        wstk.close()
        fstk = ExitStack()
        woT = self.SB("fwoT" + uniq, [128, 8 * D], BF16, fstk)
        uoT = self.SB("fuoT" + uniq, [128, 8 * D], BF16, fstk)
        cpart = self.SB("cpart" + uniq, [128, 2 * D], F32, fstk)
        for nm, tgt in (("fwo", woT), ("fuo", uoT)):
            for fot in range(8):
                lw = dma("pool", "Lw", wstg2[:, :],
                         dr[nm].ap()[fot * 128:(fot + 1) * 128, :],
                         wait=[("pe", scan_pe), ("dve", self.cnt["dve"])])
                for ec in range(8):
                    self.trblk(wstg2[:, ec * 128:(ec + 1) * 128],
                               tgt[:, ec * D + fot * 128: ec * D + fot * 128 + 128],
                               wait=[("Lw", lw)] if ec == 0 else ())
        wo_done = self.cnt["dve"]
        if self.dev.get("dbgf") == self.cnt["cc"]:
            self.dma("pool", "A2", dr["t_flnxT"].ap(), lnxT[:, 0:NTOK],
                     wait=[("dve", wo_done)])
            self.dma("pool", "A2", dr["t_fcbf"].ap(), cbf[:, :],
                     wait=[("dve", wo_done)])
        for stl in range(2):
            for foc in range(2):
                for fmc in range(8):
                    on("pe", lambda e, o=py[:, 0:512],
                       l=cbf[:, fmc * SL + stl * 128: fmc * SL + stl * 128 + 128],
                       r=uoT[:, fmc * D + foc * 512: fmc * D + foc * 512 + 512],
                       s=(fmc == 0), t_=(fmc == 7): e.matmul(o, l, r, start=s, stop=t_),
                       wait=[("dve", wo_done), ("dve", cbf_done),
                             ("act", self.cnt["act"])] if fmc == 0 else (),
                       inc="pe")
                pe0 = self.cnt["pe"]
                on("act", lambda e, o=cpart[:, stl * D + foc * 512: stl * D + foc * 512 + 512],
                   i=py[:, 0:512]: e.activation(out=o, in_=i, func=AF.Copy),
                   wait=[("pe", pe0)], inc="act")
        cpart_done = self.cnt["act"]
        pass
        xold = {}
        for tt in range(2):
            xold[tt] = dma("sp", "Lx%d" % (tt % 2), xt[tt % 2][:, :], tile_ap(dr[src], tt),
                           wait=[("dve", ln_dve)])
        for tt in range(NT2):
            xb = xt[tt % 2]
            stl = tt % 2
            for foc in range(2):
                for fic in range(8):
                    on("pe", lambda e, o=py[:, 0:512],
                       l=lnxT[:, fic * NTOK + tt * 128: fic * NTOK + tt * 128 + 128],
                       r=woT[:, fic * D + foc * 512: fic * D + foc * 512 + 512],
                       s=(fic == 0): e.matmul(o, l, r, start=s, stop=False),
                       wait=[("act", cpart_done)] if fic == 0 else (), inc="pe")
                on("pe", lambda e, o=py[:, 0:512], l=sb["onesb"][0:1, 0:128],
                   r=sb["fbo_bf"][0:1, foc * 512:(foc + 1) * 512]:
                   e.matmul(o, l, r, start=False, stop=True, tile_position=(0, 0)),
                   inc="pe")
                pe0 = self.cnt["pe"]
                on("dve", lambda e, o=sb["scr"][:, :], j=py[:, 0:512],
                   i=cpart[:, stl * D + foc * 512: stl * D + foc * 512 + 512]:
                   e.tensor_tensor(out=o, in0=i, in1=j, op=ALU.add),
                   wait=[("pe", pe0), ("act", cpart_done),
                         ("act", self.cnt["act"])], inc="dve")
                dv0 = self.cnt["dve"]
                on("act", lambda e, o=sb["scr"][:, :]: e.activation(
                    out=o, in_=o, func=AF.Tanh), wait=[("dve", dv0)], inc="act")
                ac0 = self.cnt["act"]
                on("dve", lambda e, o=xb[:, foc * 512:(foc + 1) * 512],
                   i=sb["scr"][:, :]: e.tensor_tensor(out=o, in0=o, in1=i, op=ALU.add),
                   wait=[("act", ac0), ("Lx%d" % (tt % 2), xold[tt])], inc="dve")
                on("pe", None, wait=[("dve", self.cnt["dve"])])
            res_done = self.cnt["dve"]
            stsem = "St%d" % (tt % 2)
            if dst == "a2a_in":
                st_ap = tile_ap(dr["a2a_in"], tt)
            else:
                st_ap = dr["y"].ap()[tt * 128:(tt + 1) * 128, :]
            self.dma("pool", stsem, st_ap, xb[:, :], wait=[("dve", res_done)])
            nxt = tt + 2
            if nxt < NT2:
                xold[nxt] = dma("sp", "Lx%d" % (nxt % 2), xb[:, :], tile_ap(dr[src], nxt),
                                wait=[(stsem, 1)])
        self.on("sp", None, wait=[("St0", 1), ("St1", 1)])
        self.barrier()
        pstk.close()
        fstk.close()
        stk.close()

    def attn_b(self, src="a2a_out", export="a2a"):
        sb, dr, ps, on, dma = self.sb, self.dr, self.ps, self.on, self.dma
        stk = ExitStack()
        SB = lambda n, s, d: self.SB(n, s, d, stk)
        x = SB("bx", [128, NT1 * D], F32)
        SB("bsq", [128, D], F32)
        lnxT = SB("blnxT", [128, 8 * S], BF16)
        outT = SB("boutT", [128, 8 * S], BF16)
        relkT = SB("relkT", [128, NPOS + 1], BF16)
        qreb = SB("qreb", [128, S], BF16)
        qrpb = SB("qrpb", [128, S], BF16)
        kT = SB("bkT", [128, S], BF16)
        vv = SB("bvv", [128, S], BF16)
        winsb = SB("winsb", [128, WIN], BF16)
        bandT = SB("bandT", [128, 1024], BF16)
        pt = SB("bpt", [128, 1024], BF16)
        rden = SB("brden", [128, 512], BF16)
        wstg = SB("bwstg", [128, D], BF16)
        wrelstg = SB("wrelstg", [128, 256], BF16)
        pstg0 = SB("pstg0", [128, 1024], BF16)
        pstg1 = SB("pstg1", [64, 1024], BF16)
        woutb = SB("woutb", [128, 8 * D], BF16)
        band32 = SB("band32", [128, 512], F32)
        pstk = ExitStack()
        psc = self.PSm("bpsc", [128, 1024], F32, pstk)
        pwin = self.PSm("pwin", [128, 512], F32, pstk)
        ppv = self.PSm("bppv", [128, S], F32, pstk)
        wout_ld = None

        prev_b = (0, 0)
        for b in range(BC):
            impw = [("dve", prev_b[0]), ("A2", prev_b[1]), ("cc", self.cnt["cc"])]
            if src == "a2a_out":
                for th in range(2):
                    src_ap = AP(dr["a2a_out"].ap().tensor, b * SL * D + th * 128 * D,
                                [[D, 128], [BC * SL * D, NCORES], [1, D]])
                    dst_ap = AP(x[:, :].tensor, th * D,
                                [[NT1 * D, 128], [2 * D, NCORES], [1, D]])
                    xi = dma("sp", "Ls", dst_ap, src_ap, wait=impw if th == 0 else ())
            else:
                src_ap = AP(dr[src].ap().tensor, b * S * D,
                            [[D, 128], [128 * D, NT1], [1, D]])
                dst_ap = AP(x[:, :].tensor, 0, [[NT1 * D, 128], [D, NT1], [1, D]])
                xi = dma("sp", "Ls", dst_ap, src_ap, wait=impw)
            for tt in range(NT1):
                self.ln_sq = "bsq"
                self.ln_tile(x[:, tt * D:(tt + 1) * D],
                             lambda ft, tt=tt: lnxT[:, ft * S + tt * 128: ft * S + (tt + 1) * 128],
                             [("Ls", xi)])
            ln_dve = self.cnt["dve"]
            if self.dev.get("dbg8"):
                self.dma("pool", "A2", dr["t_vv"].ap(), lnxT[:, 0:S],
                         wait=[("dve", ln_dve)])
                if "t_x" in dr:
                    self.dma("pool", "A2", dr["t_x"].ap(), x[:, :],
                             wait=[("dve", ln_dve)])
                self.on("sp", None, wait=[("A2", self.cnt["A2"])])
                self.barrier()
                pstk.close()
                stk.close()
                return

            for h in range(H2):
                hc = h * 128
                for wname in ("bwq", "bwk", "bwv"):
                    lw = dma("pool", "Lw",
                             AP(wstg[:, :].tensor, 0, [[D, 128], [128, 8], [1, 128]]),
                             AP(dr[wname].ap().tensor, hc, [[D, 128], [128 * D, 8], [1, 128]]),
                             wait=[("pe", self.cnt["pe"])])
                    for qc in range(4):
                        for ec in range(8):
                            on("pe", lambda e, o=psc[:, 0:512],
                               l=wstg[:, ec * 128:(ec + 1) * 128],
                               r=lnxT[:, ec * S + qc * 512: ec * S + qc * 512 + 512],
                               s=(ec == 0), t=(ec == 7)
                               : e.matmul(o, l, r, start=s, stop=t),
                               wait=[("Lw", lw), ("dve", ln_dve)] if ec == 0 else (),
                               inc="pe")
                        pe0 = self.cnt["pe"]
                        if wname == "bwq":
                            on("act", lambda e, o=qreb[:, qc * 512:(qc + 1) * 512],
                               i=psc[:, 0:512], bi=sb["breb_t"][:, h:h + 1]:
                               e.activation(out=o, in_=i, func=AF.Identity, scale=SC2, bias=bi),
                               wait=[("pe", pe0)], inc="act")
                            on("act", lambda e, o=qrpb[:, qc * 512:(qc + 1) * 512],
                               i=psc[:, 0:512], bi=sb["brpb_t"][:, h:h + 1]:
                               e.activation(out=o, in_=i, func=AF.Identity, scale=SC2, bias=bi),
                               inc="act")
                        elif wname == "bwk":
                            on("act", lambda e, o=kT[:, qc * 512:(qc + 1) * 512],
                               i=psc[:, 0:512]: e.activation(out=o, in_=i, func=AF.Copy),
                               wait=[("pe", pe0)], inc="act")
                        else:
                            pbh = 0 if qc % 2 == 0 else 512
                            on("act", lambda e, o=pt[:, pbh:pbh + 512],
                               i=psc[:, 0:512]: e.activation(out=o, in_=i, func=AF.Copy),
                               wait=[("pe", pe0), ("dve", self.cnt["dve"])], inc="act")
                        on("pe", None, wait=[("act", self.cnt["act"])])
                        if wname == "bwv":
                            pbh = 0 if qc % 2 == 0 else 512
                            acv = self.cnt["act"]
                            for blk in range(4):
                                self.trblk(pt[:, pbh + blk * 128: pbh + (blk + 1) * 128],
                                           vv[:, (qc * 4 + blk) * 128: (qc * 4 + blk + 1) * 128],
                                           wait=[("act", acv)] if blk == 0 else ())
                vv_done = self.cnt["dve"]
                if self.dev.get("dbg6"):
                    self.dma("pool", "A2", dr["t_qreb"].ap(), qreb[:, :],
                             wait=[("dve", vv_done), ("act", self.cnt["act"])])
                    self.dma("pool", "A2", dr["t_vv"].ap(), vv[:, :],
                             wait=[("dve", vv_done)])
                    self.on("sp", None, wait=[("A2", self.cnt["A2"])])
                    self.barrier()
                    pstk.close()
                    stk.close()
                    return
                lw1 = dma("pool", "Lr", wrelstg[:, 0:128],
                          AP(dr["bwrel"].ap().tensor, hc, [[D, 128], [1, 128]]),
                          wait=[("pe", self.cnt["pe"])])
                lw2 = dma("pool", "Lr", wrelstg[0:64, 128:256],
                          AP(dr["bwrel"].ap().tensor, 128 * D + hc, [[D, 64], [1, 128]]))
                relk_ac = 0
                for c in range(8):
                    cn = min(512, NPOS - c * 512)
                    pb = c % 2
                    l1 = dma("pool", "Lp%d" % pb, pstg0[:, pb * 512: pb * 512 + cn],
                             AP(dr["posT"].ap().tensor, c * 512, [[NPOS, 128], [1, cn]]),
                             wait=[("act", relk_ac)])
                    l2 = dma("pool", "Lp%d" % pb, pstg1[0:64, pb * 512: pb * 512 + cn],
                             AP(dr["posT"].ap().tensor, 128 * NPOS + c * 512,
                                [[NPOS, 64], [1, cn]]))
                    on("pe", lambda e, o=psc[:, 0:cn], l=wrelstg[:, 0:128],
                       r=pstg0[:, pb * 512: pb * 512 + cn]: e.matmul(
                           o, l, r, start=True, stop=False),
                       wait=[("Lp%d" % pb, l2), ("Lr", lw2), ("act", relk_ac)], inc="pe")
                    on("pe", lambda e, o=psc[:, 0:cn], l=wrelstg[0:64, 128:256],
                       r=pstg1[0:64, pb * 512: pb * 512 + cn]: e.matmul(
                           o, l, r, start=False, stop=True), inc="pe")
                    pe0 = self.cnt["pe"]
                    on("act", lambda e, o=relkT[:, c * 512: c * 512 + cn],
                       i=psc[:, 0:cn]: e.activation(out=o, in_=i, func=AF.Copy),
                       wait=[("pe", pe0)], inc="act")
                    relk_ac = self.cnt["act"]
                if self.dev.get("dbg7"):
                    self.dma("pool", "A2", dr["t_relkT"].ap(), relkT[:, :],
                             wait=[("act", relk_ac)])
                    self.on("sp", None, wait=[("A2", self.cnt["A2"])])
                    self.barrier()
                    pstk.close()
                    stk.close()
                    return
                wev = relk_ac
                win_dve = self.cnt["dve"]
                for qt in range(NT1):
                    u0 = 1920 - qt * 128
                    for uc in range(5):
                        cn = min(512, WIN - uc * 512)
                        on("pe", lambda e, o=pwin[:, 0:cn],
                           l=qrpb[:, qt * 128:(qt + 1) * 128],
                           r=relkT[:, u0 + uc * 512: u0 + uc * 512 + cn]:
                           e.matmul(o, l, r, start=True, stop=True),
                           wait=[("act", wev), ("dve", win_dve)], inc="pe")
                        pe0 = self.cnt["pe"]
                        on("act", lambda e, o=winsb[:, uc * 512: uc * 512 + cn],
                           i=pwin[:, 0:cn]: e.activation(out=o, in_=i, func=AF.Copy),
                           wait=[("pe", pe0), ("Wh", 1)], inc="act")
                        wev = self.cnt["act"]
                    self.dma("sp", "Wh",
                             AP(dr["win_hbm"].ap().tensor, qt * 128 * WIN,
                                [[WIN + 1, 128], [1, WIN]]),
                             winsb[:, :], wait=[("act", wev), ("Bt0", 1), ("Bt1", 1)])
                wh_all = self.cnt["Wh"]
                if self.dev.get("dbg5"):
                    self.dma("pool", "A2", dr["t_winsb"].ap(), winsb[:, :],
                             wait=[("Wh", wh_all)])
                    self.dma("pool", "A2", dr["t_relkT"].ap(), relkT[:, :],
                             wait=[("Wh", wh_all)])
                    if self.dev.get("dbg5") == 2:
                        self.on("sp", lambda e, o=bandT[:, 0:128],
                                i=AP(dr["win_hbm"].ap().tensor, 127, [[WIN - 1, 128], [1, 128]]):
                                e.dma_start_transpose(out=o, in_=i),
                                wait=[("Wh", wh_all)], inc="Bt", incn=16)
                        self.dma("pool", "A2", dr["t_bandT"].ap(), bandT[:, :],
                                 wait=[("Bt", self.cnt["Bt"])])
                    self.on("sp", None, wait=[("A2", self.cnt["A2"])])
                    self.barrier()
                    pstk.close()
                    stk.close()
                    return
                exp_done = self.cnt["act"]
                ptf = {0: 0, 1: 0}

                def emit_band(qc, kt, bb, wv):
                    sname = "Bt%d" % bb
                    ap_in = AP(dr["win_hbm"].ap().tensor,
                               127 + qc * 512 * WIN + kt * 128,
                               [[WIN, 512], [1, 128]])
                    self.on("sp", lambda e, o=bandT[:, bb * 512:(bb + 1) * 512],
                            i=ap_in: e.dma_start_transpose(out=o, in_=i),
                            wait=wv, inc=sname, incn=16)
                    return (sname, self.cnt[sname])
                for qc in range(4):
                    btd = {}
                    for kt in range(2):
                        btd[kt] = emit_band(qc, kt, kt % 2,
                                            [("Wh", wh_all), ("dve", self.cnt["dve"])])
                    exp_hist = [exp_done, exp_done]
                    mult_hist = [0, 0]
                    pend = None
                    for kt in range(NT1):
                        bb = kt % 2
                        pb = kt % 2
                        pscb = psc[:, bb * 512:(bb + 1) * 512]
                        scrb = sb["scr"][:, :] if bb == 0 else sb["scr2"][:, :]
                        on("pe", lambda e, o=pscb,
                           l=kT[:, kt * 128:(kt + 1) * 128],
                           r=qreb[:, qc * 512:(qc + 1) * 512]:
                           e.matmul(o, l, r, start=True, stop=True),
                           wait=[("dve", mult_hist[bb])], inc="pe")
                        pe0 = self.cnt["pe"]
                        if pend is not None:
                            pend()
                        on("dve", lambda e, o=sb["band32"][:, 0:512],
                           m=bandT[:, bb * 512:(bb + 1) * 512]:
                           e.tensor_copy(o, m),
                           wait=[btd[kt], ("dve", self.cnt["dve"])], inc="dve")
                        on("dve", lambda e, o=scrb,
                           m=sb["band32"][:, 0:512],
                           i=pscb: e.tensor_tensor(out=o, in0=m, in1=i, op=ALU.add),
                           wait=[("pe", pe0), ("act", exp_hist[bb])], inc="dve")
                        dv0 = mult_hist[bb] = self.cnt["dve"]
                        on("act", lambda e, o=pt[:, pb * 512:(pb + 1) * 512],
                           i=scrb, b_=sb["nb30"][:, 0:1]: e.activation(
                               out=o, in_=i, func=AF.Exp, bias=b_),
                           wait=[("dve", dv0), ("pe", ptf[pb])], inc="act")
                        exp_done = exp_hist[bb] = self.cnt["act"]

                        def mk_pend(kt=kt, pb=pb, exp_done=exp_done, qc=qc):
                            def pend_():
                                on("pe", lambda e, o=pwin[0:1, 0:512],
                                   l=sb["onesc"][:, 0:1],
                                   r=pt[:, pb * 512:(pb + 1) * 512],
                                   s=(kt == 0), t=(kt == NT1 - 1):
                                   e.matmul(o, l, r, start=s, stop=t),
                                   wait=[("act", exp_done)], inc="pe")
                                on("pe", lambda e, o=ppv[:, qc * 512:(qc + 1) * 512],
                                   l=vv[:, kt * 128:(kt + 1) * 128],
                                   r=pt[:, pb * 512:(pb + 1) * 512],
                                   s=(kt == 0), t=(kt == NT1 - 1):
                                   e.matmul(o, l, r, start=s, stop=t),
                                   wait=[("dve", vv_done)] if kt == 0 else (), inc="pe")
                                ptf[pb] = self.cnt["pe"]
                            return pend_
                        pend = mk_pend()
                        nkt = kt + 2
                        if nkt < NT1:
                            btd[nkt] = emit_band(qc, nkt, nkt % 2,
                                                 [("Wh", wh_all), ("dve", self.cnt["dve"])])
                    pend()
                    pend = None
                    pv_done = self.cnt["pe"]
                    on("dve", lambda e, o=rden[0:1, 0:512], i=pwin[0:1, 0:512]:
                       e.reciprocal(out=o, in_=i), wait=[("pe", pv_done)], inc="dve")
                    dv0 = self.cnt["dve"]
                    on("pe", lambda e, o=pwin[:, 0:512],
                       l=sb["onesb"][0:1, 0:128], r=rden[0:1, 0:512]:
                       e.matmul(o, l, r, start=True, stop=True),
                       wait=[("dve", dv0), ("act", self.cnt["act"])], inc="pe")
                    pe0 = self.cnt["pe"]
                    on("dve", lambda e, o=sb["repsb"][:, 0:512], i=pwin[:, 0:512]:
                       e.tensor_copy(o, i), wait=[("pe", pe0)], inc="dve")
                    dv1 = self.cnt["dve"]
                    on("dve", lambda e,
                       o=outT[:, h * S + qc * 512: h * S + (qc + 1) * 512],
                       i=ppv[:, qc * 512:(qc + 1) * 512],
                       r=sb["repsb"][:, 0:512]: e.tensor_tensor(
                           out=o, in0=i, in1=r, op=ALU.mult),
                       wait=[("dve", dv1)], inc="dve")
                    on("pe", None, wait=[("dve", self.cnt["dve"])])
                on("act", None, wait=[("dve", self.cnt["dve"])])
            heads_dve = self.cnt["dve"]
            if wout_ld is None:
                for fic in range(8):
                    dma("pool", "Lo", woutb[:, fic * D:(fic + 1) * D],
                        dr["bwout"].ap()[fic * 128:(fic + 1) * 128, :])
                wout_ld = self.cnt["Lo"]
            for st2 in range(NT1):
                for foc in range(2):
                    for fic in range(8):
                        on("pe", lambda e, o=psc[:, 0:512],
                           l=outT[:, fic * S + st2 * 128: fic * S + st2 * 128 + 128],
                           r=woutb[:, fic * D + foc * 512: fic * D + foc * 512 + 512],
                           s=(fic == 0): e.matmul(o, l, r, start=s, stop=False),
                           wait=[("Lo", wout_ld), ("dve", heads_dve)] if fic == 0 else (),
                           inc="pe")
                    on("pe", lambda e, o=psc[:, 0:512], l=sb["onesb"][0:1, 0:128],
                       r=sb["bbout_bf"][0:1, foc * 512:(foc + 1) * 512]:
                       e.matmul(o, l, r, start=False, stop=True, tile_position=(0, 0)),
                       inc="pe")
                    pe0 = self.cnt["pe"]
                    on("dve", lambda e,
                       o=x[:, st2 * D + foc * 512: st2 * D + foc * 512 + 512],
                       i=psc[:, 0:512]: e.tensor_tensor(out=o, in0=o, in1=i, op=ALU.add),
                       wait=[("pe", pe0)], inc="dve")
                    on("pe", None, wait=[("dve", self.cnt["dve"])])
            res_done = self.cnt["dve"]
            if export == "a2a":
                for th in range(2):
                    eap = AP(dr["a2a_in"].ap().tensor, b * SL * D + th * 128 * D,
                             [[D, 128], [BC * SL * D, NCORES], [1, D]])
                    sap = AP(x[:, :].tensor, th * D,
                             [[NT1 * D, 128], [2 * D, NCORES], [1, D]])
                    ex = self.dma("pool", "A2", eap, sap, wait=[("dve", res_done)])
            else:
                eap = AP(dr[export].ap().tensor, b * S * D, [[D, 128], [128 * D, NT1], [1, D]])
                sap = AP(x[:, :].tensor, 0, [[NT1 * D, 128], [D, NT1], [1, D]])
                ex = self.dma("pool", "A2", eap, sap, wait=[("dve", res_done)])
            prev_b = (res_done, ex)
        self.on("sp", None, wait=[("A2", self.cnt["A2"])])
        self.barrier()
        pstk.close()
        stk.close()

    def emit(self):
        dev = self.dev
        self.setup()
        phases = dev.get("phases",
                         ("attn_a", "a2a", "ffn1", "a2a", "attn_b", "a2a", "ffn2"))
        for ph in phases:
            if ph == "attn_a":
                self.attn_a(src=dev.get("attn_a_src", "x0"),
                            export=dev.get("attn_a_export", "a2a"))
            elif ph == "a2a":
                self.a2a()
            elif ph == "ffn1":
                self.ffn(dst=dev.get("ffn1_dst", "a2a_in"), src=dev.get("ffn1_src", "a2a_out"))
            elif ph == "ffn2":
                self.ffn(dst="y")
            elif ph == "attn_b":
                self.attn_b(src=dev.get("attn_b_src", "a2a_out"),
                            export=dev.get("attn_b_export", "a2a"))
        self.on("sp", None, wait=[(s0, 1) for s0 in
                                  ("Lg", "Lw", "LwB", "Lr", "Lp0", "Lp1", "Lo", "Lx0",
                                   "Lx1", "Ls", "St0", "St1", "Wh", "Bt0", "Bt1",
                                   "A2")])

    def build(self):
        self.declare()
        with self.st:
            self.alloc_global()
            self.PSm("tr", [128, 512], BF16)
            semstk = ExitStack()
            with semstk:
                for name in SEMS:
                    self.handles[name] = semstk.enter_context(self.nc.semaphore(name))
                self.emit()
                nc = self.nc
                with nc.allow_low_precision(reason="bf16 operand staging is intentional"), \
                     nc.Block() as blk:
                    @blk.tensor
                    def _(e):
                        for t in self.q["pe"]:
                            t(e)

                    @blk.vector
                    def _(e):
                        for t in self.q["dve"]:
                            t(e)

                    @blk.scalar
                    def _(e):
                        for t in self.q["act"]:
                            t(e)

                    @blk.gpsimd
                    def _(e):
                        for t in self.q["pool"]:
                            t(e)

                    @blk.sync
                    def _(e):
                        for t in self.q["sp"]:
                            t(e)
        return self.nc


_CONSTS = None


def _consts():
    global _CONSTS
    if _CONSTS is None:
        eye = np.eye(128, dtype=ml_dtypes.bfloat16)
        posT = _positions_T()
        absd = np.abs(np.arange(128, dtype=np.float32)[:, None]
                      - np.arange(128, dtype=np.float32)[None, :])
        _CONSTS = dict(eye=eye, posT=posT, absd=absd)
    return _CONSTS


def _in_maps(inputs, extra=None):
    c = _consts()
    x = np.ascontiguousarray(np.asarray(inputs["L0_attn_a_x"], dtype=np.float32))
    base = dict(
        aWq=inputs["L0_attn_a_Wq"], aWk=inputs["L0_attn_a_Wk"],
        aWv=inputs["L0_attn_a_Wv"], aWo=inputs["L0_attn_a_Wo"],
        abq=inputs["L0_attn_a_bq"], abk=inputs["L0_attn_a_bk"],
        abv=inputs["L0_attn_a_bv"], abo=inputs["L0_attn_a_bo"],
        adecay=inputs["L0_attn_a_decay_params"], amix=inputs["L0_attn_a_mix_weights"],
        bwq=inputs["L0_attn_b_w_qs"], bwk=inputs["L0_attn_b_w_ks"],
        bwv=inputs["L0_attn_b_w_vs"], bwrel=inputs["L0_attn_b_w_rel_pos_k"],
        breb=np.asarray(inputs["L0_attn_b_rel_embed_bias"]).reshape(H2, DH2),
        brpb=np.asarray(inputs["L0_attn_b_rel_pos_bias"]).reshape(H2, DH2),
        bwout=inputs["L0_attn_b_w_out"], bbout=inputs["L0_attn_b_b_out"],
        fwf=inputs["L0_ffn_wf"], fuf=inputs["L0_ffn_uf"], fbf=inputs["L0_ffn_bf"],
        fwi=inputs["L0_ffn_wi"], fui=inputs["L0_ffn_ui"], fbi=inputs["L0_ffn_bi"],
        fwc=inputs["L0_ffn_wc"], fuc=inputs["L0_ffn_uc"], fbc=inputs["L0_ffn_bc"],
        fwo=inputs["L0_ffn_wo"], fuo=inputs["L0_ffn_uo"], fbo=inputs["L0_ffn_bo"],
        finc=inputs["L0_ffn_in_c"],
        posT=c["posT"], absd=c["absd"],
    )
    base = {k: np.ascontiguousarray(np.asarray(v, dtype=np.float32))
            for k, v in base.items()}
    base["eye"] = c["eye"]
    maps = []
    for core in range(NCORES):
        m = dict(base)
        m["x0"] = np.ascontiguousarray(x[2 * core:2 * core + 2].reshape(BC * S, D))
        if extra:
            m.update(extra(core))
        maps.append(m)
    return maps


def kernel(**inputs):
    nc = MK().build()
    maps = _in_maps(inputs)
    res = run_bass_kernel_spmd(nc, maps, core_ids=list(range(NCORES)))
    out = np.empty((B, S, D), dtype=np.float32)
    for core in range(NCORES):
        yv = res.results[core]["y"].reshape(TS, SL, D)
        out[:, core * SL:(core + 1) * SL, :] = yv
    return out


# version 09
# speedup vs baseline: 1.0351x; 1.0266x over previous
import math
import numpy as np
from contextlib import ExitStack

import concourse.bass as bass
import concourse.mybir as mybir
from concourse.ap import AP
from concourse.bass_utils import run_bass_kernel_spmd
from concourse.aot_env import allow_ici_all_to_all
import ml_dtypes

F32 = mybir.dt.float32
BF16 = mybir.dt.bfloat16
AF = mybir.ActivationFunctionType
ALU = mybir.AluOpType
AX = mybir.AxisListType

NCORES = 8
B, S, D = 16, 2048, 1024
BC = 2
H1, DH1 = 16, 64
H2, DH2 = 8, 128
RP = 192
NPOS = 2 * S - 1
WIN = S + 127
SL = S // NCORES
TS = B
NT1 = S // 128
NT2 = (TS * SL) // 128
EPS = 1e-5
SC2 = DH2 ** -0.5

ENGS = ("pe", "dve", "act", "pool", "sp")
SEMS = ("Lg", "Lw", "LwB", "Lr", "Lp0", "Lp1", "Lo", "Lx0", "Lx1", "Ls", "St0", "St1",
        "Wh", "Bt0", "Bt1", "A2", "pe", "dve", "act", "pool", "cc", "bar")


def _positions_T():
    d = np.arange(-S + 1, S, dtype=np.float64)
    n = RP // 6
    absd = np.abs(d)[:, None]
    max_range = math.log(S) / math.log(2.0)
    half_life = 2.0 ** np.linspace(3.0, max_range, n)
    feat_exp = np.exp(-math.log(2.0) / half_life[None, :] * absd)
    center_widths = 2.0 ** np.arange(1, n + 1, dtype=np.float64) - 1.0
    feat_cm = (center_widths[None, :] > absd).astype(np.float64)
    stddev = S / (2.0 * n)
    start_mean = S / n
    mean = np.linspace(start_mean, float(S), n)[None, :]
    conc = (mean / stddev) ** 2
    rate = mean / stddev ** 2
    with np.errstate(divide="ignore", invalid="ignore"):
        logd = np.where(absd > 0, np.log(np.maximum(absd, 1e-300)), 0.0)
        log_unnorm = (conc - 1.0) * logd - rate * absd
    log_unnorm = np.where(absd > 0, log_unnorm, -np.inf)
    lgam = np.array([math.lgamma(c) for c in conc[0]])[None, :]
    log_norm = lgam - conc * np.log(rate)
    probs = np.exp(log_unnorm - log_norm) + 1e-08
    feat_gamma = probs / np.max(probs, axis=-1, keepdims=True)
    emb = np.concatenate([feat_exp, feat_cm, feat_gamma], axis=-1)
    emb = np.concatenate([emb, np.sign(d)[:, None] * emb], axis=-1)
    return np.ascontiguousarray(emb.T.astype(np.float32))


class MK:
    def __init__(self, dev=None):
        self.dev = dev or {}
        self.nc = bass.Bass("TRN2", target_bir_lowering=False)
        self.q = {e: [] for e in ENGS}
        self.cnt = {s: 0 for s in SEMS}
        self.handles = {}
        self.dr = {}
        self.sb = {}
        self.ps = {}
        self.st = ExitStack()
        self.tr_guard = 0

    def on(self, eng, fn=None, wait=(), inc=None, incn=1):
        wait = list(wait)
        if fn is not None and eng in ("act", "dve"):
            assert inc in (None, eng)
            wait.append((eng, self.cnt[eng]))
            inc = eng
            incn = 1
        if inc is not None:
            self.cnt[inc] += incn
        tgt = self.cnt[inc] if inc is not None else None
        DMAS = ("Lg", "Lw", "LwB", "Lr", "Lp0", "Lp1", "Lo", "Lx0", "Lx1", "Ls", "St0",
                "St1", "Wh", "Bt0", "Bt1", "A2")
        wait = tuple((s, self.cnt[s] if s in DMAS else v) for (s, v) in wait if v)
        h = self.handles

        def thunk(e, _fn=fn, _wait=wait, _inc=inc, _incn=incn):
            for s, v in _wait:
                e.wait_ge(h[s], v)
            inst = _fn(e) if _fn is not None else None
            if _inc is not None:
                if inst is None:
                    e.sem_inc(h[_inc], _incn)
                else:
                    inst.then_inc(h[_inc], _incn)
        self.q[eng].append(thunk)
        return tgt

    def dma(self, eng, sem, out, in_, wait=(), slow=False):
        return self.on(eng, lambda e, o=out, i=in_, s=slow: e.dma_start(
            out=o, in_=i, allow_slow_non_contiguous=s), wait=wait, inc=sem, incn=16)

    def barrier(self):
        for e in ENGS:
            self.on(e, None, inc="bar")
        tgt = self.cnt["bar"]
        for e in ENGS:
            self.on(e, None, wait=[("bar", tgt)])
        return tgt

    def din(self, name, shape, dtype=F32):
        self.dr[name] = self.nc.dram_tensor(name, shape, dtype, kind="ExternalInput")
        return self.dr[name]

    def dout(self, name, shape, dtype=F32):
        self.dr[name] = self.nc.dram_tensor(name, shape, dtype, kind="ExternalOutput")
        return self.dr[name]

    def dint(self, name, shape, dtype=F32):
        self.dr[name] = self.nc.dram_tensor(name, shape, dtype)
        return self.dr[name]

    def SB(self, name, shape, dtype, stack=None):
        t = (stack or self.st).enter_context(self.nc.sbuf_tensor(name, shape, dtype))
        self.sb[name] = t
        return t

    def PSm(self, name, shape, dtype=F32, stack=None):
        t = (stack or self.st).enter_context(self.nc.psum_tensor(name, shape, dtype))
        self.ps[name] = t
        return t

    def trblk(self, src, dst, wait=()):
        tr = self.ps["tr"]
        eye = self.sb["s_eye"]
        self.on("pe", lambda e, s=src, t=tr[:, 0:128], y=eye[:, :]: e.transpose(t, s, y),
                wait=list(wait) + [("dve", self.tr_guard)], inc="pe")
        pe0 = self.cnt["pe"]
        self.on("dve", lambda e, d=dst, t=tr[:, 0:128]: e.tensor_copy(d, t),
                wait=[("pe", pe0)], inc="dve")
        self.tr_guard = self.cnt["dve"]
        return self.tr_guard

    def declare(self):
        din, dout, dint = self.din, self.dout, self.dint
        din("x0", [BC * S, D])
        for nm in ("aWq", "aWk", "aWv", "aWo", "bwq", "bwk", "bwv", "bwout",
                   "fwf", "fuf", "fwi", "fui", "fwc", "fuc", "fwo", "fuo"):
            din(nm, [D, D])
        for nm in ("abq", "abk", "abv", "abo", "bbout", "fbf", "fbi", "fbc",
                   "fbo", "finc"):
            din(nm, [D])
        din("adecay", [H1, 2]); din("amix", [H1, 2])
        din("bwrel", [RP, D]); din("breb", [H2, DH2]); din("brpb", [H2, DH2])
        din("eye", [128, 128], BF16)
        din("posT", [RP, NPOS])
        din("absd", [128, 128])
        dout("y", [TS * SL, D])
        dint("a2a_in", [NCORES, BC, SL, D])
        dint("a2a_out", [NCORES, BC, SL, D])
        dint("win_hbm", [127 + S * WIN + WIN], BF16)
        for tname, tshape in self.dev.get("taps", {}).items():
            dout(tname, tshape)
        if self.dev.get("devffn"):
            din("devffn", [NCORES, BC, SL, D])

    def alloc_global(self):
        SB = self.SB
        SB("s_eye", [128, 128], BF16)
        SB("lnt", [128, D], BF16)
        SB("stats", [128, 16], F32)
        SB("onesb", [128, 512], BF16)
        SB("onesc", [128, 1], BF16)
        SB("repsb", [128, 512], F32)
        SB("scr", [128, 512], F32)
        SB("scr2", [128, 512], F32)

        SB("amixs2", [16, 2], F32)
        for nm in ("abq_t", "abk_t", "abv_t", "fbf_t", "fbi_t", "fbc_t", "finc_t"):
            SB(nm, [128, 8], F32)
        SB("breb_t", [128, H2], F32)
        SB("brpb_t", [128, H2], F32)
        SB("abo_bf", [1, D], BF16)
        SB("bbout_bf", [1, D], BF16)
        SB("fbo_bf", [1, D], BF16)
        SB("adp", [H1, 2], F32)
        SB("amixs", [H1, 2], F32)
        SB("alogT", [2, H1], F32)
        SB("alogTn", [2, H1], F32)
        SB("alogT2047", [2, H1], F32)
        SB("alogT127", [2, H1], F32)
        SB("amixT", [2, H1], F32)
        SB("alogF", [1, 2 * H1], F32)
        SB("amixF", [1, 2 * H1], F32)
        SB("alogFb", [1, 2 * H1], BF16)
        SB("amixFb", [1, 2 * H1], BF16)
        SB("s_absd", [128, 128], F32)
        SB("epsc", [128, 1], F32)
        SB("nb30", [128, 1], F32)

    def setup(self):
        sb, dr, dma, on = self.sb, self.dr, self.dma, self.on

        def col_ap(v):
            return AP(v.ap().tensor, 0, [[1, 128], [128, 8]])
        dma("pool", "Lg", sb["s_eye"][:, :], dr["eye"].ap())
        dma("pool", "Lg", sb["s_absd"][:, :], dr["absd"].ap())
        for nm, src in (("abq_t", "abq"), ("abk_t", "abk"), ("abv_t", "abv"),
                        ("fbf_t", "fbf"), ("fbi_t", "fbi"), ("fbc_t", "fbc"),
                        ("finc_t", "finc")):
            dma("pool", "Lg", sb[nm][:, :], col_ap(dr[src]), slow=True)
        for nm, src in (("abo_bf", "abo"), ("bbout_bf", "bbout"), ("fbo_bf", "fbo")):
            dma("pool", "Lg", sb[nm][0:1, :], AP(dr[src].ap().tensor, 0, [[D, 1], [1, D]]))
        dma("pool", "Lg", sb["breb_t"][:, :],
            AP(dr["breb"].ap().tensor, 0, [[1, DH2], [DH2, H2]]), slow=True)
        dma("pool", "Lg", sb["brpb_t"][:, :],
            AP(dr["brpb"].ap().tensor, 0, [[1, DH2], [DH2, H2]]), slow=True)
        dma("pool", "Lg", sb["adp"][:, :], dr["adecay"].ap())
        dma("pool", "Lg", sb["amixs"][:, :], dr["amix"].ap())
        lg = self.cnt["Lg"]
        on("dve", lambda e, o=sb["onesb"][:, :]: e.memset(o, 1.0), inc="dve")
        on("dve", lambda e, o=sb["onesc"][:, :]: e.memset(o, 1.0), inc="dve")
        on("dve", lambda e, o=sb["epsc"][:, :]: e.memset(o, EPS), inc="dve")
        on("dve", lambda e, o=sb["nb30"][:, :]: e.memset(o, -60.0), inc="dve")
        on("dve", lambda e, o=sb["adp"][:, :]: e.tensor_scalar(
            out=o, in0=o, scalar1=0.01, scalar2=0.9999, op0=ALU.max, op1=ALU.min),
            wait=[("Lg", lg)], inc="dve")
        dv = self.cnt["dve"]
        on("act", lambda e, o=sb["stats"][0:H1, 8:10], i=sb["adp"][:, :]: e.activation(
            out=o, in_=i, func=AF.Ln), wait=[("dve", dv)], inc="act")
        ln_ac = self.cnt["act"]
        on("dve", lambda e, o=sb["stats"][0:H1, 0:1], i=sb["amixs"][:, 0:1],
           j=sb["amixs"][:, 1:2]: e.tensor_tensor(out=o, in0=i, in1=j, op=ALU.subtract),
           wait=[("Lg", lg)], inc="dve")
        dvd = self.cnt["dve"]
        on("act", lambda e, o=sb["amixs2"][:, 0:1], i=sb["stats"][0:H1, 0:1]:
           e.activation(out=o, in_=i, func=AF.Sigmoid), wait=[("dve", dvd)], inc="act")
        on("act", lambda e, o=sb["amixs2"][:, 1:2], i=sb["stats"][0:H1, 0:1]:
           e.activation(out=o, in_=i, func=AF.Sigmoid, scale=-1.0), inc="act")
        ac2 = self.cnt["act"]
        on("dve", None, wait=[("act", ac2)], inc="dve")
        dv = self.cnt["dve"]
        for h in range(H1):
            for n in range(2):
                dma("pool", "Lg", sb["alogT"][n:n + 1, h:h + 1],
                    sb["stats"][h:h + 1, 8 + n:9 + n], wait=[("act", ln_ac), ("dve", dv)],
                    slow=True)
                dma("pool", "Lg", sb["amixT"][n:n + 1, h:h + 1],
                    sb["amixs2"][h:h + 1, n:n + 1], wait=[("dve", dv)], slow=True)
                dma("pool", "Lg", sb["alogF"][0:1, 2 * h + n:2 * h + n + 1],
                    sb["stats"][h:h + 1, 8 + n:9 + n], wait=[("act", ln_ac)], slow=True)
                dma("pool", "Lg", sb["amixF"][0:1, 2 * h + n:2 * h + n + 1],
                    sb["amixs2"][h:h + 1, n:n + 1], wait=[("dve", dv)], slow=True)
        lg = self.cnt["Lg"]
        on("dve", lambda e, o=sb["alogTn"][:, :], i=sb["alogT"][:, :]: e.tensor_scalar(
            out=o, in0=i, scalar1=-1.0, scalar2=None, op0=ALU.mult),
            wait=[("Lg", lg)], inc="dve")
        on("dve", lambda e, o=sb["alogT2047"][:, :], i=sb["alogT"][:, :]: e.tensor_scalar(
            out=o, in0=i, scalar1=2047.0, scalar2=None, op0=ALU.mult), inc="dve")
        on("dve", lambda e, o=sb["alogT127"][:, :], i=sb["alogT"][:, :]: e.tensor_scalar(
            out=o, in0=i, scalar1=127.0, scalar2=None, op0=ALU.mult), inc="dve")
        on("dve", lambda e, o=sb["alogFb"][:, :], i=sb["alogF"][:, :]: e.tensor_copy(o, i),
           wait=[("Lg", self.cnt["Lg"])], inc="dve")
        on("dve", lambda e, o=sb["amixFb"][:, :], i=sb["amixF"][:, :]: e.tensor_copy(o, i),
           inc="dve")
        self.on("pool", None, wait=[("Lg", self.cnt["Lg"])])
        if self.dev.get("dbg0"):
            dv = self.cnt["dve"]
            for nm, apx in (("t_amixs", sb["amixs"][:, :]), ("t_stats", sb["stats"][:, 0:16]),
                            ("t_amixs2", sb["amixs2"][:, :]),
                            ("t_alogF", sb["alogF"][:, :]), ("t_amixF", sb["amixF"][:, :]),
                            ("t_alogT", sb["alogT"][:, :]), ("t_amixT", sb["amixT"][:, :]),
                            ("t_adp", sb["adp"][:, :])):
                if nm in self.dr:
                    self.dma("pool", "A2", self.dr[nm].ap(), apx,
                             wait=[("dve", dv), ("Lg", self.cnt["Lg"])])
            self.on("sp", None, wait=[("A2", self.cnt["A2"])])
        self.barrier()

    def ln_tile(self, xt, lnxT_col_fn, wait, xt_halves=None):
        sb, on = self.sb, self.on
        c1 = 1.0 / D
        stats = sb["stats"]
        wait = list(wait) + [("dve", getattr(self, "ln_guard", 0))]
        on("dve", lambda e, o=stats[:, 0:1], i=xt: e.tensor_reduce(
            out=o, in_=i, axis=AX.X, op=ALU.add), wait=wait, inc="dve")
        sqb = sb[self.ln_sq]
        if xt_halves is None:
            on("act", lambda e, o=sqb[:, 0:D], i=xt: e.activation(
                out=o, in_=i, func=AF.Square), wait=wait, inc="act")
            ac0 = self.cnt["act"]
            on("dve", lambda e, o=stats[:, 1:2], i=sqb[:, 0:D]: e.tensor_reduce(
                out=o, in_=i, axis=AX.X, op=ALU.add), wait=[("act", ac0)], inc="dve")
        else:
            h0, h1 = xt_halves
            on("act", lambda e, o=sqb[:, 0:512], i=h0: e.activation(
                out=o, in_=i, func=AF.Square), wait=wait, inc="act")
            ac0a = self.cnt["act"]
            on("dve", lambda e, o=stats[:, 1:2], i=sqb[:, 0:512]: e.tensor_reduce(
                out=o, in_=i, axis=AX.X, op=ALU.add), wait=[("act", ac0a)], inc="dve")
            dv0a = self.cnt["dve"]
            on("act", lambda e, o=sqb[:, 0:512], i=h1: e.activation(
                out=o, in_=i, func=AF.Square), wait=[("dve", dv0a)], inc="act")
            ac0 = self.cnt["act"]
            on("dve", lambda e, o=stats[:, 8:9], i=sqb[:, 0:512]: e.tensor_reduce(
                out=o, in_=i, axis=AX.X, op=ALU.add), wait=[("act", ac0)], inc="dve")
            on("dve", lambda e, o=stats[:, 1:2], i=stats[:, 1:2], j=stats[:, 8:9]:
               e.tensor_tensor(out=o, in0=i, in1=j, op=ALU.add), inc="dve")
        dv0 = self.cnt["dve"]
        on("dve", lambda e, o=stats[:, 2:3], i=stats[:, 0:1]: e.tensor_scalar(
            out=o, in0=i, scalar1=c1, scalar2=None, op0=ALU.mult),
            wait=[("dve", dv0), ("act", ac0)], inc="dve")
        on("dve", lambda e, o=stats[:, 3:4], i=stats[:, 2:3]: e.tensor_tensor(
            out=o, in0=i, in1=i, op=ALU.mult), inc="dve")
        on("dve", lambda e, o=stats[:, 4:5], i=stats[:, 1:2], j=stats[:, 3:4]:
            e.scalar_tensor_tensor(out=o, in0=i, scalar=c1, in1=j,
                                   op0=ALU.mult, op1=ALU.subtract),
            wait=[("act", getattr(self, "ln_act_guard", 0))], inc="dve")
        dv1 = self.cnt["dve"]
        self.ln_guard = dv1
        on("act", lambda e, o=stats[:, 5:6], i=stats[:, 4:5], b=sb["epsc"][:, 0:1]:
            e.activation(out=o, in_=i, func=AF.Sqrt, bias=b), wait=[("dve", dv1)], inc="act")
        self.ln_act_guard = self.cnt["act"]
        ac1 = self.cnt["act"]
        on("dve", lambda e, o=stats[:, 6:7], i=stats[:, 5:6]: e.reciprocal(out=o, in_=i),
           wait=[("act", ac1)], inc="dve")
        on("dve", lambda e, o=stats[:, 7:8], i=stats[:, 2:3], j=stats[:, 6:7]:
            e.tensor_tensor(out=o, in0=i, in1=j, op=ALU.mult), inc="dve")
        on("dve", lambda e, o=stats[:, 7:8], i=stats[:, 7:8]: e.tensor_scalar(
            out=o, in0=i, scalar1=-1.0, scalar2=None, op0=ALU.mult), inc="dve")
        dvr = self.cnt["dve"]
        lnt = sb["lnt"]
        on("act", lambda e, o=lnt[:, :], i=xt, r=stats[:, 6:7], b=stats[:, 7:8]:
            e.activation(out=o, in_=i, func=AF.Identity, scale=r, bias=b),
            wait=[("dve", dvr)], inc="act")
        ac2 = self.cnt["act"]
        on("dve", None, wait=[("act", ac2)], inc="dve")
        dv2 = self.cnt["dve"]
        for ft in range(8):
            self.trblk(lnt[:, ft * 128:(ft + 1) * 128], lnxT_col_fn(ft),
                       wait=[("dve", dv2)])
        return self.cnt["dve"]

    def attn_a(self, src="x0", export="a2a"):
        sb, dr, ps, on, dma = self.sb, self.dr, self.ps, self.on, self.dma
        stk = ExitStack()
        SB = lambda n, s, d: self.SB(n, s, d, stk)
        x = SB("x", [128, NT1 * D], F32)
        lnxT = SB("lnxT", [128, 8 * S], BF16)
        outT = SB("outT", [128, 8 * S], BF16)
        iota = SB("iota", [2, S], mybir.dt.uint16)
        tasc = SB("tasc", [2, S], BF16)
        tdesc = SB("tdesc", [2, S], BF16)
        mlo = SB("mlo", [2, 128], BF16)
        mup = SB("mup", [2, 128], BF16)
        diag = SB("diag", [128, H1 * 128], BF16)
        rden = SB("rden", [128, 512], BF16)
        wstg = SB("wstg", [128, D], BF16)
        wstgB = SB("wstgB", [128, D], BF16)
        wsl = [wstg, wstgB]
        wpe = [self.cnt["pe"], self.cnt["pe"]]
        wix = [0]
        wT = SB("wT", [128, D], BF16)
        bufA = SB("bufA", [128, S], BF16)
        bufK = SB("bufK", [128, S], BF16)
        bufV = SB("bufV", [128, 2112], BF16)
        pt = SB("pt", [128, 1024], BF16)
        msk = SB("msk", [128, 1024], F32)
        self.ln_sq = "msk"
        SB("l128", [128, 2], F32)
        SB("e128", [128, 128], F32)
        SB("tt2", [2, 128], F32)
        SB("m128", [128, 2], F32)
        SB("dtmp", [128, 128], F32)
        SB("dtmp2", [128, 128], F32)
        woT = SB("woT", [128, 8 * D], BF16)
        pstk = ExitStack()
        psc = self.PSm("psc", [128, 1024], F32, pstk)
        ppv = self.PSm("ppv", [128, S], F32, pstk)
        paux = self.PSm("paux", [128, 512], F32, pstk)

        on("pool", lambda e, o=iota[:, :]: e.iota(
            o, [[1, S]], channel_multiplier=0, allow_small_or_imprecise_dtypes=True),
            inc="pool")
        pl0 = self.cnt["pool"]
        on("dve", lambda e, o=bufV[:, :]: e.memset(o, 1.0))
        on("dve", lambda e, o=psc[:, 0:1024]: e.memset(o, 0.0))
        for h in range(H1):
            on("pe", lambda e, o=paux[:, 0:2], l=sb["onesb"][0:1, 0:128],
               r=sb["alogFb"][0:1, 2 * h:2 * h + 2]: e.matmul(o, l, r, start=True, stop=True),
               wait=[("Lg", self.cnt["Lg"]), ("dve", self.cnt["dve"]),
                     ("act", self.cnt["act"])], inc="pe")
            pe0 = self.cnt["pe"]
            on("dve", lambda e, o=sb["l128"][:, :], i=paux[:, 0:2]: e.tensor_copy(o, i),
               wait=[("pe", pe0)], inc="dve")
            dvl = self.cnt["dve"]
            on("pe", lambda e, o=paux[:, 0:2], l=sb["onesb"][0:1, 0:128],
               r=sb["amixFb"][0:1, 2 * h:2 * h + 2]: e.matmul(o, l, r, start=True, stop=True),
               wait=[("dve", dvl)], inc="pe")
            pe0 = self.cnt["pe"]
            on("dve", lambda e, o=sb["m128"][:, :], i=paux[:, 0:2]: e.tensor_copy(o, i),
               wait=[("pe", pe0)], inc="dve")
            dvm = self.cnt["dve"]
            for n in range(2):
                o = sb["e128"][:, :] if n == 0 else sb["dtmp2"][:, :]
                on("act", lambda e, o=o, i=sb["s_absd"][:, :], s=sb["l128"][:, n:n + 1]:
                   e.activation(out=o, in_=i, func=AF.Exp, scale=s),
                   wait=[("dve", dvl)], inc="act")
            a2 = self.cnt["act"]
            dgs = diag[:, h * 128:(h + 1) * 128]
            on("act", lambda e, o=sb["dtmp"][:, :], i=sb["dtmp2"][:, :],
               s=sb["m128"][:, 1:2]: e.activation(
                   out=o, in_=i, func=AF.Identity, scale=s),
               wait=[("dve", dvm)], inc="act")
            on("act", lambda e, o=sb["dtmp2"][:, :], i=sb["e128"][:, :],
               s=sb["m128"][:, 0:1]: e.activation(
                   out=o, in_=i, func=AF.Identity, scale=s), inc="act")
            a3 = self.cnt["act"]
            on("dve", lambda e, o=dgs, i=sb["dtmp"][:, :], j=sb["dtmp2"][:, :]:
               e.tensor_tensor(out=o, in0=i, in1=j, op=ALU.add),
               wait=[("act", a3)], inc="dve")
        diag_done = self.cnt["dve"]
        woT_done = None

        prev_b = (0, 0)
        for b in range(BC):
            impw = [("dve", prev_b[0]), ("A2", prev_b[1]), ("cc", self.cnt["cc"])]
            if src == "x0":
                src_ap = AP(dr["x0"].ap().tensor, b * S * D,
                            [[D, 128], [128 * D, NT1], [1, D]])
                dst_ap = AP(x[:, :].tensor, 0, [[NT1 * D, 128], [D, NT1], [1, D]])
                xi = dma("sp", "Ls", dst_ap, src_ap, wait=impw)
            else:
                for th in range(2):
                    src_ap = AP(dr["a2a_out"].ap().tensor, b * SL * D + th * 128 * D,
                                [[D, 128], [BC * SL * D, NCORES], [1, D]])
                    dst_ap = AP(x[:, :].tensor, th * D,
                                [[NT1 * D, 128], [2 * D, NCORES], [1, D]])
                    xi = dma("sp", "Ls", dst_ap, src_ap, wait=impw if th == 0 else ())
            for tt in range(NT1):
                self.ln_tile(x[:, tt * D:(tt + 1) * D],
                             lambda ft, tt=tt: lnxT[:, ft * S + tt * 128: ft * S + (tt + 1) * 128],
                             [("Ls", xi)])
            ln_dve = self.cnt["dve"]

            for pair in range(H1 // 2):
                r0 = pair * 128
                projs = (("aWv", "abv_t", bufA, True), ("aWq", "abq_t", bufA, False),
                         ("aWk", "abk_t", bufK, False))
                for wname, bname, tgt, is_v in projs:
                    sl = wix[0] % 2
                    wix[0] += 1
                    stg = wsl[sl]
                    wsem = ("Lw", "LwB")[sl]
                    lw = dma("pool", wsem, stg[:, :], dr[wname].ap()[r0:r0 + 128, :],
                             wait=[("pe", wpe[sl])])
                    for ec in range(8):
                        self.trblk(stg[:, ec * 128:(ec + 1) * 128],
                                   wT[:, ec * 128:(ec + 1) * 128],
                                   wait=[(wsem, lw)] if ec == 0 else ())
                    wpe[sl] = self.cnt["pe"]
                    tdone = self.cnt["dve"]
                    for qc in range(4):
                        for ec in range(8):
                            on("pe", lambda e, o=psc[:, 0:512],
                               l=wT[:, ec * 128:(ec + 1) * 128],
                               r=lnxT[:, ec * S + qc * 512: ec * S + qc * 512 + 512],
                               s=(ec == 0), t=(ec == 7): e.matmul(o, l, r, start=s, stop=t),
                               wait=[("dve", tdone), ("dve", ln_dve)] if ec == 0 else (),
                               inc="pe")
                        pe0 = self.cnt["pe"]
                        on("act", lambda e, o=tgt[:, qc * 512:(qc + 1) * 512],
                           i=psc[:, 0:512], bi=sb[bname][:, pair:pair + 1]:
                           e.activation(out=o, in_=i, func=AF.Identity, bias=bi),
                           wait=[("pe", pe0)], inc="act")
                        on("pe", None, wait=[("act", self.cnt["act"])])
                    if is_v:
                        acv = self.cnt["act"]
                        tr = ps["tr"]
                        for kt in range(NT1):
                            on("pe", lambda e, s=bufA[:, kt * 128:(kt + 1) * 128],
                               t=tr[:, 0:128], y=sb["s_eye"][:, :]: e.transpose(t, s, y),
                               wait=([("act", acv)] if kt == 0 else []) +
                                    [("dve", self.tr_guard)], inc="pe")
                            pe0 = self.cnt["pe"]
                            on("dve", lambda e, o=bufV[:, kt * 129: kt * 129 + 64],
                               t=tr[:, 0:64]: e.tensor_copy(o, t),
                               wait=[("pe", pe0)], inc="dve")
                            on("dve", lambda e, o=bufV[:, kt * 129 + 65: kt * 129 + 129],
                               t=tr[:, 64:128]: e.tensor_copy(o, t), inc="dve")
                            self.tr_guard = self.cnt["dve"]
                proj_act = self.cnt["act"]

                for hi in range(2):
                    h = 2 * pair + hi
                    ho = 64 * hi
                    pe_now = self.cnt["pe"]
                    on("act", lambda e, o=tasc[:, :], i=iota[:, :],
                       s=sb["alogT"][:, h:h + 1]: e.activation(
                           out=o, in_=i, func=AF.Exp, scale=s),
                       wait=[("pool", pl0), ("pe", pe_now)], inc="act")
                    on("act", lambda e, o=tdesc[:, :], i=iota[:, :],
                       s=sb["alogTn"][:, h:h + 1], bi=sb["alogT2047"][:, h:h + 1]:
                       e.activation(out=o, in_=i, func=AF.Exp, scale=s, bias=bi),
                       inc="act")
                    on("act", lambda e, o=sb["tt2"][:, :], i=iota[:, 0:128],
                       s=sb["alogTn"][:, h:h + 1], bi=sb["alogT127"][:, h:h + 1]:
                       e.activation(out=o, in_=i, func=AF.Exp, scale=s, bias=bi),
                       inc="act")
                    on("act", lambda e, o=mlo[:, :], i=sb["tt2"][:, :],
                       s=sb["amixT"][:, h:h + 1]: e.activation(
                           out=o, in_=i, func=AF.Identity, scale=s), inc="act")
                    on("act", lambda e, o=sb["tt2"][:, :], i=iota[:, 0:128],
                       s=sb["alogT"][:, h:h + 1]: e.activation(
                           out=o, in_=i, func=AF.Exp, scale=s), inc="act")
                    on("act", lambda e, o=mup[:, :], i=sb["tt2"][:, :],
                       s=sb["amixT"][:, h:h + 1]: e.activation(
                           out=o, in_=i, func=AF.Identity, scale=s), inc="act")
                    tab_act = self.cnt["act"]
                    on("dve", None, wait=[("act", tab_act)], inc="dve")
                    tab_done = self.cnt["dve"]

                    mult_done = 0
                    mult_hist = [0, 0]
                    exp_done = proj_act
                    exp_hist = [proj_act, proj_act]
                    mevac_done = 0
                    mevac_hist = [proj_act, proj_act]
                    ptf = {0: 0, 1: 0}
                    ft = h // 2
                    for qc in range(4):
                        q0c, q1c = qc * 512, qc * 512 + 512
                        pend = None
                        for kt in range(NT1):
                            k0 = kt * 128
                            mb = kt % 2
                            bm = psc[:, mb * 512:(mb + 1) * 512]
                            bs = psc[:, (1 - mb) * 512:(2 - mb) * 512]
                            scrb = sb["scr"][:, :] if mb == 0 else sb["scr2"][:, :]
                            msl = msk[:, mb * 512:(mb + 1) * 512]
                            mwait = [("dve", tab_done), ("act", mevac_hist[mb]),
                                     ("dve", mult_hist[mb])]
                            ju = min(k0, q1c)
                            if ju > q0c:
                                ln_ = ju - q0c
                                on("pe", lambda e, o=bm[:, 0:ln_], l=mup[:, :],
                                   r=tdesc[:, q0c + 2047 - k0: q0c + 2047 - k0 + ln_]:
                                   e.matmul(o, l, r, start=True, stop=True),
                                   wait=mwait, inc="pe")
                            jl = max(k0 + 128, q0c)
                            if q1c > jl:
                                ln_ = q1c - jl
                                off = jl - q0c
                                on("pe", lambda e, o=bm[:, off:off + ln_], l=mlo[:, :],
                                   r=tasc[:, jl - k0 - 127: jl - k0 - 127 + ln_]:
                                   e.matmul(o, l, r, start=True, stop=True),
                                   wait=mwait, inc="pe")
                            pe_m = self.cnt["pe"]
                            on("pe", lambda e, o=bs,
                               l=bufK[ho:ho + 64, kt * 128:(kt + 1) * 128],
                               r=bufA[ho:ho + 64, qc * 512:(qc + 1) * 512]:
                               e.matmul(o, l, r, start=True, stop=True),
                               wait=[("act", mevac_hist[1 - mb]),
                                     ("dve", mult_hist[1 - mb])], inc="pe")
                            pe_s = self.cnt["pe"]
                            if pend is not None:
                                pend()
                            on("act", lambda e, o=msl, i=bm: e.activation(
                                out=o, in_=i, func=AF.Copy),
                                wait=[("pe", pe_m), ("dve", mult_hist[mb])], inc="act")
                            mevac_done = mevac_hist[mb] = self.cnt["act"]
                            dg_done = 0
                            if q0c <= k0 < q1c:
                                off = k0 - q0c
                                on("dve", lambda e, o=msl[:, off:off + 128],
                                   i=diag[:, h * 128:(h + 1) * 128]: e.tensor_copy(o, i),
                                   wait=[("act", mevac_done), ("dve", diag_done)],
                                   inc="dve")
                                dg_done = self.cnt["dve"]
                            on("dve", lambda e, o=scrb, m=msl,
                               i=bs: e.tensor_tensor(
                                out=o, in0=m, in1=i, op=ALU.mult),
                                wait=[("pe", pe_s), ("act", mevac_done),
                                      ("dve", dg_done), ("act", exp_hist[mb])],
                                inc="dve")
                            mult_done = mult_hist[1 - mb] = self.cnt["dve"]
                            pb = kt % 2
                            on("act", lambda e, o=pt[:, pb * 512:(pb + 1) * 512],
                               i=scrb: e.activation(
                                   out=o, in_=i, func=AF.Exp, scale=0.125),
                                wait=[("dve", mult_done), ("pe", ptf[pb])], inc="act")
                            exp_done = exp_hist[mb] = self.cnt["act"]

                            def mk_pend(kt=kt, pb=pb, exp_done=exp_done, qc=qc,
                                        ho=ho, hi=hi):
                                def pend_():
                                    if hi == 0:
                                        on("pe", lambda e,
                                           o=ppv[0:65, qc * 512:(qc + 1) * 512],
                                           l=bufV[:, kt * 129: kt * 129 + 65],
                                           r=pt[:, pb * 512:(pb + 1) * 512],
                                           s=(kt == 0), t=(kt == NT1 - 1):
                                           e.matmul(o, l, r, start=s, stop=t),
                                           wait=[("act", exp_done)], inc="pe")
                                    else:
                                        on("pe", lambda e, o=paux[0:1, 0:512],
                                           l=sb["onesc"][:, 0:1],
                                           r=pt[:, pb * 512:(pb + 1) * 512],
                                           s=(kt == 0), t=(kt == NT1 - 1):
                                           e.matmul(o, l, r, start=s, stop=t),
                                           wait=[("act", exp_done)], inc="pe")
                                        on("pe", lambda e,
                                           o=ppv[64:128, qc * 512:(qc + 1) * 512],
                                           l=bufV[:, kt * 129 + 65: kt * 129 + 129],
                                           r=pt[:, pb * 512:(pb + 1) * 512],
                                           s=(kt == 0), t=(kt == NT1 - 1), tp=(0, 64):
                                           e.matmul(o, l, r, start=s, stop=t,
                                                    tile_position=tp), inc="pe")
                                    ptf[pb] = self.cnt["pe"]
                                return pend_
                            pend = mk_pend()
                        pend()
                        pend = None
                        pv_done = self.cnt["pe"]
                        drow = 64 if hi == 0 else 0
                        dsrc = (ppv[64:65, qc * 512:(qc + 1) * 512] if hi == 0
                                else paux[0:1, 0:512])
                        on("dve", lambda e, o=rden[drow:drow + 1, 0:512], i=dsrc:
                           e.reciprocal(out=o, in_=i), wait=[("pe", pv_done)], inc="dve")
                        dv0 = self.cnt["dve"]
                        on("pe", lambda e, o=paux[:, 0:512],
                           l=sb["onesb"][drow:drow + 1, 0:128],
                           r=rden[drow:drow + 1, 0:512]:
                           e.matmul(o, l, r, start=True, stop=True),
                           wait=[("dve", dv0), ("act", mevac_done)], inc="pe")
                        pe0 = self.cnt["pe"]
                        on("dve", lambda e, o=sb["repsb"][:, 0:512], i=paux[:, 0:512]:
                           e.tensor_copy(o, i), wait=[("pe", pe0)], inc="dve")
                        dv1 = self.cnt["dve"]
                        on("dve", lambda e,
                           o=outT[ho:ho + 64, ft * S + qc * 512: ft * S + (qc + 1) * 512],
                           i=ppv[ho:ho + 64, qc * 512:(qc + 1) * 512],
                           r=sb["repsb"][ho:ho + 64, 0:512]: e.tensor_tensor(
                               out=o, in0=i, in1=r, op=ALU.mult),
                           wait=[("dve", dv1)], inc="dve")
                        on("pe", None, wait=[("dve", self.cnt["dve"])])
                    on("act", None, wait=[("dve", self.cnt["dve"])])
                    if self.dev.get("dbg3") and pair == self.dev.get("dbg3_pair", 0) and hi == 1:
                        self.dma("pool", "A2", dr["t_outT_all"].ap(), outT[:, :],
                                 wait=[("dve", self.cnt["dve"])])
                        if "t_rden2" in dr:
                            self.dma("pool", "A2", dr["t_rden2"].ap(), rden[:, :],
                                     wait=[("dve", self.cnt["dve"])])
                        if "t_repsb" in dr:
                            self.dma("pool", "A2", dr["t_repsb"].ap(), sb["repsb"][:, :],
                                     wait=[("dve", self.cnt["dve"])])
                        self.on("sp", None, wait=[("A2", self.cnt["A2"])])
                        self.barrier()
                        pstk.close()
                        stk.close()
                        return
                    if self.dev.get("dbg1"):
                        dv = self.cnt["dve"]
                        for nm, apx in (("t_tasc", tasc[:, :]), ("t_tdesc", tdesc[:, :]),
                                        ("t_mlo", mlo[:, :]), ("t_mup", mup[:, :]),
                                        ("t_msk", msk[:, :]), ("t_pt", pt[:, :]),
                                        ("t_rden", rden[:, :]),
                                        ("t_outT", outT[:, 0:S]),
                                        ("t_lnxT", lnxT[:, 0:S]),
                                        ("t_bufA", bufA[:, :]), ("t_bufK", bufK[:, :]),
                                        ("t_bufV", bufV[:, :]),
                                        ("t_diag", diag[:, 0:256])):
                            if nm in dr:
                                self.dma("pool", "A2", dr[nm].ap(), apx,
                                         wait=[("dve", dv)])
                        self.on("sp", None, wait=[("A2", self.cnt["A2"])])
                        self.barrier()
                        pstk.close()
                        stk.close()
                        return
            heads_dve = self.cnt["dve"]
            if self.dev.get("dbg2"):
                self.dma("pool", "A2", dr["t_outT_all"].ap(), outT[:, :],
                         wait=[("dve", heads_dve)])
                self.dma("pool", "A2", dr["t_rden2"].ap(), rden[:, :],
                         wait=[("dve", heads_dve)])
                self.on("sp", None, wait=[("A2", self.cnt["A2"])])
                self.barrier()
                pstk.close()
                stk.close()
                return

            if woT_done is None:
                for fot in range(8):
                    sl = wix[0] % 2
                    wix[0] += 1
                    stg = wsl[sl]
                    wsem = ("Lw", "LwB")[sl]
                    lw = dma("pool", wsem, stg[:, :],
                             dr["aWo"].ap()[fot * 128:(fot + 1) * 128, :],
                             wait=[("pe", wpe[sl])])
                    for ec in range(8):
                        self.trblk(stg[:, ec * 128:(ec + 1) * 128],
                                   woT[:, ec * D + fot * 128: ec * D + fot * 128 + 128],
                                   wait=[(wsem, lw)] if ec == 0 else ())
                    wpe[sl] = self.cnt["pe"]
                woT_done = self.cnt["dve"]
            for st2 in range(NT1):
                for foc in range(2):
                    for fic in range(8):
                        on("pe", lambda e, o=psc[:, 0:512],
                           l=outT[:, fic * S + st2 * 128: fic * S + st2 * 128 + 128],
                           r=woT[:, fic * D + foc * 512: fic * D + foc * 512 + 512],
                           s=(fic == 0): e.matmul(o, l, r, start=s, stop=False),
                           wait=[("dve", woT_done), ("dve", heads_dve)] if fic == 0 else (),
                           inc="pe")
                    on("pe", lambda e, o=psc[:, 0:512], l=sb["onesb"][0:1, 0:128],
                       r=sb["abo_bf"][0:1, foc * 512:(foc + 1) * 512]:
                       e.matmul(o, l, r, start=False, stop=True, tile_position=(0, 0)),
                       inc="pe")
                    pe0 = self.cnt["pe"]
                    on("dve", lambda e,
                       o=x[:, st2 * D + foc * 512: st2 * D + foc * 512 + 512],
                       i=psc[:, 0:512]: e.tensor_tensor(out=o, in0=o, in1=i, op=ALU.add),
                       wait=[("pe", pe0)], inc="dve")
                    on("pe", None, wait=[("dve", self.cnt["dve"])])
            res_done = self.cnt["dve"]
            if export == "a2a":
                for th in range(2):
                    eap = AP(dr["a2a_in"].ap().tensor, b * SL * D + th * 128 * D,
                             [[D, 128], [BC * SL * D, NCORES], [1, D]])
                    sap = AP(x[:, :].tensor, th * D,
                             [[NT1 * D, 128], [2 * D, NCORES], [1, D]])
                    ex = self.dma("pool", "A2", eap, sap, wait=[("dve", res_done)])
            else:
                eap = AP(dr[export].ap().tensor, b * S * D, [[D, 128], [128 * D, NT1], [1, D]])
                sap = AP(x[:, :].tensor, 0, [[NT1 * D, 128], [D, NT1], [1, D]])
                ex = self.dma("pool", "A2", eap, sap, wait=[("dve", res_done)])
            prev_b = (res_done, ex)
        self.on("sp", None, wait=[("A2", self.cnt["A2"])])
        self.barrier()
        pstk.close()
        stk.close()

    def a2a(self):
        cw = [("A2", self.cnt["A2"]), ("St0", 1), ("St1", 1)]
        ins = [self.dr["a2a_in"].ap().opt()]
        outs = [self.dr["a2a_out"].ap().opt()]

        def fn(e, ins=ins, outs=outs):
            with allow_ici_all_to_all():
                return e.collective_compute(
                    "AllToAll", ALU.bypass, replica_groups=[list(range(NCORES))],
                    ins=ins, outs=outs)
        self.on("pool", fn, wait=cw, inc="cc")

    def ffn(self, dst="a2a_in", src="a2a_out"):
        sb, dr, ps, on, dma = self.sb, self.dr, self.ps, self.on, self.dma
        stk = ExitStack()
        SB = lambda n, s, d: self.SB(n, s, d, stk)
        uniq = "f%d" % self.cnt["cc"]
        NTOK = NT2 * 128
        lnxT = SB("lnxT" + uniq, [128, 8 * NTOK], BF16)
        cst = SB("c" + uniq, [128, 8 * SL], F32)
        cbf = SB("cbf" + uniq, [128, 8 * SL], BF16)
        xt = [SB("xt0" + uniq, [128, D], F32), SB("xt1" + uniq, [128, D], F32)]
        gf = SB("gf" + uniq, [128, SL], F32)
        gi = SB("gi" + uniq, [128, SL], F32)
        gc = SB("gc" + uniq, [128, SL], F32)
        gtmp = SB("gtmp" + uniq, [128, SL], F32)
        wstg2 = SB("wstg2" + uniq, [128, D], BF16)
        wstg2B = SB("wstg2B" + uniq, [128, D], BF16)
        wsl2 = [wstg2, wstg2B]
        wpe2 = [self.cnt["pe"], self.cnt["pe"]]
        dve_ent = self.cnt["dve"]
        wix2 = [0]
        pstk = ExitStack()
        pg = {"f": self.PSm("pgf" + uniq, [128, 512], F32, pstk),
              "i": self.PSm("pgi" + uniq, [128, 512], F32, pstk),
              "c": self.PSm("pgc" + uniq, [128, 512], F32, pstk)}
        py = self.PSm("py" + uniq, [128, 512], F32, pstk)

        def tile_ap(dram, tt):
            return AP(dram.ap().tensor,
                      (tt // 4) * (BC * SL * D) + ((tt // 2) % 2) * (SL * D)
                      + (tt % 2) * 128 * D,
                      [[D, 128], [1, D]])

        ccw = [("cc", self.cnt["cc"])]
        wstk = ExitStack()
        mats = {}
        for nm in ("fwf", "fuf", "fwi", "fui", "fwc", "fuc"):
            mats[nm] = self.SB(nm + uniq, [128, 8 * D], BF16, wstk)
            for fot in range(8):
                sl = wix2[0] % 2
                wsem = ("Lw", "LwB")[sl]
                stg = wsl2[sl]
                lw = dma("pool", wsem, stg[:, :],
                         dr[nm].ap()[fot * 128:(fot + 1) * 128, :],
                         wait=[("pe", wpe2[sl])] +
                              ([("dve", dve_ent)] if wix2[0] < 2 else []))
                wix2[0] += 1
                for ec in range(8):
                    self.trblk(stg[:, ec * 128:(ec + 1) * 128],
                               mats[nm][:, ec * D + fot * 128: ec * D + fot * 128 + 128],
                               wait=[(wsem, lw)] if ec == 0 else ())
                wpe2[sl] = self.cnt["pe"]
        mats_done = self.cnt["dve"]
        xld = []
        for tt in range(NT2):
            xb = xt[tt % 2]
            w = list(ccw)
            if tt >= 2:
                w.append(("dve", xld[tt - 2]))
            ld = dma("sp", "Lx%d" % (tt % 2), xb[:, :], tile_ap(dr[src], tt), wait=w)
            self.ln_sq = "c" + uniq
            dv = self.ln_tile(xb[:, :],
                              lambda ft, tt=tt: lnxT[:, ft * NTOK + tt * 128: ft * NTOK + (tt + 1) * 128],
                              [("Lx%d" % (tt % 2), ld)])
            xld.append(dv)
        ln_dve = self.cnt["dve"]
        for fot in range(8):
            on("act", lambda e, o=cst[:, fot * SL:(fot + 1) * SL],
               i=sb["onesb"][:, 0:SL], b=sb["finc_t"][:, fot:fot + 1]: e.activation(
                   out=o, in_=i, func=AF.Identity, scale=0.0, bias=b),
               wait=[("dve", ln_dve)] if fot == 0 else (), inc="act")
            ac9 = self.cnt["act"]
            on("dve", lambda e, o=cbf[:, fot * SL:(fot + 1) * SL],
               i=cst[:, fot * SL:(fot + 1) * SL]: e.tensor_copy(o, i),
               wait=[("act", ac9)], inc="dve")
        cbf_done = self.cnt["dve"]
        gates = (("fwf", "fuf", "f", gf, AF.Sigmoid, "fbf_t"),
                 ("fwi", "fui", "i", gi, AF.Sigmoid, "fbi_t"),
                 ("fwc", "fuc", "c", gc, AF.Tanh, "fbc_t"))
        for t in range(TS):
            gate_ev = getattr(self, "_gate_ev", None)
            if gate_ev is None or t == 0:
                gate_ev = self._gate_ev = {"f": 0, "i": 0, "c": 0}
            for fot in range(8):
                for wn, un, pn, gb, fn_, bn in gates:
                    pgn = pg[pn]
                    for fic in range(8):
                        on("pe", lambda e, o=pgn[:, 0:SL],
                           l=mats[wn][:, fic * D + fot * 128: fic * D + fot * 128 + 128],
                           r=lnxT[:, fic * NTOK + t * SL: fic * NTOK + (t + 1) * SL],
                           s=(fic == 0): e.matmul(o, l, r, start=s, stop=False),
                           wait=[("dve", mats_done), ("dve", ln_dve),
                                 ("dve", cbf_done), ("act", gate_ev[pn])]
                           if fic == 0 else (), inc="pe")
                    for fic in range(8):
                        on("pe", lambda e, o=pgn[:, 0:SL],
                           l=mats[un][:, fic * D + fot * 128: fic * D + fot * 128 + 128],
                           r=cbf[:, fic * SL:(fic + 1) * SL],
                           t_=(fic == 7): e.matmul(o, l, r, start=False, stop=t_),
                           inc="pe")
                    pe0 = self.cnt["pe"]
                    on("act", lambda e, o=gb[:, :], i=pgn[:, 0:SL], fn_=fn_,
                       bi=sb[bn][:, fot:fot + 1]: e.activation(
                           out=o, in_=i, func=fn_, bias=bi),
                       wait=[("pe", pe0), ("dve", self.cnt["dve"])], inc="act")
                    gate_ev[pn] = self.cnt["act"]
                ev = self.cnt["act"]
                cs = cst[:, fot * SL:(fot + 1) * SL]
                on("dve", lambda e, o=gtmp[:, :], i=gi[:, :], j=gc[:, :]:
                   e.tensor_tensor(out=o, in0=i, in1=j, op=ALU.mult),
                   wait=[("act", ev)], inc="dve")
                on("dve", lambda e, o=cs, i=gf[:, :]: e.tensor_tensor(
                    out=o, in0=o, in1=i, op=ALU.mult), inc="dve")
                on("dve", lambda e, o=cs, i=gtmp[:, :]: e.tensor_tensor(
                    out=o, in0=o, in1=i, op=ALU.add), inc="dve")
            step_pe = self.cnt["pe"]
            for fot in range(8):
                on("dve", lambda e, o=cbf[:, fot * SL:(fot + 1) * SL],
                   i=cst[:, fot * SL:(fot + 1) * SL]: e.tensor_copy(o, i),
                   wait=[("pe", step_pe)] if fot == 0 else (), inc="dve")
            cbf_done = self.cnt["dve"]
        scan_pe = self.cnt["pe"]
        wstk.close()
        fstk = ExitStack()
        woT = self.SB("fwoT" + uniq, [128, 8 * D], BF16, fstk)
        uoT = self.SB("fuoT" + uniq, [128, 8 * D], BF16, fstk)
        cpart = self.SB("cpart" + uniq, [128, 2 * D], F32, fstk)
        for nm, tgt in (("fwo", woT), ("fuo", uoT)):
            for fot in range(8):
                sl = wix2[0] % 2
                wsem = ("Lw", "LwB")[sl]
                stg = wsl2[sl]
                lw = dma("pool", wsem, stg[:, :],
                         dr[nm].ap()[fot * 128:(fot + 1) * 128, :],
                         wait=[("pe", wpe2[sl])])
                wix2[0] += 1
                for ec in range(8):
                    self.trblk(stg[:, ec * 128:(ec + 1) * 128],
                               tgt[:, ec * D + fot * 128: ec * D + fot * 128 + 128],
                               wait=[(wsem, lw)] if ec == 0 else ())
                wpe2[sl] = self.cnt["pe"]
        wo_done = self.cnt["dve"]
        if self.dev.get("dbgf") == self.cnt["cc"]:
            self.dma("pool", "A2", dr["t_flnxT"].ap(), lnxT[:, 0:NTOK],
                     wait=[("dve", wo_done)])
            self.dma("pool", "A2", dr["t_fcbf"].ap(), cbf[:, :],
                     wait=[("dve", wo_done)])
        for stl in range(2):
            for foc in range(2):
                for fmc in range(8):
                    on("pe", lambda e, o=py[:, 0:512],
                       l=cbf[:, fmc * SL + stl * 128: fmc * SL + stl * 128 + 128],
                       r=uoT[:, fmc * D + foc * 512: fmc * D + foc * 512 + 512],
                       s=(fmc == 0), t_=(fmc == 7): e.matmul(o, l, r, start=s, stop=t_),
                       wait=[("dve", wo_done), ("dve", cbf_done),
                             ("act", self.cnt["act"])] if fmc == 0 else (),
                       inc="pe")
                pe0 = self.cnt["pe"]
                on("act", lambda e, o=cpart[:, stl * D + foc * 512: stl * D + foc * 512 + 512],
                   i=py[:, 0:512]: e.activation(out=o, in_=i, func=AF.Copy),
                   wait=[("pe", pe0)], inc="act")
        cpart_done = self.cnt["act"]
        pass
        xold = {}
        for tt in range(2):
            xold[tt] = dma("sp", "Lx%d" % (tt % 2), xt[tt % 2][:, :], tile_ap(dr[src], tt),
                           wait=[("dve", ln_dve)])
        for tt in range(NT2):
            xb = xt[tt % 2]
            stl = tt % 2
            for foc in range(2):
                for fic in range(8):
                    on("pe", lambda e, o=py[:, 0:512],
                       l=lnxT[:, fic * NTOK + tt * 128: fic * NTOK + tt * 128 + 128],
                       r=woT[:, fic * D + foc * 512: fic * D + foc * 512 + 512],
                       s=(fic == 0): e.matmul(o, l, r, start=s, stop=False),
                       wait=[("act", cpart_done)] if fic == 0 else (), inc="pe")
                on("pe", lambda e, o=py[:, 0:512], l=sb["onesb"][0:1, 0:128],
                   r=sb["fbo_bf"][0:1, foc * 512:(foc + 1) * 512]:
                   e.matmul(o, l, r, start=False, stop=True, tile_position=(0, 0)),
                   inc="pe")
                pe0 = self.cnt["pe"]
                on("dve", lambda e, o=sb["scr"][:, :], j=py[:, 0:512],
                   i=cpart[:, stl * D + foc * 512: stl * D + foc * 512 + 512]:
                   e.tensor_tensor(out=o, in0=i, in1=j, op=ALU.add),
                   wait=[("pe", pe0), ("act", cpart_done),
                         ("act", self.cnt["act"])], inc="dve")
                dv0 = self.cnt["dve"]
                on("act", lambda e, o=sb["scr"][:, :]: e.activation(
                    out=o, in_=o, func=AF.Tanh), wait=[("dve", dv0)], inc="act")
                ac0 = self.cnt["act"]
                on("dve", lambda e, o=xb[:, foc * 512:(foc + 1) * 512],
                   i=sb["scr"][:, :]: e.tensor_tensor(out=o, in0=o, in1=i, op=ALU.add),
                   wait=[("act", ac0), ("Lx%d" % (tt % 2), xold[tt])], inc="dve")
                on("pe", None, wait=[("dve", self.cnt["dve"])])
            res_done = self.cnt["dve"]
            stsem = "St%d" % (tt % 2)
            if dst == "a2a_in":
                st_ap = tile_ap(dr["a2a_in"], tt)
            else:
                st_ap = dr["y"].ap()[tt * 128:(tt + 1) * 128, :]
            self.dma("pool", stsem, st_ap, xb[:, :], wait=[("dve", res_done)])
            nxt = tt + 2
            if nxt < NT2:
                xold[nxt] = dma("sp", "Lx%d" % (nxt % 2), xb[:, :], tile_ap(dr[src], nxt),
                                wait=[(stsem, 1)])
        self.on("sp", None, wait=[("St0", 1), ("St1", 1)])
        self.barrier()
        pstk.close()
        fstk.close()
        stk.close()

    def attn_b(self, src="a2a_out", export="a2a"):
        sb, dr, ps, on, dma = self.sb, self.dr, self.ps, self.on, self.dma
        stk = ExitStack()
        SB = lambda n, s, d: self.SB(n, s, d, stk)
        x = SB("bx", [128, NT1 * D], F32)
        lnxT = SB("blnxT", [128, 8 * S], BF16)
        outT = SB("boutT", [128, 8 * S], BF16)
        relkT = SB("relkT", [128, NPOS + 1], BF16)
        qreb = SB("qreb", [128, S], BF16)
        qrpb = SB("qrpb", [128, S], BF16)
        kT = SB("bkT", [128, S], BF16)
        vv = SB("bvv", [128, S], BF16)
        winsb = SB("winsb", [128, WIN], BF16)
        bandT = SB("bandT", [128, 1024], BF16)
        pt = SB("bpt", [128, 1024], BF16)
        rden = SB("brden", [128, 512], BF16)
        wstg = SB("bwstg", [128, D], BF16)
        wstgD = SB("bwstgD", [128, D], BF16)
        wslb = [wstg, wstgD]
        wpeb = [self.cnt["pe"], self.cnt["pe"]]
        wixb = [0]
        wrelstg = SB("wrelstg", [128, 256], BF16)
        pstg0 = SB("pstg0", [128, 1024], BF16)
        pstg1 = SB("pstg1", [64, 1024], BF16)
        woutb = SB("woutb", [128, 8 * D], BF16)
        band32 = SB("band32", [128, 512], F32)
        pstk = ExitStack()
        psc = self.PSm("bpsc", [128, 1024], F32, pstk)
        pwin = self.PSm("pwin", [128, 512], F32, pstk)
        ppv = self.PSm("bppv", [128, S], F32, pstk)
        wout_ld = None

        prev_b = (0, 0)
        for b in range(BC):
            impw = [("dve", prev_b[0]), ("A2", prev_b[1]), ("cc", self.cnt["cc"])]
            if src == "a2a_out":
                for th in range(2):
                    src_ap = AP(dr["a2a_out"].ap().tensor, b * SL * D + th * 128 * D,
                                [[D, 128], [BC * SL * D, NCORES], [1, D]])
                    dst_ap = AP(x[:, :].tensor, th * D,
                                [[NT1 * D, 128], [2 * D, NCORES], [1, D]])
                    xi = dma("sp", "Ls", dst_ap, src_ap, wait=impw if th == 0 else ())
            else:
                src_ap = AP(dr[src].ap().tensor, b * S * D,
                            [[D, 128], [128 * D, NT1], [1, D]])
                dst_ap = AP(x[:, :].tensor, 0, [[NT1 * D, 128], [D, NT1], [1, D]])
                xi = dma("sp", "Ls", dst_ap, src_ap, wait=impw)
            for tt in range(NT1):
                self.ln_sq = "scr"
                self.ln_tile(x[:, tt * D:(tt + 1) * D],
                             lambda ft, tt=tt: lnxT[:, ft * S + tt * 128: ft * S + (tt + 1) * 128],
                             [("Ls", xi)],
                             xt_halves=(x[:, tt * D:tt * D + 512],
                                        x[:, tt * D + 512:(tt + 1) * D]))
            ln_dve = self.cnt["dve"]
            if self.dev.get("dbg8"):
                self.dma("pool", "A2", dr["t_vv"].ap(), lnxT[:, 0:S],
                         wait=[("dve", ln_dve)])
                if "t_x" in dr:
                    self.dma("pool", "A2", dr["t_x"].ap(), x[:, :],
                             wait=[("dve", ln_dve)])
                self.on("sp", None, wait=[("A2", self.cnt["A2"])])
                self.barrier()
                pstk.close()
                stk.close()
                return

            for h in range(H2):
                hc = h * 128
                for wname in ("bwq", "bwk", "bwv"):
                    sl = wixb[0] % 2
                    wixb[0] += 1
                    wsem = ("Lw", "LwB")[sl]
                    stgb = wslb[sl]
                    lw = dma("pool", wsem,
                             AP(stgb[:, :].tensor, 0, [[D, 128], [128, 8], [1, 128]]),
                             AP(dr[wname].ap().tensor, hc, [[D, 128], [128 * D, 8], [1, 128]]),
                             wait=[("pe", wpeb[sl])])
                    for qc in range(4):
                        for ec in range(8):
                            on("pe", lambda e, o=psc[:, 0:512],
                               l=stgb[:, ec * 128:(ec + 1) * 128],
                               r=lnxT[:, ec * S + qc * 512: ec * S + qc * 512 + 512],
                               s=(ec == 0), t=(ec == 7)
                               : e.matmul(o, l, r, start=s, stop=t),
                               wait=[(wsem, lw), ("dve", ln_dve)] if ec == 0 else (),
                               inc="pe")
                        pe0 = self.cnt["pe"]
                        if wname == "bwq":
                            on("act", lambda e, o=qreb[:, qc * 512:(qc + 1) * 512],
                               i=psc[:, 0:512], bi=sb["breb_t"][:, h:h + 1]:
                               e.activation(out=o, in_=i, func=AF.Identity, scale=SC2, bias=bi),
                               wait=[("pe", pe0)], inc="act")
                            on("act", lambda e, o=qrpb[:, qc * 512:(qc + 1) * 512],
                               i=psc[:, 0:512], bi=sb["brpb_t"][:, h:h + 1]:
                               e.activation(out=o, in_=i, func=AF.Identity, scale=SC2, bias=bi),
                               inc="act")
                        elif wname == "bwk":
                            on("act", lambda e, o=kT[:, qc * 512:(qc + 1) * 512],
                               i=psc[:, 0:512]: e.activation(out=o, in_=i, func=AF.Copy),
                               wait=[("pe", pe0)], inc="act")
                        else:
                            pbh = 0 if qc % 2 == 0 else 512
                            on("act", lambda e, o=pt[:, pbh:pbh + 512],
                               i=psc[:, 0:512]: e.activation(out=o, in_=i, func=AF.Copy),
                               wait=[("pe", pe0), ("dve", self.cnt["dve"])], inc="act")
                        on("pe", None, wait=[("act", self.cnt["act"])])
                        if wname == "bwv":
                            pbh = 0 if qc % 2 == 0 else 512
                            acv = self.cnt["act"]
                            for blk in range(4):
                                self.trblk(pt[:, pbh + blk * 128: pbh + (blk + 1) * 128],
                                           vv[:, (qc * 4 + blk) * 128: (qc * 4 + blk + 1) * 128],
                                           wait=[("act", acv)] if blk == 0 else ())
                    wpeb[sl] = self.cnt["pe"]
                vv_done = self.cnt["dve"]
                if self.dev.get("dbg6"):
                    self.dma("pool", "A2", dr["t_qreb"].ap(), qreb[:, :],
                             wait=[("dve", vv_done), ("act", self.cnt["act"])])
                    self.dma("pool", "A2", dr["t_vv"].ap(), vv[:, :],
                             wait=[("dve", vv_done)])
                    self.on("sp", None, wait=[("A2", self.cnt["A2"])])
                    self.barrier()
                    pstk.close()
                    stk.close()
                    return
                lw1 = dma("pool", "Lr", wrelstg[:, 0:128],
                          AP(dr["bwrel"].ap().tensor, hc, [[D, 128], [1, 128]]),
                          wait=[("pe", self.cnt["pe"])])
                lw2 = dma("pool", "Lr", wrelstg[0:64, 128:256],
                          AP(dr["bwrel"].ap().tensor, 128 * D + hc, [[D, 64], [1, 128]]))
                relk_ac = 0
                for c in range(8):
                    cn = min(512, NPOS - c * 512)
                    pb = c % 2
                    l1 = dma("pool", "Lp%d" % pb, pstg0[:, pb * 512: pb * 512 + cn],
                             AP(dr["posT"].ap().tensor, c * 512, [[NPOS, 128], [1, cn]]),
                             wait=[("act", relk_ac)])
                    l2 = dma("pool", "Lp%d" % pb, pstg1[0:64, pb * 512: pb * 512 + cn],
                             AP(dr["posT"].ap().tensor, 128 * NPOS + c * 512,
                                [[NPOS, 64], [1, cn]]))
                    on("pe", lambda e, o=psc[:, 0:cn], l=wrelstg[:, 0:128],
                       r=pstg0[:, pb * 512: pb * 512 + cn]: e.matmul(
                           o, l, r, start=True, stop=False),
                       wait=[("Lp%d" % pb, l2), ("Lr", lw2), ("act", relk_ac)], inc="pe")
                    on("pe", lambda e, o=psc[:, 0:cn], l=wrelstg[0:64, 128:256],
                       r=pstg1[0:64, pb * 512: pb * 512 + cn]: e.matmul(
                           o, l, r, start=False, stop=True), inc="pe")
                    pe0 = self.cnt["pe"]
                    on("act", lambda e, o=relkT[:, c * 512: c * 512 + cn],
                       i=psc[:, 0:cn]: e.activation(out=o, in_=i, func=AF.Copy),
                       wait=[("pe", pe0)], inc="act")
                    relk_ac = self.cnt["act"]
                if self.dev.get("dbg7"):
                    self.dma("pool", "A2", dr["t_relkT"].ap(), relkT[:, :],
                             wait=[("act", relk_ac)])
                    self.on("sp", None, wait=[("A2", self.cnt["A2"])])
                    self.barrier()
                    pstk.close()
                    stk.close()
                    return
                wev = relk_ac
                win_dve = self.cnt["dve"]
                for qt in range(NT1):
                    u0 = 1920 - qt * 128
                    for uc in range(5):
                        cn = min(512, WIN - uc * 512)
                        on("pe", lambda e, o=pwin[:, 0:cn],
                           l=qrpb[:, qt * 128:(qt + 1) * 128],
                           r=relkT[:, u0 + uc * 512: u0 + uc * 512 + cn]:
                           e.matmul(o, l, r, start=True, stop=True),
                           wait=[("act", wev), ("dve", win_dve)], inc="pe")
                        pe0 = self.cnt["pe"]
                        on("act", lambda e, o=winsb[:, uc * 512: uc * 512 + cn],
                           i=pwin[:, 0:cn]: e.activation(out=o, in_=i, func=AF.Copy),
                           wait=[("pe", pe0), ("Wh", 1)], inc="act")
                        wev = self.cnt["act"]
                    self.dma("sp", "Wh",
                             AP(dr["win_hbm"].ap().tensor, qt * 128 * WIN,
                                [[WIN + 1, 128], [1, WIN]]),
                             winsb[:, :], wait=[("act", wev), ("Bt0", 1), ("Bt1", 1)])
                wh_all = self.cnt["Wh"]
                if self.dev.get("dbg5"):
                    self.dma("pool", "A2", dr["t_winsb"].ap(), winsb[:, :],
                             wait=[("Wh", wh_all)])
                    self.dma("pool", "A2", dr["t_relkT"].ap(), relkT[:, :],
                             wait=[("Wh", wh_all)])
                    if self.dev.get("dbg5") == 2:
                        self.on("sp", lambda e, o=bandT[:, 0:128],
                                i=AP(dr["win_hbm"].ap().tensor, 127, [[WIN - 1, 128], [1, 128]]):
                                e.dma_start_transpose(out=o, in_=i),
                                wait=[("Wh", wh_all)], inc="Bt", incn=16)
                        self.dma("pool", "A2", dr["t_bandT"].ap(), bandT[:, :],
                                 wait=[("Bt", self.cnt["Bt"])])
                    self.on("sp", None, wait=[("A2", self.cnt["A2"])])
                    self.barrier()
                    pstk.close()
                    stk.close()
                    return
                exp_done = self.cnt["act"]
                ptf = {0: 0, 1: 0}

                def emit_band(qc, kt, bb, wv):
                    sname = "Bt%d" % bb
                    ap_in = AP(dr["win_hbm"].ap().tensor,
                               127 + qc * 512 * WIN + kt * 128,
                               [[WIN, 512], [1, 128]])
                    self.on("sp", lambda e, o=bandT[:, bb * 512:(bb + 1) * 512],
                            i=ap_in: e.dma_start_transpose(out=o, in_=i),
                            wait=wv, inc=sname, incn=16)
                    return (sname, self.cnt[sname])
                for qc in range(4):
                    btd = {}
                    for kt in range(2):
                        btd[kt] = emit_band(qc, kt, kt % 2,
                                            [("Wh", wh_all), ("dve", self.cnt["dve"])])
                    exp_hist = [exp_done, exp_done]
                    mult_hist = [0, 0]
                    pend = None
                    for kt in range(NT1):
                        bb = kt % 2
                        pb = kt % 2
                        pscb = psc[:, bb * 512:(bb + 1) * 512]
                        scrb = sb["scr"][:, :] if bb == 0 else sb["scr2"][:, :]
                        on("pe", lambda e, o=pscb,
                           l=kT[:, kt * 128:(kt + 1) * 128],
                           r=qreb[:, qc * 512:(qc + 1) * 512]:
                           e.matmul(o, l, r, start=True, stop=True),
                           wait=[("dve", mult_hist[bb])], inc="pe")
                        pe0 = self.cnt["pe"]
                        if pend is not None:
                            pend()
                        on("dve", lambda e, o=sb["band32"][:, 0:512],
                           m=bandT[:, bb * 512:(bb + 1) * 512]:
                           e.tensor_copy(o, m),
                           wait=[btd[kt], ("dve", self.cnt["dve"])], inc="dve")
                        on("dve", lambda e, o=scrb,
                           m=sb["band32"][:, 0:512],
                           i=pscb: e.tensor_tensor(out=o, in0=m, in1=i, op=ALU.add),
                           wait=[("pe", pe0), ("act", exp_hist[bb])], inc="dve")
                        dv0 = mult_hist[bb] = self.cnt["dve"]
                        on("act", lambda e, o=pt[:, pb * 512:(pb + 1) * 512],
                           i=scrb, b_=sb["nb30"][:, 0:1]: e.activation(
                               out=o, in_=i, func=AF.Exp, bias=b_),
                           wait=[("dve", dv0), ("pe", ptf[pb])], inc="act")
                        exp_done = exp_hist[bb] = self.cnt["act"]

                        def mk_pend(kt=kt, pb=pb, exp_done=exp_done, qc=qc):
                            def pend_():
                                on("pe", lambda e, o=pwin[0:1, 0:512],
                                   l=sb["onesc"][:, 0:1],
                                   r=pt[:, pb * 512:(pb + 1) * 512],
                                   s=(kt == 0), t=(kt == NT1 - 1):
                                   e.matmul(o, l, r, start=s, stop=t),
                                   wait=[("act", exp_done)], inc="pe")
                                on("pe", lambda e, o=ppv[:, qc * 512:(qc + 1) * 512],
                                   l=vv[:, kt * 128:(kt + 1) * 128],
                                   r=pt[:, pb * 512:(pb + 1) * 512],
                                   s=(kt == 0), t=(kt == NT1 - 1):
                                   e.matmul(o, l, r, start=s, stop=t),
                                   wait=[("dve", vv_done)] if kt == 0 else (), inc="pe")
                                ptf[pb] = self.cnt["pe"]
                            return pend_
                        pend = mk_pend()
                        nkt = kt + 2
                        if nkt < NT1:
                            btd[nkt] = emit_band(qc, nkt, nkt % 2,
                                                 [("Wh", wh_all), ("dve", self.cnt["dve"])])
                    pend()
                    pend = None
                    pv_done = self.cnt["pe"]
                    on("dve", lambda e, o=rden[0:1, 0:512], i=pwin[0:1, 0:512]:
                       e.reciprocal(out=o, in_=i), wait=[("pe", pv_done)], inc="dve")
                    dv0 = self.cnt["dve"]
                    on("pe", lambda e, o=pwin[:, 0:512],
                       l=sb["onesb"][0:1, 0:128], r=rden[0:1, 0:512]:
                       e.matmul(o, l, r, start=True, stop=True),
                       wait=[("dve", dv0), ("act", self.cnt["act"])], inc="pe")
                    pe0 = self.cnt["pe"]
                    on("dve", lambda e, o=sb["repsb"][:, 0:512], i=pwin[:, 0:512]:
                       e.tensor_copy(o, i), wait=[("pe", pe0)], inc="dve")
                    dv1 = self.cnt["dve"]
                    on("dve", lambda e,
                       o=outT[:, h * S + qc * 512: h * S + (qc + 1) * 512],
                       i=ppv[:, qc * 512:(qc + 1) * 512],
                       r=sb["repsb"][:, 0:512]: e.tensor_tensor(
                           out=o, in0=i, in1=r, op=ALU.mult),
                       wait=[("dve", dv1)], inc="dve")
                    on("pe", None, wait=[("dve", self.cnt["dve"])])
                on("act", None, wait=[("dve", self.cnt["dve"])])
            heads_dve = self.cnt["dve"]
            if wout_ld is None:
                for fic in range(8):
                    dma("pool", "Lo", woutb[:, fic * D:(fic + 1) * D],
                        dr["bwout"].ap()[fic * 128:(fic + 1) * 128, :])
                wout_ld = self.cnt["Lo"]
            for st2 in range(NT1):
                for foc in range(2):
                    for fic in range(8):
                        on("pe", lambda e, o=psc[:, 0:512],
                           l=outT[:, fic * S + st2 * 128: fic * S + st2 * 128 + 128],
                           r=woutb[:, fic * D + foc * 512: fic * D + foc * 512 + 512],
                           s=(fic == 0): e.matmul(o, l, r, start=s, stop=False),
                           wait=[("Lo", wout_ld), ("dve", heads_dve)] if fic == 0 else (),
                           inc="pe")
                    on("pe", lambda e, o=psc[:, 0:512], l=sb["onesb"][0:1, 0:128],
                       r=sb["bbout_bf"][0:1, foc * 512:(foc + 1) * 512]:
                       e.matmul(o, l, r, start=False, stop=True, tile_position=(0, 0)),
                       inc="pe")
                    pe0 = self.cnt["pe"]
                    on("dve", lambda e,
                       o=x[:, st2 * D + foc * 512: st2 * D + foc * 512 + 512],
                       i=psc[:, 0:512]: e.tensor_tensor(out=o, in0=o, in1=i, op=ALU.add),
                       wait=[("pe", pe0)], inc="dve")
                    on("pe", None, wait=[("dve", self.cnt["dve"])])
            res_done = self.cnt["dve"]
            if export == "a2a":
                for th in range(2):
                    eap = AP(dr["a2a_in"].ap().tensor, b * SL * D + th * 128 * D,
                             [[D, 128], [BC * SL * D, NCORES], [1, D]])
                    sap = AP(x[:, :].tensor, th * D,
                             [[NT1 * D, 128], [2 * D, NCORES], [1, D]])
                    ex = self.dma("pool", "A2", eap, sap, wait=[("dve", res_done)])
            else:
                eap = AP(dr[export].ap().tensor, b * S * D, [[D, 128], [128 * D, NT1], [1, D]])
                sap = AP(x[:, :].tensor, 0, [[NT1 * D, 128], [D, NT1], [1, D]])
                ex = self.dma("pool", "A2", eap, sap, wait=[("dve", res_done)])
            prev_b = (res_done, ex)
        self.on("sp", None, wait=[("A2", self.cnt["A2"])])
        self.barrier()
        pstk.close()
        stk.close()

    def emit(self):
        dev = self.dev
        self.setup()
        phases = dev.get("phases",
                         ("attn_a", "a2a", "ffn1", "a2a", "attn_b", "a2a", "ffn2"))
        for ph in phases:
            if ph == "attn_a":
                self.attn_a(src=dev.get("attn_a_src", "x0"),
                            export=dev.get("attn_a_export", "a2a"))
            elif ph == "a2a":
                self.a2a()
            elif ph == "ffn1":
                self.ffn(dst=dev.get("ffn1_dst", "a2a_in"), src=dev.get("ffn1_src", "a2a_out"))
            elif ph == "ffn2":
                self.ffn(dst="y")
            elif ph == "attn_b":
                self.attn_b(src=dev.get("attn_b_src", "a2a_out"),
                            export=dev.get("attn_b_export", "a2a"))
        self.on("sp", None, wait=[(s0, 1) for s0 in
                                  ("Lg", "Lw", "LwB", "Lr", "Lp0", "Lp1", "Lo", "Lx0",
                                   "Lx1", "Ls", "St0", "St1", "Wh", "Bt0", "Bt1",
                                   "A2")])

    def build(self):
        self.declare()
        with self.st:
            self.alloc_global()
            self.PSm("tr", [128, 512], BF16)
            semstk = ExitStack()
            with semstk:
                for name in SEMS:
                    self.handles[name] = semstk.enter_context(self.nc.semaphore(name))
                self.emit()
                nc = self.nc
                with nc.allow_low_precision(reason="bf16 operand staging is intentional"), \
                     nc.Block() as blk:
                    @blk.tensor
                    def _(e):
                        for t in self.q["pe"]:
                            t(e)

                    @blk.vector
                    def _(e):
                        for t in self.q["dve"]:
                            t(e)

                    @blk.scalar
                    def _(e):
                        for t in self.q["act"]:
                            t(e)

                    @blk.gpsimd
                    def _(e):
                        for t in self.q["pool"]:
                            t(e)

                    @blk.sync
                    def _(e):
                        for t in self.q["sp"]:
                            t(e)
        return self.nc


_CONSTS = None


def _consts():
    global _CONSTS
    if _CONSTS is None:
        eye = np.eye(128, dtype=ml_dtypes.bfloat16)
        posT = _positions_T()
        absd = np.abs(np.arange(128, dtype=np.float32)[:, None]
                      - np.arange(128, dtype=np.float32)[None, :])
        _CONSTS = dict(eye=eye, posT=posT, absd=absd)
    return _CONSTS


def _in_maps(inputs, extra=None):
    c = _consts()
    x = np.ascontiguousarray(np.asarray(inputs["L0_attn_a_x"], dtype=np.float32))
    base = dict(
        aWq=inputs["L0_attn_a_Wq"], aWk=inputs["L0_attn_a_Wk"],
        aWv=inputs["L0_attn_a_Wv"], aWo=inputs["L0_attn_a_Wo"],
        abq=inputs["L0_attn_a_bq"], abk=inputs["L0_attn_a_bk"],
        abv=inputs["L0_attn_a_bv"], abo=inputs["L0_attn_a_bo"],
        adecay=inputs["L0_attn_a_decay_params"], amix=inputs["L0_attn_a_mix_weights"],
        bwq=inputs["L0_attn_b_w_qs"], bwk=inputs["L0_attn_b_w_ks"],
        bwv=inputs["L0_attn_b_w_vs"], bwrel=inputs["L0_attn_b_w_rel_pos_k"],
        breb=np.asarray(inputs["L0_attn_b_rel_embed_bias"]).reshape(H2, DH2),
        brpb=np.asarray(inputs["L0_attn_b_rel_pos_bias"]).reshape(H2, DH2),
        bwout=inputs["L0_attn_b_w_out"], bbout=inputs["L0_attn_b_b_out"],
        fwf=inputs["L0_ffn_wf"], fuf=inputs["L0_ffn_uf"], fbf=inputs["L0_ffn_bf"],
        fwi=inputs["L0_ffn_wi"], fui=inputs["L0_ffn_ui"], fbi=inputs["L0_ffn_bi"],
        fwc=inputs["L0_ffn_wc"], fuc=inputs["L0_ffn_uc"], fbc=inputs["L0_ffn_bc"],
        fwo=inputs["L0_ffn_wo"], fuo=inputs["L0_ffn_uo"], fbo=inputs["L0_ffn_bo"],
        finc=inputs["L0_ffn_in_c"],
        posT=c["posT"], absd=c["absd"],
    )
    base = {k: np.ascontiguousarray(np.asarray(v, dtype=np.float32))
            for k, v in base.items()}
    base["eye"] = c["eye"]
    maps = []
    for core in range(NCORES):
        m = dict(base)
        m["x0"] = np.ascontiguousarray(x[2 * core:2 * core + 2].reshape(BC * S, D))
        if extra:
            m.update(extra(core))
        maps.append(m)
    return maps


def kernel(**inputs):
    nc = MK().build()
    maps = _in_maps(inputs)
    res = run_bass_kernel_spmd(nc, maps, core_ids=list(range(NCORES)))
    out = np.empty((B, S, D), dtype=np.float32)
    for core in range(NCORES):
        yv = res.results[core]["y"].reshape(TS, SL, D)
        out[:, core * SL:(core + 1) * SL, :] = yv
    return out


# version 10 (final)
# speedup vs baseline: 1.0493x; 1.0138x over previous
import math
import numpy as np
from contextlib import ExitStack

import concourse.bass as bass
import concourse.mybir as mybir
from concourse.ap import AP
from concourse.bass_utils import run_bass_kernel_spmd
from concourse.aot_env import allow_ici_all_to_all
import ml_dtypes

F32 = mybir.dt.float32
BF16 = mybir.dt.bfloat16
AF = mybir.ActivationFunctionType
ALU = mybir.AluOpType
AX = mybir.AxisListType

NCORES = 8
B, S, D = 16, 2048, 1024
BC = 2
H1, DH1 = 16, 64
H2, DH2 = 8, 128
RP = 192
NPOS = 2 * S - 1
WIN = S + 127
SL = S // NCORES
TS = B
NT1 = S // 128
NT2 = (TS * SL) // 128
EPS = 1e-5
SC2 = DH2 ** -0.5

ENGS = ("pe", "dve", "act", "pool", "sp")
SEMS = ("Lg", "Lw", "LwB", "Lr", "Lp0", "Lp1", "Lo", "Lx0", "Lx1", "Ls", "St0", "St1",
        "Wh", "Bt0", "Bt1", "A2", "pe", "dve", "act", "pool", "cc", "bar")


def _positions_T():
    d = np.arange(-S + 1, S, dtype=np.float64)
    n = RP // 6
    absd = np.abs(d)[:, None]
    max_range = math.log(S) / math.log(2.0)
    half_life = 2.0 ** np.linspace(3.0, max_range, n)
    feat_exp = np.exp(-math.log(2.0) / half_life[None, :] * absd)
    center_widths = 2.0 ** np.arange(1, n + 1, dtype=np.float64) - 1.0
    feat_cm = (center_widths[None, :] > absd).astype(np.float64)
    stddev = S / (2.0 * n)
    start_mean = S / n
    mean = np.linspace(start_mean, float(S), n)[None, :]
    conc = (mean / stddev) ** 2
    rate = mean / stddev ** 2
    with np.errstate(divide="ignore", invalid="ignore"):
        logd = np.where(absd > 0, np.log(np.maximum(absd, 1e-300)), 0.0)
        log_unnorm = (conc - 1.0) * logd - rate * absd
    log_unnorm = np.where(absd > 0, log_unnorm, -np.inf)
    lgam = np.array([math.lgamma(c) for c in conc[0]])[None, :]
    log_norm = lgam - conc * np.log(rate)
    probs = np.exp(log_unnorm - log_norm) + 1e-08
    feat_gamma = probs / np.max(probs, axis=-1, keepdims=True)
    emb = np.concatenate([feat_exp, feat_cm, feat_gamma], axis=-1)
    emb = np.concatenate([emb, np.sign(d)[:, None] * emb], axis=-1)
    return np.ascontiguousarray(emb.T.astype(np.float32))


class MK:
    def __init__(self, dev=None):
        self.dev = dev or {}
        self.nc = bass.Bass("TRN2", target_bir_lowering=False)
        self.q = {e: [] for e in ENGS}
        self.cnt = {s: 0 for s in SEMS}
        self.handles = {}
        self.dr = {}
        self.sb = {}
        self.ps = {}
        self.st = ExitStack()
        self.tr_guard = 0

    def on(self, eng, fn=None, wait=(), inc=None, incn=1):
        wait = list(wait)
        if fn is not None and eng in ("act", "dve"):
            assert inc in (None, eng)
            wait.append((eng, self.cnt[eng]))
            inc = eng
            incn = 1
        if inc is not None:
            self.cnt[inc] += incn
        tgt = self.cnt[inc] if inc is not None else None
        DMAS = ("Lg", "Lw", "LwB", "Lr", "Lp0", "Lp1", "Lo", "Lx0", "Lx1", "Ls", "St0",
                "St1", "Wh", "Bt0", "Bt1", "A2")
        wait = tuple((s, self.cnt[s] if s in DMAS else v) for (s, v) in wait if v)
        h = self.handles

        def thunk(e, _fn=fn, _wait=wait, _inc=inc, _incn=incn):
            for s, v in _wait:
                e.wait_ge(h[s], v)
            inst = _fn(e) if _fn is not None else None
            if _inc is not None:
                if inst is None:
                    e.sem_inc(h[_inc], _incn)
                else:
                    inst.then_inc(h[_inc], _incn)
        self.q[eng].append(thunk)
        return tgt

    def dma(self, eng, sem, out, in_, wait=(), slow=False):
        return self.on(eng, lambda e, o=out, i=in_, s=slow: e.dma_start(
            out=o, in_=i, allow_slow_non_contiguous=s), wait=wait, inc=sem, incn=16)

    def barrier(self):
        for e in ENGS:
            self.on(e, None, inc="bar")
        tgt = self.cnt["bar"]
        for e in ENGS:
            self.on(e, None, wait=[("bar", tgt)])
        return tgt

    def din(self, name, shape, dtype=F32):
        self.dr[name] = self.nc.dram_tensor(name, shape, dtype, kind="ExternalInput")
        return self.dr[name]

    def dout(self, name, shape, dtype=F32):
        self.dr[name] = self.nc.dram_tensor(name, shape, dtype, kind="ExternalOutput")
        return self.dr[name]

    def dint(self, name, shape, dtype=F32):
        self.dr[name] = self.nc.dram_tensor(name, shape, dtype)
        return self.dr[name]

    def SB(self, name, shape, dtype, stack=None):
        t = (stack or self.st).enter_context(self.nc.sbuf_tensor(name, shape, dtype))
        self.sb[name] = t
        return t

    def PSm(self, name, shape, dtype=F32, stack=None):
        t = (stack or self.st).enter_context(self.nc.psum_tensor(name, shape, dtype))
        self.ps[name] = t
        return t

    def trblk(self, src, dst, wait=()):
        tr = self.ps["tr"]
        eye = self.sb["s_eye"]
        self.on("pe", lambda e, s=src, t=tr[:, 0:128], y=eye[:, :]: e.transpose(t, s, y),
                wait=list(wait) + [("dve", self.tr_guard)], inc="pe")
        pe0 = self.cnt["pe"]
        self.on("dve", lambda e, d=dst, t=tr[:, 0:128]: e.tensor_copy(d, t),
                wait=[("pe", pe0)], inc="dve")
        self.tr_guard = self.cnt["dve"]
        return self.tr_guard

    def declare(self):
        din, dout, dint = self.din, self.dout, self.dint
        din("x0", [BC * S, D])
        for nm in ("aWq", "aWk", "aWv", "aWo", "bwq", "bwk", "bwv", "bwout",
                   "fwf", "fuf", "fwi", "fui", "fwc", "fuc", "fwo", "fuo"):
            din(nm, [D, D])
        for nm in ("abq", "abk", "abv", "abo", "bbout", "fbf", "fbi", "fbc",
                   "fbo", "finc"):
            din(nm, [D])
        din("adecay", [H1, 2]); din("amix", [H1, 2])
        din("bwrel", [RP, D]); din("breb", [H2, DH2]); din("brpb", [H2, DH2])
        din("eye", [128, 128], BF16)
        din("posT", [RP, NPOS])
        din("absd", [128, 128])
        dout("y", [TS * SL, D])
        dint("a2a_in", [NCORES, BC, SL, D])
        dint("a2a_out", [NCORES, BC, SL, D])
        dint("win_hbm", [127 + S * WIN + WIN], BF16)
        for tname, tshape in self.dev.get("taps", {}).items():
            dout(tname, tshape)
        if self.dev.get("devffn"):
            din("devffn", [NCORES, BC, SL, D])

    def alloc_global(self):
        SB = self.SB
        SB("s_eye", [128, 128], BF16)
        SB("lnt", [128, D], BF16)
        SB("stats", [128, 16], F32)
        SB("onesb", [128, 512], BF16)
        SB("onesc", [128, 1], BF16)
        SB("repsb", [128, 512], F32)
        SB("scr", [128, 512], F32)
        SB("scr2", [128, 512], F32)

        SB("amixs2", [16, 2], F32)
        for nm in ("abq_t", "abk_t", "abv_t", "fbf_t", "fbi_t", "fbc_t", "finc_t"):
            SB(nm, [128, 8], F32)
        SB("breb_t", [128, H2], F32)
        SB("brpb_t", [128, H2], F32)
        SB("abo_bf", [1, D], BF16)
        SB("bbout_bf", [1, D], BF16)
        SB("fbo_bf", [1, D], BF16)
        SB("adp", [H1, 2], F32)
        SB("amixs", [H1, 2], F32)
        SB("alogT", [2, H1], F32)
        SB("alogTn", [2, H1], F32)
        SB("alogT2047", [2, H1], F32)
        SB("alogT127", [2, H1], F32)
        SB("amixT", [2, H1], F32)
        SB("alogF", [1, 2 * H1], F32)
        SB("amixF", [1, 2 * H1], F32)
        SB("alogFb", [1, 2 * H1], BF16)
        SB("amixFb", [1, 2 * H1], BF16)
        SB("s_absd", [128, 128], F32)
        SB("epsc", [128, 1], F32)
        SB("nb30", [128, 1], F32)

    def setup(self):
        sb, dr, dma, on = self.sb, self.dr, self.dma, self.on

        def col_ap(v):
            return AP(v.ap().tensor, 0, [[1, 128], [128, 8]])
        dma("pool", "Lg", sb["s_eye"][:, :], dr["eye"].ap())
        dma("pool", "Lg", sb["s_absd"][:, :], dr["absd"].ap())
        for nm, src in (("abq_t", "abq"), ("abk_t", "abk"), ("abv_t", "abv"),
                        ("fbf_t", "fbf"), ("fbi_t", "fbi"), ("fbc_t", "fbc"),
                        ("finc_t", "finc")):
            dma("pool", "Lg", sb[nm][:, :], col_ap(dr[src]), slow=True)
        for nm, src in (("abo_bf", "abo"), ("bbout_bf", "bbout"), ("fbo_bf", "fbo")):
            dma("pool", "Lg", sb[nm][0:1, :], AP(dr[src].ap().tensor, 0, [[D, 1], [1, D]]))
        dma("pool", "Lg", sb["breb_t"][:, :],
            AP(dr["breb"].ap().tensor, 0, [[1, DH2], [DH2, H2]]), slow=True)
        dma("pool", "Lg", sb["brpb_t"][:, :],
            AP(dr["brpb"].ap().tensor, 0, [[1, DH2], [DH2, H2]]), slow=True)
        dma("pool", "Lg", sb["adp"][:, :], dr["adecay"].ap())
        dma("pool", "Lg", sb["amixs"][:, :], dr["amix"].ap())
        lg = self.cnt["Lg"]
        on("dve", lambda e, o=sb["onesb"][:, :]: e.memset(o, 1.0), inc="dve")
        on("dve", lambda e, o=sb["onesc"][:, :]: e.memset(o, 1.0), inc="dve")
        on("dve", lambda e, o=sb["epsc"][:, :]: e.memset(o, EPS), inc="dve")
        on("dve", lambda e, o=sb["nb30"][:, :]: e.memset(o, -60.0), inc="dve")
        on("dve", lambda e, o=sb["adp"][:, :]: e.tensor_scalar(
            out=o, in0=o, scalar1=0.01, scalar2=0.9999, op0=ALU.max, op1=ALU.min),
            wait=[("Lg", lg)], inc="dve")
        dv = self.cnt["dve"]
        on("act", lambda e, o=sb["stats"][0:H1, 8:10], i=sb["adp"][:, :]: e.activation(
            out=o, in_=i, func=AF.Ln), wait=[("dve", dv)], inc="act")
        ln_ac = self.cnt["act"]
        on("dve", lambda e, o=sb["stats"][0:H1, 0:1], i=sb["amixs"][:, 0:1],
           j=sb["amixs"][:, 1:2]: e.tensor_tensor(out=o, in0=i, in1=j, op=ALU.subtract),
           wait=[("Lg", lg)], inc="dve")
        dvd = self.cnt["dve"]
        on("act", lambda e, o=sb["amixs2"][:, 0:1], i=sb["stats"][0:H1, 0:1]:
           e.activation(out=o, in_=i, func=AF.Sigmoid), wait=[("dve", dvd)], inc="act")
        on("act", lambda e, o=sb["amixs2"][:, 1:2], i=sb["stats"][0:H1, 0:1]:
           e.activation(out=o, in_=i, func=AF.Sigmoid, scale=-1.0), inc="act")
        ac2 = self.cnt["act"]
        on("dve", None, wait=[("act", ac2)], inc="dve")
        dv = self.cnt["dve"]
        for h in range(H1):
            for n in range(2):
                dma("pool", "Lg", sb["alogT"][n:n + 1, h:h + 1],
                    sb["stats"][h:h + 1, 8 + n:9 + n], wait=[("act", ln_ac), ("dve", dv)],
                    slow=True)
                dma("pool", "Lg", sb["amixT"][n:n + 1, h:h + 1],
                    sb["amixs2"][h:h + 1, n:n + 1], wait=[("dve", dv)], slow=True)
                dma("pool", "Lg", sb["alogF"][0:1, 2 * h + n:2 * h + n + 1],
                    sb["stats"][h:h + 1, 8 + n:9 + n], wait=[("act", ln_ac)], slow=True)
                dma("pool", "Lg", sb["amixF"][0:1, 2 * h + n:2 * h + n + 1],
                    sb["amixs2"][h:h + 1, n:n + 1], wait=[("dve", dv)], slow=True)
        lg = self.cnt["Lg"]
        on("dve", lambda e, o=sb["alogTn"][:, :], i=sb["alogT"][:, :]: e.tensor_scalar(
            out=o, in0=i, scalar1=-1.0, scalar2=None, op0=ALU.mult),
            wait=[("Lg", lg)], inc="dve")
        on("dve", lambda e, o=sb["alogT2047"][:, :], i=sb["alogT"][:, :]: e.tensor_scalar(
            out=o, in0=i, scalar1=2047.0, scalar2=None, op0=ALU.mult), inc="dve")
        on("dve", lambda e, o=sb["alogT127"][:, :], i=sb["alogT"][:, :]: e.tensor_scalar(
            out=o, in0=i, scalar1=127.0, scalar2=None, op0=ALU.mult), inc="dve")
        on("dve", lambda e, o=sb["alogFb"][:, :], i=sb["alogF"][:, :]: e.tensor_copy(o, i),
           wait=[("Lg", self.cnt["Lg"])], inc="dve")
        on("dve", lambda e, o=sb["amixFb"][:, :], i=sb["amixF"][:, :]: e.tensor_copy(o, i),
           inc="dve")
        self.on("pool", None, wait=[("Lg", self.cnt["Lg"])])
        if self.dev.get("dbg0"):
            dv = self.cnt["dve"]
            for nm, apx in (("t_amixs", sb["amixs"][:, :]), ("t_stats", sb["stats"][:, 0:16]),
                            ("t_amixs2", sb["amixs2"][:, :]),
                            ("t_alogF", sb["alogF"][:, :]), ("t_amixF", sb["amixF"][:, :]),
                            ("t_alogT", sb["alogT"][:, :]), ("t_amixT", sb["amixT"][:, :]),
                            ("t_adp", sb["adp"][:, :])):
                if nm in self.dr:
                    self.dma("pool", "A2", self.dr[nm].ap(), apx,
                             wait=[("dve", dv), ("Lg", self.cnt["Lg"])])
            self.on("sp", None, wait=[("A2", self.cnt["A2"])])
        self.barrier()

    def ln_tile(self, xt, lnxT_col_fn, wait, xt_halves=None):
        sb, on = self.sb, self.on
        c1 = 1.0 / D
        stats = sb["stats"]
        wait = list(wait) + [("dve", getattr(self, "ln_guard", 0))]
        on("dve", lambda e, o=stats[:, 0:1], i=xt: e.tensor_reduce(
            out=o, in_=i, axis=AX.X, op=ALU.add), wait=wait, inc="dve")
        sqb = sb[self.ln_sq]
        if xt_halves is None:
            on("act", lambda e, o=sqb[:, 0:D], i=xt: e.activation(
                out=o, in_=i, func=AF.Square), wait=wait, inc="act")
            ac0 = self.cnt["act"]
            on("dve", lambda e, o=stats[:, 1:2], i=sqb[:, 0:D]: e.tensor_reduce(
                out=o, in_=i, axis=AX.X, op=ALU.add), wait=[("act", ac0)], inc="dve")
        else:
            h0, h1 = xt_halves
            on("act", lambda e, o=sqb[:, 0:512], i=h0: e.activation(
                out=o, in_=i, func=AF.Square), wait=wait, inc="act")
            ac0a = self.cnt["act"]
            on("dve", lambda e, o=stats[:, 1:2], i=sqb[:, 0:512]: e.tensor_reduce(
                out=o, in_=i, axis=AX.X, op=ALU.add), wait=[("act", ac0a)], inc="dve")
            dv0a = self.cnt["dve"]
            on("act", lambda e, o=sqb[:, 0:512], i=h1: e.activation(
                out=o, in_=i, func=AF.Square), wait=[("dve", dv0a)], inc="act")
            ac0 = self.cnt["act"]
            on("dve", lambda e, o=stats[:, 8:9], i=sqb[:, 0:512]: e.tensor_reduce(
                out=o, in_=i, axis=AX.X, op=ALU.add), wait=[("act", ac0)], inc="dve")
            on("dve", lambda e, o=stats[:, 1:2], i=stats[:, 1:2], j=stats[:, 8:9]:
               e.tensor_tensor(out=o, in0=i, in1=j, op=ALU.add), inc="dve")
        dv0 = self.cnt["dve"]
        on("dve", lambda e, o=stats[:, 2:3], i=stats[:, 0:1]: e.tensor_scalar(
            out=o, in0=i, scalar1=c1, scalar2=None, op0=ALU.mult),
            wait=[("dve", dv0), ("act", ac0)], inc="dve")
        on("dve", lambda e, o=stats[:, 3:4], i=stats[:, 2:3]: e.tensor_tensor(
            out=o, in0=i, in1=i, op=ALU.mult), inc="dve")
        on("dve", lambda e, o=stats[:, 4:5], i=stats[:, 1:2], j=stats[:, 3:4]:
            e.scalar_tensor_tensor(out=o, in0=i, scalar=c1, in1=j,
                                   op0=ALU.mult, op1=ALU.subtract),
            wait=[("act", getattr(self, "ln_act_guard", 0))], inc="dve")
        dv1 = self.cnt["dve"]
        self.ln_guard = dv1
        on("act", lambda e, o=stats[:, 5:6], i=stats[:, 4:5], b=sb["epsc"][:, 0:1]:
            e.activation(out=o, in_=i, func=AF.Sqrt, bias=b), wait=[("dve", dv1)], inc="act")
        self.ln_act_guard = self.cnt["act"]
        ac1 = self.cnt["act"]
        on("dve", lambda e, o=stats[:, 6:7], i=stats[:, 5:6]: e.reciprocal(out=o, in_=i),
           wait=[("act", ac1)], inc="dve")
        on("dve", lambda e, o=stats[:, 7:8], i=stats[:, 2:3], j=stats[:, 6:7]:
            e.tensor_tensor(out=o, in0=i, in1=j, op=ALU.mult), inc="dve")
        on("dve", lambda e, o=stats[:, 7:8], i=stats[:, 7:8]: e.tensor_scalar(
            out=o, in0=i, scalar1=-1.0, scalar2=None, op0=ALU.mult), inc="dve")
        dvr = self.cnt["dve"]
        lnt = sb["lnt"]
        on("act", lambda e, o=lnt[:, :], i=xt, r=stats[:, 6:7], b=stats[:, 7:8]:
            e.activation(out=o, in_=i, func=AF.Identity, scale=r, bias=b),
            wait=[("dve", dvr)], inc="act")
        ac2 = self.cnt["act"]
        on("dve", None, wait=[("act", ac2)], inc="dve")
        dv2 = self.cnt["dve"]
        for ft in range(8):
            self.trblk(lnt[:, ft * 128:(ft + 1) * 128], lnxT_col_fn(ft),
                       wait=[("dve", dv2)])
        return self.cnt["dve"]

    def attn_a(self, src="x0", export="a2a"):
        sb, dr, ps, on, dma = self.sb, self.dr, self.ps, self.on, self.dma
        stk = ExitStack()
        SB = lambda n, s, d: self.SB(n, s, d, stk)
        x = SB("x", [128, NT1 * D], F32)
        lnxT = SB("lnxT", [128, 8 * S], BF16)
        outT = SB("outT", [128, 8 * S], BF16)
        iota = SB("iota", [2, S], mybir.dt.uint16)
        tasc = SB("tasc", [2, S], BF16)
        tdesc = SB("tdesc", [2, S], BF16)
        mlo = SB("mlo", [2, 128], BF16)
        mup = SB("mup", [2, 128], BF16)
        diag = SB("diag", [128, H1 * 128], BF16)
        rden = SB("rden", [128, 512], BF16)
        wstg = SB("wstg", [128, D], BF16)
        wstgB = SB("wstgB", [128, D], BF16)
        wsl = [wstg, wstgB]
        wpe = [self.cnt["pe"], self.cnt["pe"]]
        wix = [0]
        wT = SB("wT", [128, D], BF16)
        bufA = SB("bufA", [128, S], BF16)
        bufK = SB("bufK", [128, S], BF16)
        bufV = SB("bufV", [128, 2112], BF16)
        pt = SB("pt", [128, 1024], BF16)
        msk = SB("msk", [128, 1024], F32)
        self.ln_sq = "msk"
        SB("l128", [128, 2], F32)
        SB("e128", [128, 128], F32)
        SB("tt2", [2, 128], F32)
        SB("m128", [128, 2], F32)
        SB("dtmp", [128, 128], F32)
        SB("dtmp2", [128, 128], F32)
        woT = SB("woT", [128, 8 * D], BF16)
        pstk = ExitStack()
        psc = self.PSm("psc", [128, 1024], F32, pstk)
        ppv = self.PSm("ppv", [128, S], F32, pstk)
        paux = self.PSm("paux", [128, 512], F32, pstk)

        on("pool", lambda e, o=iota[:, :]: e.iota(
            o, [[1, S]], channel_multiplier=0, allow_small_or_imprecise_dtypes=True),
            inc="pool")
        pl0 = self.cnt["pool"]
        on("dve", lambda e, o=bufV[:, :]: e.memset(o, 1.0))
        on("dve", lambda e, o=psc[:, 0:1024]: e.memset(o, 0.0))
        for h in range(H1):
            on("pe", lambda e, o=paux[:, 0:2], l=sb["onesb"][0:1, 0:128],
               r=sb["alogFb"][0:1, 2 * h:2 * h + 2]: e.matmul(o, l, r, start=True, stop=True),
               wait=[("Lg", self.cnt["Lg"]), ("dve", self.cnt["dve"]),
                     ("act", self.cnt["act"])], inc="pe")
            pe0 = self.cnt["pe"]
            on("dve", lambda e, o=sb["l128"][:, :], i=paux[:, 0:2]: e.tensor_copy(o, i),
               wait=[("pe", pe0)], inc="dve")
            dvl = self.cnt["dve"]
            on("pe", lambda e, o=paux[:, 0:2], l=sb["onesb"][0:1, 0:128],
               r=sb["amixFb"][0:1, 2 * h:2 * h + 2]: e.matmul(o, l, r, start=True, stop=True),
               wait=[("dve", dvl)], inc="pe")
            pe0 = self.cnt["pe"]
            on("dve", lambda e, o=sb["m128"][:, :], i=paux[:, 0:2]: e.tensor_copy(o, i),
               wait=[("pe", pe0)], inc="dve")
            dvm = self.cnt["dve"]
            for n in range(2):
                o = sb["e128"][:, :] if n == 0 else sb["dtmp2"][:, :]
                on("act", lambda e, o=o, i=sb["s_absd"][:, :], s=sb["l128"][:, n:n + 1]:
                   e.activation(out=o, in_=i, func=AF.Exp, scale=s),
                   wait=[("dve", dvl)], inc="act")
            a2 = self.cnt["act"]
            dgs = diag[:, h * 128:(h + 1) * 128]
            on("act", lambda e, o=sb["dtmp"][:, :], i=sb["dtmp2"][:, :],
               s=sb["m128"][:, 1:2]: e.activation(
                   out=o, in_=i, func=AF.Identity, scale=s),
               wait=[("dve", dvm)], inc="act")
            on("act", lambda e, o=sb["dtmp2"][:, :], i=sb["e128"][:, :],
               s=sb["m128"][:, 0:1]: e.activation(
                   out=o, in_=i, func=AF.Identity, scale=s), inc="act")
            a3 = self.cnt["act"]
            on("dve", lambda e, o=dgs, i=sb["dtmp"][:, :], j=sb["dtmp2"][:, :]:
               e.tensor_tensor(out=o, in0=i, in1=j, op=ALU.add),
               wait=[("act", a3)], inc="dve")
        diag_done = self.cnt["dve"]
        woT_done = None

        self.apb = {0: self.cnt["act"], 512: self.cnt["act"]}
        prev_b = (0, 0)
        for b in range(BC):
            impw = [("dve", prev_b[0]), ("A2", prev_b[1]), ("cc", self.cnt["cc"])]
            if src == "x0":
                src_ap = AP(dr["x0"].ap().tensor, b * S * D,
                            [[D, 128], [128 * D, NT1], [1, D]])
                dst_ap = AP(x[:, :].tensor, 0, [[NT1 * D, 128], [D, NT1], [1, D]])
                xi = dma("sp", "Ls", dst_ap, src_ap, wait=impw)
            else:
                for th in range(2):
                    src_ap = AP(dr["a2a_out"].ap().tensor, b * SL * D + th * 128 * D,
                                [[D, 128], [BC * SL * D, NCORES], [1, D]])
                    dst_ap = AP(x[:, :].tensor, th * D,
                                [[NT1 * D, 128], [2 * D, NCORES], [1, D]])
                    xi = dma("sp", "Ls", dst_ap, src_ap, wait=impw if th == 0 else ())
            for tt in range(NT1):
                self.ln_tile(x[:, tt * D:(tt + 1) * D],
                             lambda ft, tt=tt: lnxT[:, ft * S + tt * 128: ft * S + (tt + 1) * 128],
                             [("Ls", xi)])
            ln_dve = self.cnt["dve"]

            for pair in range(H1 // 2):
                r0 = pair * 128
                projs = (("aWv", "abv_t", bufA, True), ("aWq", "abq_t", bufA, False),
                         ("aWk", "abk_t", bufK, False))
                for wname, bname, tgt, is_v in projs:
                    sl = wix[0] % 2
                    wix[0] += 1
                    stg = wsl[sl]
                    wsem = ("Lw", "LwB")[sl]
                    lw = dma("pool", wsem, stg[:, :], dr[wname].ap()[r0:r0 + 128, :],
                             wait=[("pe", wpe[sl])])
                    for ec in range(8):
                        self.trblk(stg[:, ec * 128:(ec + 1) * 128],
                                   wT[:, ec * 128:(ec + 1) * 128],
                                   wait=[(wsem, lw)] if ec == 0 else ())
                    wpe[sl] = self.cnt["pe"]
                    tdone = self.cnt["dve"]
                    for qc in range(4):
                        pb = (qc % 2) * 512
                        for ec in range(8):
                            on("pe", lambda e, o=psc[:, pb:pb + 512],
                               l=wT[:, ec * 128:(ec + 1) * 128],
                               r=lnxT[:, ec * S + qc * 512: ec * S + qc * 512 + 512],
                               s=(ec == 0), t=(ec == 7): e.matmul(o, l, r, start=s, stop=t),
                               wait=[("dve", tdone), ("dve", ln_dve),
                                     ("act", self.apb[pb])] if ec == 0 else (),
                               inc="pe")
                        pe0 = self.cnt["pe"]
                        on("act", lambda e, o=tgt[:, qc * 512:(qc + 1) * 512],
                           i=psc[:, pb:pb + 512], bi=sb[bname][:, pair:pair + 1]:
                           e.activation(out=o, in_=i, func=AF.Identity, bias=bi),
                           wait=[("pe", pe0)], inc="act")
                        self.apb[pb] = self.cnt["act"]
                    on("pe", None, wait=[("act", self.cnt["act"])])
                    if is_v:
                        acv = self.cnt["act"]
                        tr = ps["tr"]
                        for kt in range(NT1):
                            on("pe", lambda e, s=bufA[:, kt * 128:(kt + 1) * 128],
                               t=tr[:, 0:128], y=sb["s_eye"][:, :]: e.transpose(t, s, y),
                               wait=([("act", acv)] if kt == 0 else []) +
                                    [("dve", self.tr_guard)], inc="pe")
                            pe0 = self.cnt["pe"]
                            on("dve", lambda e, o=bufV[:, kt * 129: kt * 129 + 64],
                               t=tr[:, 0:64]: e.tensor_copy(o, t),
                               wait=[("pe", pe0)], inc="dve")
                            on("dve", lambda e, o=bufV[:, kt * 129 + 65: kt * 129 + 129],
                               t=tr[:, 64:128]: e.tensor_copy(o, t), inc="dve")
                            self.tr_guard = self.cnt["dve"]
                proj_act = self.cnt["act"]

                for hi in range(2):
                    h = 2 * pair + hi
                    ho = 64 * hi
                    pe_now = self.cnt["pe"]
                    on("act", lambda e, o=tasc[:, :], i=iota[:, :],
                       s=sb["alogT"][:, h:h + 1]: e.activation(
                           out=o, in_=i, func=AF.Exp, scale=s),
                       wait=[("pool", pl0), ("pe", pe_now)], inc="act")
                    on("act", lambda e, o=tdesc[:, :], i=iota[:, :],
                       s=sb["alogTn"][:, h:h + 1], bi=sb["alogT2047"][:, h:h + 1]:
                       e.activation(out=o, in_=i, func=AF.Exp, scale=s, bias=bi),
                       inc="act")
                    on("act", lambda e, o=sb["tt2"][:, :], i=iota[:, 0:128],
                       s=sb["alogTn"][:, h:h + 1], bi=sb["alogT127"][:, h:h + 1]:
                       e.activation(out=o, in_=i, func=AF.Exp, scale=s, bias=bi),
                       inc="act")
                    on("act", lambda e, o=mlo[:, :], i=sb["tt2"][:, :],
                       s=sb["amixT"][:, h:h + 1]: e.activation(
                           out=o, in_=i, func=AF.Identity, scale=s), inc="act")
                    on("act", lambda e, o=sb["tt2"][:, :], i=iota[:, 0:128],
                       s=sb["alogT"][:, h:h + 1]: e.activation(
                           out=o, in_=i, func=AF.Exp, scale=s), inc="act")
                    on("act", lambda e, o=mup[:, :], i=sb["tt2"][:, :],
                       s=sb["amixT"][:, h:h + 1]: e.activation(
                           out=o, in_=i, func=AF.Identity, scale=s), inc="act")
                    tab_act = self.cnt["act"]
                    on("dve", None, wait=[("act", tab_act)], inc="dve")
                    tab_done = self.cnt["dve"]

                    mult_done = 0
                    mult_hist = [0, 0]
                    exp_done = proj_act
                    exp_hist = [proj_act, proj_act]
                    mevac_done = 0
                    mevac_hist = [proj_act, proj_act]
                    ptf = {0: 0, 1: 0}
                    ft = h // 2
                    for qc in range(4):
                        q0c, q1c = qc * 512, qc * 512 + 512
                        pend = None
                        for kt in range(NT1):
                            k0 = kt * 128
                            mb = kt % 2
                            bm = psc[:, mb * 512:(mb + 1) * 512]
                            bs = psc[:, (1 - mb) * 512:(2 - mb) * 512]
                            scrb = sb["scr"][:, :] if mb == 0 else sb["scr2"][:, :]
                            msl = msk[:, mb * 512:(mb + 1) * 512]
                            mwait = [("dve", tab_done), ("act", mevac_hist[mb]),
                                     ("dve", mult_hist[mb])]
                            ju = min(k0, q1c)
                            if ju > q0c:
                                ln_ = ju - q0c
                                on("pe", lambda e, o=bm[:, 0:ln_], l=mup[:, :],
                                   r=tdesc[:, q0c + 2047 - k0: q0c + 2047 - k0 + ln_]:
                                   e.matmul(o, l, r, start=True, stop=True),
                                   wait=mwait, inc="pe")
                            jl = max(k0 + 128, q0c)
                            if q1c > jl:
                                ln_ = q1c - jl
                                off = jl - q0c
                                on("pe", lambda e, o=bm[:, off:off + ln_], l=mlo[:, :],
                                   r=tasc[:, jl - k0 - 127: jl - k0 - 127 + ln_]:
                                   e.matmul(o, l, r, start=True, stop=True),
                                   wait=mwait, inc="pe")
                            pe_m = self.cnt["pe"]
                            on("pe", lambda e, o=bs,
                               l=bufK[ho:ho + 64, kt * 128:(kt + 1) * 128],
                               r=bufA[ho:ho + 64, qc * 512:(qc + 1) * 512]:
                               e.matmul(o, l, r, start=True, stop=True),
                               wait=[("act", mevac_hist[1 - mb]),
                                     ("dve", mult_hist[1 - mb])], inc="pe")
                            pe_s = self.cnt["pe"]
                            if pend is not None:
                                pend()
                            on("act", lambda e, o=msl, i=bm: e.activation(
                                out=o, in_=i, func=AF.Copy),
                                wait=[("pe", pe_m), ("dve", mult_hist[mb])], inc="act")
                            mevac_done = mevac_hist[mb] = self.cnt["act"]
                            dg_done = 0
                            if q0c <= k0 < q1c:
                                off = k0 - q0c
                                on("dve", lambda e, o=msl[:, off:off + 128],
                                   i=diag[:, h * 128:(h + 1) * 128]: e.tensor_copy(o, i),
                                   wait=[("act", mevac_done), ("dve", diag_done)],
                                   inc="dve")
                                dg_done = self.cnt["dve"]
                            on("dve", lambda e, o=scrb, m=msl,
                               i=bs: e.tensor_tensor(
                                out=o, in0=m, in1=i, op=ALU.mult),
                                wait=[("pe", pe_s), ("act", mevac_done),
                                      ("dve", dg_done), ("act", exp_hist[mb])],
                                inc="dve")
                            mult_done = mult_hist[1 - mb] = self.cnt["dve"]
                            pb = kt % 2
                            on("act", lambda e, o=pt[:, pb * 512:(pb + 1) * 512],
                               i=scrb: e.activation(
                                   out=o, in_=i, func=AF.Exp, scale=0.125),
                                wait=[("dve", mult_done), ("pe", ptf[pb])], inc="act")
                            exp_done = exp_hist[mb] = self.cnt["act"]

                            def mk_pend(kt=kt, pb=pb, exp_done=exp_done, qc=qc,
                                        ho=ho, hi=hi):
                                def pend_():
                                    if hi == 0:
                                        on("pe", lambda e,
                                           o=ppv[0:65, qc * 512:(qc + 1) * 512],
                                           l=bufV[:, kt * 129: kt * 129 + 65],
                                           r=pt[:, pb * 512:(pb + 1) * 512],
                                           s=(kt == 0), t=(kt == NT1 - 1):
                                           e.matmul(o, l, r, start=s, stop=t),
                                           wait=[("act", exp_done)], inc="pe")
                                    else:
                                        on("pe", lambda e, o=paux[0:1, 0:512],
                                           l=sb["onesc"][:, 0:1],
                                           r=pt[:, pb * 512:(pb + 1) * 512],
                                           s=(kt == 0), t=(kt == NT1 - 1):
                                           e.matmul(o, l, r, start=s, stop=t),
                                           wait=[("act", exp_done)], inc="pe")
                                        on("pe", lambda e,
                                           o=ppv[64:128, qc * 512:(qc + 1) * 512],
                                           l=bufV[:, kt * 129 + 65: kt * 129 + 129],
                                           r=pt[:, pb * 512:(pb + 1) * 512],
                                           s=(kt == 0), t=(kt == NT1 - 1), tp=(0, 64):
                                           e.matmul(o, l, r, start=s, stop=t,
                                                    tile_position=tp), inc="pe")
                                    ptf[pb] = self.cnt["pe"]
                                return pend_
                            pend = mk_pend()
                        pend()
                        pend = None
                        pv_done = self.cnt["pe"]
                        drow = 64 if hi == 0 else 0
                        dsrc = (ppv[64:65, qc * 512:(qc + 1) * 512] if hi == 0
                                else paux[0:1, 0:512])
                        on("dve", lambda e, o=rden[drow:drow + 1, 0:512], i=dsrc:
                           e.reciprocal(out=o, in_=i), wait=[("pe", pv_done)], inc="dve")
                        dv0 = self.cnt["dve"]
                        on("pe", lambda e, o=paux[:, 0:512],
                           l=sb["onesb"][drow:drow + 1, 0:128],
                           r=rden[drow:drow + 1, 0:512]:
                           e.matmul(o, l, r, start=True, stop=True),
                           wait=[("dve", dv0), ("act", mevac_done)], inc="pe")
                        pe0 = self.cnt["pe"]
                        on("dve", lambda e, o=sb["repsb"][:, 0:512], i=paux[:, 0:512]:
                           e.tensor_copy(o, i), wait=[("pe", pe0)], inc="dve")
                        dv1 = self.cnt["dve"]
                        on("dve", lambda e,
                           o=outT[ho:ho + 64, ft * S + qc * 512: ft * S + (qc + 1) * 512],
                           i=ppv[ho:ho + 64, qc * 512:(qc + 1) * 512],
                           r=sb["repsb"][ho:ho + 64, 0:512]: e.tensor_tensor(
                               out=o, in0=i, in1=r, op=ALU.mult),
                           wait=[("dve", dv1)], inc="dve")
                        on("pe", None, wait=[("dve", self.cnt["dve"])])
                    on("act", None, wait=[("dve", self.cnt["dve"])])
                    if self.dev.get("dbg3") and pair == self.dev.get("dbg3_pair", 0) and hi == 1:
                        self.dma("pool", "A2", dr["t_outT_all"].ap(), outT[:, :],
                                 wait=[("dve", self.cnt["dve"])])
                        if "t_rden2" in dr:
                            self.dma("pool", "A2", dr["t_rden2"].ap(), rden[:, :],
                                     wait=[("dve", self.cnt["dve"])])
                        if "t_repsb" in dr:
                            self.dma("pool", "A2", dr["t_repsb"].ap(), sb["repsb"][:, :],
                                     wait=[("dve", self.cnt["dve"])])
                        self.on("sp", None, wait=[("A2", self.cnt["A2"])])
                        self.barrier()
                        pstk.close()
                        stk.close()
                        return
                    if self.dev.get("dbg1"):
                        dv = self.cnt["dve"]
                        for nm, apx in (("t_tasc", tasc[:, :]), ("t_tdesc", tdesc[:, :]),
                                        ("t_mlo", mlo[:, :]), ("t_mup", mup[:, :]),
                                        ("t_msk", msk[:, :]), ("t_pt", pt[:, :]),
                                        ("t_rden", rden[:, :]),
                                        ("t_outT", outT[:, 0:S]),
                                        ("t_lnxT", lnxT[:, 0:S]),
                                        ("t_bufA", bufA[:, :]), ("t_bufK", bufK[:, :]),
                                        ("t_bufV", bufV[:, :]),
                                        ("t_diag", diag[:, 0:256])):
                            if nm in dr:
                                self.dma("pool", "A2", dr[nm].ap(), apx,
                                         wait=[("dve", dv)])
                        self.on("sp", None, wait=[("A2", self.cnt["A2"])])
                        self.barrier()
                        pstk.close()
                        stk.close()
                        return
            heads_dve = self.cnt["dve"]
            if self.dev.get("dbg2"):
                self.dma("pool", "A2", dr["t_outT_all"].ap(), outT[:, :],
                         wait=[("dve", heads_dve)])
                self.dma("pool", "A2", dr["t_rden2"].ap(), rden[:, :],
                         wait=[("dve", heads_dve)])
                self.on("sp", None, wait=[("A2", self.cnt["A2"])])
                self.barrier()
                pstk.close()
                stk.close()
                return

            if woT_done is None:
                for fot in range(8):
                    sl = wix[0] % 2
                    wix[0] += 1
                    stg = wsl[sl]
                    wsem = ("Lw", "LwB")[sl]
                    lw = dma("pool", wsem, stg[:, :],
                             dr["aWo"].ap()[fot * 128:(fot + 1) * 128, :],
                             wait=[("pe", wpe[sl])])
                    for ec in range(8):
                        self.trblk(stg[:, ec * 128:(ec + 1) * 128],
                                   woT[:, ec * D + fot * 128: ec * D + fot * 128 + 128],
                                   wait=[(wsem, lw)] if ec == 0 else ())
                    wpe[sl] = self.cnt["pe"]
                woT_done = self.cnt["dve"]
            for st2 in range(NT1):
                for foc in range(2):
                    for fic in range(8):
                        on("pe", lambda e, o=psc[:, 0:512],
                           l=outT[:, fic * S + st2 * 128: fic * S + st2 * 128 + 128],
                           r=woT[:, fic * D + foc * 512: fic * D + foc * 512 + 512],
                           s=(fic == 0): e.matmul(o, l, r, start=s, stop=False),
                           wait=[("dve", woT_done), ("dve", heads_dve)] if fic == 0 else (),
                           inc="pe")
                    on("pe", lambda e, o=psc[:, 0:512], l=sb["onesb"][0:1, 0:128],
                       r=sb["abo_bf"][0:1, foc * 512:(foc + 1) * 512]:
                       e.matmul(o, l, r, start=False, stop=True, tile_position=(0, 0)),
                       inc="pe")
                    pe0 = self.cnt["pe"]
                    on("dve", lambda e,
                       o=x[:, st2 * D + foc * 512: st2 * D + foc * 512 + 512],
                       i=psc[:, 0:512]: e.tensor_tensor(out=o, in0=o, in1=i, op=ALU.add),
                       wait=[("pe", pe0)], inc="dve")
                    on("pe", None, wait=[("dve", self.cnt["dve"])])
            res_done = self.cnt["dve"]
            if export == "a2a":
                for th in range(2):
                    eap = AP(dr["a2a_in"].ap().tensor, b * SL * D + th * 128 * D,
                             [[D, 128], [BC * SL * D, NCORES], [1, D]])
                    sap = AP(x[:, :].tensor, th * D,
                             [[NT1 * D, 128], [2 * D, NCORES], [1, D]])
                    ex = self.dma("pool", "A2", eap, sap, wait=[("dve", res_done)])
            else:
                eap = AP(dr[export].ap().tensor, b * S * D, [[D, 128], [128 * D, NT1], [1, D]])
                sap = AP(x[:, :].tensor, 0, [[NT1 * D, 128], [D, NT1], [1, D]])
                ex = self.dma("pool", "A2", eap, sap, wait=[("dve", res_done)])
            prev_b = (res_done, ex)
        self.on("sp", None, wait=[("A2", self.cnt["A2"])])
        self.barrier()
        pstk.close()
        stk.close()

    def a2a(self):
        cw = [("A2", self.cnt["A2"]), ("St0", 1), ("St1", 1)]
        ins = [self.dr["a2a_in"].ap().opt()]
        outs = [self.dr["a2a_out"].ap().opt()]

        def fn(e, ins=ins, outs=outs):
            with allow_ici_all_to_all():
                return e.collective_compute(
                    "AllToAll", ALU.bypass, replica_groups=[list(range(NCORES))],
                    ins=ins, outs=outs)
        self.on("pool", fn, wait=cw, inc="cc")

    def ffn(self, dst="a2a_in", src="a2a_out"):
        sb, dr, ps, on, dma = self.sb, self.dr, self.ps, self.on, self.dma
        stk = ExitStack()
        SB = lambda n, s, d: self.SB(n, s, d, stk)
        uniq = "f%d" % self.cnt["cc"]
        NTOK = NT2 * 128
        lnxT = SB("lnxT" + uniq, [128, 8 * NTOK], BF16)
        cst = SB("c" + uniq, [128, 8 * SL], F32)
        cbf = SB("cbf" + uniq, [128, 8 * SL], BF16)
        xt = [SB("xt0" + uniq, [128, D], F32), SB("xt1" + uniq, [128, D], F32)]
        gf = SB("gf" + uniq, [128, SL], F32)
        gi = SB("gi" + uniq, [128, SL], F32)
        gc = SB("gc" + uniq, [128, SL], F32)
        gtmp = SB("gtmp" + uniq, [128, SL], F32)
        wstg2 = SB("wstg2" + uniq, [128, D], BF16)
        wstg2B = SB("wstg2B" + uniq, [128, D], BF16)
        wsl2 = [wstg2, wstg2B]
        wpe2 = [self.cnt["pe"], self.cnt["pe"]]
        dve_ent = self.cnt["dve"]
        wix2 = [0]
        pstk = ExitStack()
        pg = {"f": self.PSm("pgf" + uniq, [128, 512], F32, pstk),
              "i": self.PSm("pgi" + uniq, [128, 512], F32, pstk),
              "c": self.PSm("pgc" + uniq, [128, 512], F32, pstk)}
        py = self.PSm("py" + uniq, [128, 512], F32, pstk)

        def tile_ap(dram, tt):
            return AP(dram.ap().tensor,
                      (tt // 4) * (BC * SL * D) + ((tt // 2) % 2) * (SL * D)
                      + (tt % 2) * 128 * D,
                      [[D, 128], [1, D]])

        ccw = [("cc", self.cnt["cc"])]
        wstk = ExitStack()
        mats = {}
        for nm in ("fwf", "fuf", "fwi", "fui", "fwc", "fuc"):
            mats[nm] = self.SB(nm + uniq, [128, 8 * D], BF16, wstk)
            for fot in range(8):
                sl = wix2[0] % 2
                wsem = ("Lw", "LwB")[sl]
                stg = wsl2[sl]
                lw = dma("pool", wsem, stg[:, :],
                         dr[nm].ap()[fot * 128:(fot + 1) * 128, :],
                         wait=[("pe", wpe2[sl])] +
                              ([("dve", dve_ent)] if wix2[0] < 2 else []))
                wix2[0] += 1
                for ec in range(8):
                    self.trblk(stg[:, ec * 128:(ec + 1) * 128],
                               mats[nm][:, ec * D + fot * 128: ec * D + fot * 128 + 128],
                               wait=[(wsem, lw)] if ec == 0 else ())
                wpe2[sl] = self.cnt["pe"]
        mats_done = self.cnt["dve"]
        xld = []
        for tt in range(NT2):
            xb = xt[tt % 2]
            w = list(ccw)
            if tt >= 2:
                w.append(("dve", xld[tt - 2]))
            ld = dma("sp", "Lx%d" % (tt % 2), xb[:, :], tile_ap(dr[src], tt), wait=w)
            self.ln_sq = "c" + uniq
            dv = self.ln_tile(xb[:, :],
                              lambda ft, tt=tt: lnxT[:, ft * NTOK + tt * 128: ft * NTOK + (tt + 1) * 128],
                              [("Lx%d" % (tt % 2), ld)])
            xld.append(dv)
        ln_dve = self.cnt["dve"]
        for fot in range(8):
            on("act", lambda e, o=cst[:, fot * SL:(fot + 1) * SL],
               i=sb["onesb"][:, 0:SL], b=sb["finc_t"][:, fot:fot + 1]: e.activation(
                   out=o, in_=i, func=AF.Identity, scale=0.0, bias=b),
               wait=[("dve", ln_dve)] if fot == 0 else (), inc="act")
            ac9 = self.cnt["act"]
            on("dve", lambda e, o=cbf[:, fot * SL:(fot + 1) * SL],
               i=cst[:, fot * SL:(fot + 1) * SL]: e.tensor_copy(o, i),
               wait=[("act", ac9)], inc="dve")
        cbf_done = self.cnt["dve"]
        gates = (("fwf", "fuf", "f", gf, AF.Sigmoid, "fbf_t"),
                 ("fwi", "fui", "i", gi, AF.Sigmoid, "fbi_t"),
                 ("fwc", "fuc", "c", gc, AF.Tanh, "fbc_t"))
        for t in range(TS):
            gate_ev = getattr(self, "_gate_ev", None)
            if gate_ev is None or t == 0:
                gate_ev = self._gate_ev = {"f": 0, "i": 0, "c": 0}
            for fot in range(8):
                for wn, un, pn, gb, fn_, bn in gates:
                    pgn = pg[pn]
                    for fic in range(8):
                        on("pe", lambda e, o=pgn[:, 0:SL],
                           l=mats[wn][:, fic * D + fot * 128: fic * D + fot * 128 + 128],
                           r=lnxT[:, fic * NTOK + t * SL: fic * NTOK + (t + 1) * SL],
                           s=(fic == 0): e.matmul(o, l, r, start=s, stop=False),
                           wait=[("dve", mats_done), ("dve", ln_dve),
                                 ("dve", cbf_done), ("act", gate_ev[pn])]
                           if fic == 0 else (), inc="pe")
                    for fic in range(8):
                        on("pe", lambda e, o=pgn[:, 0:SL],
                           l=mats[un][:, fic * D + fot * 128: fic * D + fot * 128 + 128],
                           r=cbf[:, fic * SL:(fic + 1) * SL],
                           t_=(fic == 7): e.matmul(o, l, r, start=False, stop=t_),
                           inc="pe")
                    pe0 = self.cnt["pe"]
                    on("act", lambda e, o=gb[:, :], i=pgn[:, 0:SL], fn_=fn_,
                       bi=sb[bn][:, fot:fot + 1]: e.activation(
                           out=o, in_=i, func=fn_, bias=bi),
                       wait=[("pe", pe0), ("dve", self.cnt["dve"])], inc="act")
                    gate_ev[pn] = self.cnt["act"]
                ev = self.cnt["act"]
                cs = cst[:, fot * SL:(fot + 1) * SL]
                on("dve", lambda e, o=gtmp[:, :], i=gi[:, :], j=gc[:, :]:
                   e.tensor_tensor(out=o, in0=i, in1=j, op=ALU.mult),
                   wait=[("act", ev)], inc="dve")
                on("dve", lambda e, o=cs, i=gf[:, :]: e.tensor_tensor(
                    out=o, in0=o, in1=i, op=ALU.mult), inc="dve")
                on("dve", lambda e, o=cs, i=gtmp[:, :]: e.tensor_tensor(
                    out=o, in0=o, in1=i, op=ALU.add), inc="dve")
            step_pe = self.cnt["pe"]
            for fot in range(8):
                on("dve", lambda e, o=cbf[:, fot * SL:(fot + 1) * SL],
                   i=cst[:, fot * SL:(fot + 1) * SL]: e.tensor_copy(o, i),
                   wait=[("pe", step_pe)] if fot == 0 else (), inc="dve")
            cbf_done = self.cnt["dve"]
        scan_pe = self.cnt["pe"]
        wstk.close()
        fstk = ExitStack()
        woT = self.SB("fwoT" + uniq, [128, 8 * D], BF16, fstk)
        uoT = self.SB("fuoT" + uniq, [128, 8 * D], BF16, fstk)
        cpart = self.SB("cpart" + uniq, [128, 2 * D], F32, fstk)
        for nm, tgt in (("fwo", woT), ("fuo", uoT)):
            for fot in range(8):
                sl = wix2[0] % 2
                wsem = ("Lw", "LwB")[sl]
                stg = wsl2[sl]
                lw = dma("pool", wsem, stg[:, :],
                         dr[nm].ap()[fot * 128:(fot + 1) * 128, :],
                         wait=[("pe", wpe2[sl])])
                wix2[0] += 1
                for ec in range(8):
                    self.trblk(stg[:, ec * 128:(ec + 1) * 128],
                               tgt[:, ec * D + fot * 128: ec * D + fot * 128 + 128],
                               wait=[(wsem, lw)] if ec == 0 else ())
                wpe2[sl] = self.cnt["pe"]
        wo_done = self.cnt["dve"]
        if self.dev.get("dbgf") == self.cnt["cc"]:
            self.dma("pool", "A2", dr["t_flnxT"].ap(), lnxT[:, 0:NTOK],
                     wait=[("dve", wo_done)])
            self.dma("pool", "A2", dr["t_fcbf"].ap(), cbf[:, :],
                     wait=[("dve", wo_done)])
        for stl in range(2):
            for foc in range(2):
                for fmc in range(8):
                    on("pe", lambda e, o=py[:, 0:512],
                       l=cbf[:, fmc * SL + stl * 128: fmc * SL + stl * 128 + 128],
                       r=uoT[:, fmc * D + foc * 512: fmc * D + foc * 512 + 512],
                       s=(fmc == 0), t_=(fmc == 7): e.matmul(o, l, r, start=s, stop=t_),
                       wait=[("dve", wo_done), ("dve", cbf_done),
                             ("act", self.cnt["act"])] if fmc == 0 else (),
                       inc="pe")
                pe0 = self.cnt["pe"]
                on("act", lambda e, o=cpart[:, stl * D + foc * 512: stl * D + foc * 512 + 512],
                   i=py[:, 0:512]: e.activation(out=o, in_=i, func=AF.Copy),
                   wait=[("pe", pe0)], inc="act")
        cpart_done = self.cnt["act"]
        pass
        xold = {}
        for tt in range(2):
            xold[tt] = dma("sp", "Lx%d" % (tt % 2), xt[tt % 2][:, :], tile_ap(dr[src], tt),
                           wait=[("dve", ln_dve)])
        for tt in range(NT2):
            xb = xt[tt % 2]
            stl = tt % 2
            for foc in range(2):
                for fic in range(8):
                    on("pe", lambda e, o=py[:, 0:512],
                       l=lnxT[:, fic * NTOK + tt * 128: fic * NTOK + tt * 128 + 128],
                       r=woT[:, fic * D + foc * 512: fic * D + foc * 512 + 512],
                       s=(fic == 0): e.matmul(o, l, r, start=s, stop=False),
                       wait=[("act", cpart_done)] if fic == 0 else (), inc="pe")
                on("pe", lambda e, o=py[:, 0:512], l=sb["onesb"][0:1, 0:128],
                   r=sb["fbo_bf"][0:1, foc * 512:(foc + 1) * 512]:
                   e.matmul(o, l, r, start=False, stop=True, tile_position=(0, 0)),
                   inc="pe")
                pe0 = self.cnt["pe"]
                on("dve", lambda e, o=sb["scr"][:, :], j=py[:, 0:512],
                   i=cpart[:, stl * D + foc * 512: stl * D + foc * 512 + 512]:
                   e.tensor_tensor(out=o, in0=i, in1=j, op=ALU.add),
                   wait=[("pe", pe0), ("act", cpart_done),
                         ("act", self.cnt["act"])], inc="dve")
                dv0 = self.cnt["dve"]
                on("act", lambda e, o=sb["scr"][:, :]: e.activation(
                    out=o, in_=o, func=AF.Tanh), wait=[("dve", dv0)], inc="act")
                ac0 = self.cnt["act"]
                on("dve", lambda e, o=xb[:, foc * 512:(foc + 1) * 512],
                   i=sb["scr"][:, :]: e.tensor_tensor(out=o, in0=o, in1=i, op=ALU.add),
                   wait=[("act", ac0), ("Lx%d" % (tt % 2), xold[tt])], inc="dve")
                on("pe", None, wait=[("dve", self.cnt["dve"])])
            res_done = self.cnt["dve"]
            stsem = "St%d" % (tt % 2)
            if dst == "a2a_in":
                st_ap = tile_ap(dr["a2a_in"], tt)
            else:
                st_ap = dr["y"].ap()[tt * 128:(tt + 1) * 128, :]
            self.dma("pool", stsem, st_ap, xb[:, :], wait=[("dve", res_done)])
            nxt = tt + 2
            if nxt < NT2:
                xold[nxt] = dma("sp", "Lx%d" % (nxt % 2), xb[:, :], tile_ap(dr[src], nxt),
                                wait=[(stsem, 1)])
        self.on("sp", None, wait=[("St0", 1), ("St1", 1)])
        self.barrier()
        pstk.close()
        fstk.close()
        stk.close()

    def attn_b(self, src="a2a_out", export="a2a"):
        sb, dr, ps, on, dma = self.sb, self.dr, self.ps, self.on, self.dma
        stk = ExitStack()
        SB = lambda n, s, d: self.SB(n, s, d, stk)
        x = SB("bx", [128, NT1 * D], F32)
        lnxT = SB("blnxT", [128, 8 * S], BF16)
        outT = SB("boutT", [128, 8 * S], BF16)
        relkT = SB("relkT", [128, NPOS + 1], BF16)
        qreb = SB("qreb", [128, S], BF16)
        qrpb = SB("qrpb", [128, S], BF16)
        kT = SB("bkT", [128, S], BF16)
        vv = SB("bvv", [128, S], BF16)
        winsb = SB("winsb", [128, WIN], BF16)
        bandT = SB("bandT", [128, 1024], BF16)
        pt = SB("bpt", [128, 1024], BF16)
        rden = SB("brden", [128, 512], BF16)
        wstg = SB("bwstg", [128, D], BF16)
        wstgD = SB("bwstgD", [128, D], BF16)
        wslb = [wstg, wstgD]
        wpeb = [self.cnt["pe"], self.cnt["pe"]]
        wixb = [0]
        wrelstg = SB("wrelstg", [128, 256], BF16)
        pstg0 = SB("pstg0", [128, 1024], BF16)
        pstg1 = SB("pstg1", [64, 1024], BF16)
        woutb = SB("woutb", [128, 8 * D], BF16)
        band32 = SB("band32", [128, 512], F32)
        pstk = ExitStack()
        psc = self.PSm("bpsc", [128, 1024], F32, pstk)
        pwin = self.PSm("pwin", [128, 512], F32, pstk)
        ppv = self.PSm("bppv", [128, S], F32, pstk)
        wout_ld = None

        prev_b = (0, 0)
        for b in range(BC):
            impw = [("dve", prev_b[0]), ("A2", prev_b[1]), ("cc", self.cnt["cc"])]
            if src == "a2a_out":
                for th in range(2):
                    src_ap = AP(dr["a2a_out"].ap().tensor, b * SL * D + th * 128 * D,
                                [[D, 128], [BC * SL * D, NCORES], [1, D]])
                    dst_ap = AP(x[:, :].tensor, th * D,
                                [[NT1 * D, 128], [2 * D, NCORES], [1, D]])
                    xi = dma("sp", "Ls", dst_ap, src_ap, wait=impw if th == 0 else ())
            else:
                src_ap = AP(dr[src].ap().tensor, b * S * D,
                            [[D, 128], [128 * D, NT1], [1, D]])
                dst_ap = AP(x[:, :].tensor, 0, [[NT1 * D, 128], [D, NT1], [1, D]])
                xi = dma("sp", "Ls", dst_ap, src_ap, wait=impw)
            for tt in range(NT1):
                self.ln_sq = "scr"
                self.ln_tile(x[:, tt * D:(tt + 1) * D],
                             lambda ft, tt=tt: lnxT[:, ft * S + tt * 128: ft * S + (tt + 1) * 128],
                             [("Ls", xi)],
                             xt_halves=(x[:, tt * D:tt * D + 512],
                                        x[:, tt * D + 512:(tt + 1) * D]))
            ln_dve = self.cnt["dve"]
            if self.dev.get("dbg8"):
                self.dma("pool", "A2", dr["t_vv"].ap(), lnxT[:, 0:S],
                         wait=[("dve", ln_dve)])
                if "t_x" in dr:
                    self.dma("pool", "A2", dr["t_x"].ap(), x[:, :],
                             wait=[("dve", ln_dve)])
                self.on("sp", None, wait=[("A2", self.cnt["A2"])])
                self.barrier()
                pstk.close()
                stk.close()
                return

            for h in range(H2):
                hc = h * 128
                for wname in ("bwq", "bwk", "bwv"):
                    sl = wixb[0] % 2
                    wixb[0] += 1
                    wsem = ("Lw", "LwB")[sl]
                    stgb = wslb[sl]
                    lw = dma("pool", wsem,
                             AP(stgb[:, :].tensor, 0, [[D, 128], [128, 8], [1, 128]]),
                             AP(dr[wname].ap().tensor, hc, [[D, 128], [128 * D, 8], [1, 128]]),
                             wait=[("pe", wpeb[sl])])
                    for qc in range(4):
                        for ec in range(8):
                            on("pe", lambda e, o=psc[:, 0:512],
                               l=stgb[:, ec * 128:(ec + 1) * 128],
                               r=lnxT[:, ec * S + qc * 512: ec * S + qc * 512 + 512],
                               s=(ec == 0), t=(ec == 7)
                               : e.matmul(o, l, r, start=s, stop=t),
                               wait=[(wsem, lw), ("dve", ln_dve)] if ec == 0 else (),
                               inc="pe")
                        pe0 = self.cnt["pe"]
                        if wname == "bwq":
                            on("act", lambda e, o=qreb[:, qc * 512:(qc + 1) * 512],
                               i=psc[:, 0:512], bi=sb["breb_t"][:, h:h + 1]:
                               e.activation(out=o, in_=i, func=AF.Identity, scale=SC2, bias=bi),
                               wait=[("pe", pe0)], inc="act")
                            on("act", lambda e, o=qrpb[:, qc * 512:(qc + 1) * 512],
                               i=psc[:, 0:512], bi=sb["brpb_t"][:, h:h + 1]:
                               e.activation(out=o, in_=i, func=AF.Identity, scale=SC2, bias=bi),
                               inc="act")
                        elif wname == "bwk":
                            on("act", lambda e, o=kT[:, qc * 512:(qc + 1) * 512],
                               i=psc[:, 0:512]: e.activation(out=o, in_=i, func=AF.Copy),
                               wait=[("pe", pe0)], inc="act")
                        else:
                            pbh = 0 if qc % 2 == 0 else 512
                            on("act", lambda e, o=pt[:, pbh:pbh + 512],
                               i=psc[:, 0:512]: e.activation(out=o, in_=i, func=AF.Copy),
                               wait=[("pe", pe0), ("dve", self.cnt["dve"])], inc="act")
                        on("pe", None, wait=[("act", self.cnt["act"])])
                        if wname == "bwv":
                            pbh = 0 if qc % 2 == 0 else 512
                            acv = self.cnt["act"]
                            for blk in range(4):
                                self.trblk(pt[:, pbh + blk * 128: pbh + (blk + 1) * 128],
                                           vv[:, (qc * 4 + blk) * 128: (qc * 4 + blk + 1) * 128],
                                           wait=[("act", acv)] if blk == 0 else ())
                    wpeb[sl] = self.cnt["pe"]
                vv_done = self.cnt["dve"]
                if self.dev.get("dbg6"):
                    self.dma("pool", "A2", dr["t_qreb"].ap(), qreb[:, :],
                             wait=[("dve", vv_done), ("act", self.cnt["act"])])
                    self.dma("pool", "A2", dr["t_vv"].ap(), vv[:, :],
                             wait=[("dve", vv_done)])
                    self.on("sp", None, wait=[("A2", self.cnt["A2"])])
                    self.barrier()
                    pstk.close()
                    stk.close()
                    return
                lw1 = dma("pool", "Lr", wrelstg[:, 0:128],
                          AP(dr["bwrel"].ap().tensor, hc, [[D, 128], [1, 128]]),
                          wait=[("pe", self.cnt["pe"])])
                lw2 = dma("pool", "Lr", wrelstg[0:64, 128:256],
                          AP(dr["bwrel"].ap().tensor, 128 * D + hc, [[D, 64], [1, 128]]))
                relk_ac = 0
                for c in range(8):
                    cn = min(512, NPOS - c * 512)
                    pb = c % 2
                    l1 = dma("pool", "Lp%d" % pb, pstg0[:, pb * 512: pb * 512 + cn],
                             AP(dr["posT"].ap().tensor, c * 512, [[NPOS, 128], [1, cn]]),
                             wait=[("act", relk_ac)])
                    l2 = dma("pool", "Lp%d" % pb, pstg1[0:64, pb * 512: pb * 512 + cn],
                             AP(dr["posT"].ap().tensor, 128 * NPOS + c * 512,
                                [[NPOS, 64], [1, cn]]))
                    on("pe", lambda e, o=psc[:, 0:cn], l=wrelstg[:, 0:128],
                       r=pstg0[:, pb * 512: pb * 512 + cn]: e.matmul(
                           o, l, r, start=True, stop=False),
                       wait=[("Lp%d" % pb, l2), ("Lr", lw2), ("act", relk_ac)], inc="pe")
                    on("pe", lambda e, o=psc[:, 0:cn], l=wrelstg[0:64, 128:256],
                       r=pstg1[0:64, pb * 512: pb * 512 + cn]: e.matmul(
                           o, l, r, start=False, stop=True), inc="pe")
                    pe0 = self.cnt["pe"]
                    on("act", lambda e, o=relkT[:, c * 512: c * 512 + cn],
                       i=psc[:, 0:cn]: e.activation(out=o, in_=i, func=AF.Copy),
                       wait=[("pe", pe0)], inc="act")
                    relk_ac = self.cnt["act"]
                if self.dev.get("dbg7"):
                    self.dma("pool", "A2", dr["t_relkT"].ap(), relkT[:, :],
                             wait=[("act", relk_ac)])
                    self.on("sp", None, wait=[("A2", self.cnt["A2"])])
                    self.barrier()
                    pstk.close()
                    stk.close()
                    return
                wev = relk_ac
                win_dve = self.cnt["dve"]
                for qt in range(NT1):
                    u0 = 1920 - qt * 128
                    for uc in range(5):
                        cn = min(512, WIN - uc * 512)
                        on("pe", lambda e, o=pwin[:, 0:cn],
                           l=qrpb[:, qt * 128:(qt + 1) * 128],
                           r=relkT[:, u0 + uc * 512: u0 + uc * 512 + cn]:
                           e.matmul(o, l, r, start=True, stop=True),
                           wait=[("act", wev), ("dve", win_dve)], inc="pe")
                        pe0 = self.cnt["pe"]
                        on("act", lambda e, o=winsb[:, uc * 512: uc * 512 + cn],
                           i=pwin[:, 0:cn]: e.activation(out=o, in_=i, func=AF.Copy),
                           wait=[("pe", pe0), ("Wh", 1)], inc="act")
                        wev = self.cnt["act"]
                    self.dma("sp", "Wh",
                             AP(dr["win_hbm"].ap().tensor, qt * 128 * WIN,
                                [[WIN + 1, 128], [1, WIN]]),
                             winsb[:, :], wait=[("act", wev), ("Bt0", 1), ("Bt1", 1)])
                wh_all = self.cnt["Wh"]
                if self.dev.get("dbg5"):
                    self.dma("pool", "A2", dr["t_winsb"].ap(), winsb[:, :],
                             wait=[("Wh", wh_all)])
                    self.dma("pool", "A2", dr["t_relkT"].ap(), relkT[:, :],
                             wait=[("Wh", wh_all)])
                    if self.dev.get("dbg5") == 2:
                        self.on("sp", lambda e, o=bandT[:, 0:128],
                                i=AP(dr["win_hbm"].ap().tensor, 127, [[WIN - 1, 128], [1, 128]]):
                                e.dma_start_transpose(out=o, in_=i),
                                wait=[("Wh", wh_all)], inc="Bt", incn=16)
                        self.dma("pool", "A2", dr["t_bandT"].ap(), bandT[:, :],
                                 wait=[("Bt", self.cnt["Bt"])])
                    self.on("sp", None, wait=[("A2", self.cnt["A2"])])
                    self.barrier()
                    pstk.close()
                    stk.close()
                    return
                exp_done = self.cnt["act"]
                ptf = {0: 0, 1: 0}

                def emit_band(qc, kt, bb, wv):
                    sname = "Bt%d" % bb
                    ap_in = AP(dr["win_hbm"].ap().tensor,
                               127 + qc * 512 * WIN + kt * 128,
                               [[WIN, 512], [1, 128]])
                    self.on("sp", lambda e, o=bandT[:, bb * 512:(bb + 1) * 512],
                            i=ap_in: e.dma_start_transpose(out=o, in_=i),
                            wait=wv, inc=sname, incn=16)
                    return (sname, self.cnt[sname])
                for qc in range(4):
                    btd = {}
                    for kt in range(2):
                        btd[kt] = emit_band(qc, kt, kt % 2,
                                            [("Wh", wh_all), ("dve", self.cnt["dve"])])
                    exp_hist = [exp_done, exp_done]
                    mult_hist = [0, 0]
                    pend = None
                    for kt in range(NT1):
                        bb = kt % 2
                        pb = kt % 2
                        pscb = psc[:, bb * 512:(bb + 1) * 512]
                        scrb = sb["scr"][:, :] if bb == 0 else sb["scr2"][:, :]
                        on("pe", lambda e, o=pscb,
                           l=kT[:, kt * 128:(kt + 1) * 128],
                           r=qreb[:, qc * 512:(qc + 1) * 512]:
                           e.matmul(o, l, r, start=True, stop=True),
                           wait=[("dve", mult_hist[bb])], inc="pe")
                        pe0 = self.cnt["pe"]
                        if pend is not None:
                            pend()
                        on("dve", lambda e, o=sb["band32"][:, 0:512],
                           m=bandT[:, bb * 512:(bb + 1) * 512]:
                           e.tensor_copy(o, m),
                           wait=[btd[kt], ("dve", self.cnt["dve"])], inc="dve")
                        on("dve", lambda e, o=scrb,
                           m=sb["band32"][:, 0:512],
                           i=pscb: e.tensor_tensor(out=o, in0=m, in1=i, op=ALU.add),
                           wait=[("pe", pe0), ("act", exp_hist[bb])], inc="dve")
                        dv0 = mult_hist[bb] = self.cnt["dve"]
                        on("act", lambda e, o=pt[:, pb * 512:(pb + 1) * 512],
                           i=scrb, b_=sb["nb30"][:, 0:1]: e.activation(
                               out=o, in_=i, func=AF.Exp, bias=b_),
                           wait=[("dve", dv0), ("pe", ptf[pb])], inc="act")
                        exp_done = exp_hist[bb] = self.cnt["act"]

                        def mk_pend(kt=kt, pb=pb, exp_done=exp_done, qc=qc):
                            def pend_():
                                on("pe", lambda e, o=pwin[0:1, 0:512],
                                   l=sb["onesc"][:, 0:1],
                                   r=pt[:, pb * 512:(pb + 1) * 512],
                                   s=(kt == 0), t=(kt == NT1 - 1):
                                   e.matmul(o, l, r, start=s, stop=t),
                                   wait=[("act", exp_done)], inc="pe")
                                on("pe", lambda e, o=ppv[:, qc * 512:(qc + 1) * 512],
                                   l=vv[:, kt * 128:(kt + 1) * 128],
                                   r=pt[:, pb * 512:(pb + 1) * 512],
                                   s=(kt == 0), t=(kt == NT1 - 1):
                                   e.matmul(o, l, r, start=s, stop=t),
                                   wait=[("dve", vv_done)] if kt == 0 else (), inc="pe")
                                ptf[pb] = self.cnt["pe"]
                            return pend_
                        pend = mk_pend()
                        nkt = kt + 2
                        if nkt < NT1:
                            btd[nkt] = emit_band(qc, nkt, nkt % 2,
                                                 [("Wh", wh_all), ("dve", self.cnt["dve"])])
                    pend()
                    pend = None
                    pv_done = self.cnt["pe"]
                    on("dve", lambda e, o=rden[0:1, 0:512], i=pwin[0:1, 0:512]:
                       e.reciprocal(out=o, in_=i), wait=[("pe", pv_done)], inc="dve")
                    dv0 = self.cnt["dve"]
                    on("pe", lambda e, o=pwin[:, 0:512],
                       l=sb["onesb"][0:1, 0:128], r=rden[0:1, 0:512]:
                       e.matmul(o, l, r, start=True, stop=True),
                       wait=[("dve", dv0), ("act", self.cnt["act"])], inc="pe")
                    pe0 = self.cnt["pe"]
                    on("dve", lambda e, o=sb["repsb"][:, 0:512], i=pwin[:, 0:512]:
                       e.tensor_copy(o, i), wait=[("pe", pe0)], inc="dve")
                    dv1 = self.cnt["dve"]
                    on("dve", lambda e,
                       o=outT[:, h * S + qc * 512: h * S + (qc + 1) * 512],
                       i=ppv[:, qc * 512:(qc + 1) * 512],
                       r=sb["repsb"][:, 0:512]: e.tensor_tensor(
                           out=o, in0=i, in1=r, op=ALU.mult),
                       wait=[("dve", dv1)], inc="dve")
                    on("pe", None, wait=[("dve", self.cnt["dve"])])
                on("act", None, wait=[("dve", self.cnt["dve"])])
            heads_dve = self.cnt["dve"]
            if wout_ld is None:
                for fic in range(8):
                    dma("pool", "Lo", woutb[:, fic * D:(fic + 1) * D],
                        dr["bwout"].ap()[fic * 128:(fic + 1) * 128, :])
                wout_ld = self.cnt["Lo"]
            for st2 in range(NT1):
                for foc in range(2):
                    for fic in range(8):
                        on("pe", lambda e, o=psc[:, 0:512],
                           l=outT[:, fic * S + st2 * 128: fic * S + st2 * 128 + 128],
                           r=woutb[:, fic * D + foc * 512: fic * D + foc * 512 + 512],
                           s=(fic == 0): e.matmul(o, l, r, start=s, stop=False),
                           wait=[("Lo", wout_ld), ("dve", heads_dve)] if fic == 0 else (),
                           inc="pe")
                    on("pe", lambda e, o=psc[:, 0:512], l=sb["onesb"][0:1, 0:128],
                       r=sb["bbout_bf"][0:1, foc * 512:(foc + 1) * 512]:
                       e.matmul(o, l, r, start=False, stop=True, tile_position=(0, 0)),
                       inc="pe")
                    pe0 = self.cnt["pe"]
                    on("dve", lambda e,
                       o=x[:, st2 * D + foc * 512: st2 * D + foc * 512 + 512],
                       i=psc[:, 0:512]: e.tensor_tensor(out=o, in0=o, in1=i, op=ALU.add),
                       wait=[("pe", pe0)], inc="dve")
                    on("pe", None, wait=[("dve", self.cnt["dve"])])
            res_done = self.cnt["dve"]
            if export == "a2a":
                for th in range(2):
                    eap = AP(dr["a2a_in"].ap().tensor, b * SL * D + th * 128 * D,
                             [[D, 128], [BC * SL * D, NCORES], [1, D]])
                    sap = AP(x[:, :].tensor, th * D,
                             [[NT1 * D, 128], [2 * D, NCORES], [1, D]])
                    ex = self.dma("pool", "A2", eap, sap, wait=[("dve", res_done)])
            else:
                eap = AP(dr[export].ap().tensor, b * S * D, [[D, 128], [128 * D, NT1], [1, D]])
                sap = AP(x[:, :].tensor, 0, [[NT1 * D, 128], [D, NT1], [1, D]])
                ex = self.dma("pool", "A2", eap, sap, wait=[("dve", res_done)])
            prev_b = (res_done, ex)
        self.on("sp", None, wait=[("A2", self.cnt["A2"])])
        self.barrier()
        pstk.close()
        stk.close()

    def emit(self):
        dev = self.dev
        self.setup()
        phases = dev.get("phases",
                         ("attn_a", "a2a", "ffn1", "a2a", "attn_b", "a2a", "ffn2"))
        for ph in phases:
            if ph == "attn_a":
                self.attn_a(src=dev.get("attn_a_src", "x0"),
                            export=dev.get("attn_a_export", "a2a"))
            elif ph == "a2a":
                self.a2a()
            elif ph == "ffn1":
                self.ffn(dst=dev.get("ffn1_dst", "a2a_in"), src=dev.get("ffn1_src", "a2a_out"))
            elif ph == "ffn2":
                self.ffn(dst="y")
            elif ph == "attn_b":
                self.attn_b(src=dev.get("attn_b_src", "a2a_out"),
                            export=dev.get("attn_b_export", "a2a"))
        self.on("sp", None, wait=[(s0, 1) for s0 in
                                  ("Lg", "Lw", "LwB", "Lr", "Lp0", "Lp1", "Lo", "Lx0",
                                   "Lx1", "Ls", "St0", "St1", "Wh", "Bt0", "Bt1",
                                   "A2")])

    def build(self):
        self.declare()
        with self.st:
            self.alloc_global()
            self.PSm("tr", [128, 512], BF16)
            semstk = ExitStack()
            with semstk:
                for name in SEMS:
                    self.handles[name] = semstk.enter_context(self.nc.semaphore(name))
                self.emit()
                nc = self.nc
                with nc.allow_low_precision(reason="bf16 operand staging is intentional"), \
                     nc.Block() as blk:
                    @blk.tensor
                    def _(e):
                        for t in self.q["pe"]:
                            t(e)

                    @blk.vector
                    def _(e):
                        for t in self.q["dve"]:
                            t(e)

                    @blk.scalar
                    def _(e):
                        for t in self.q["act"]:
                            t(e)

                    @blk.gpsimd
                    def _(e):
                        for t in self.q["pool"]:
                            t(e)

                    @blk.sync
                    def _(e):
                        for t in self.q["sp"]:
                            t(e)
        return self.nc


_CONSTS = None


def _consts():
    global _CONSTS
    if _CONSTS is None:
        eye = np.eye(128, dtype=ml_dtypes.bfloat16)
        posT = _positions_T()
        absd = np.abs(np.arange(128, dtype=np.float32)[:, None]
                      - np.arange(128, dtype=np.float32)[None, :])
        _CONSTS = dict(eye=eye, posT=posT, absd=absd)
    return _CONSTS


def _in_maps(inputs, extra=None):
    c = _consts()
    x = np.ascontiguousarray(np.asarray(inputs["L0_attn_a_x"], dtype=np.float32))
    base = dict(
        aWq=inputs["L0_attn_a_Wq"], aWk=inputs["L0_attn_a_Wk"],
        aWv=inputs["L0_attn_a_Wv"], aWo=inputs["L0_attn_a_Wo"],
        abq=inputs["L0_attn_a_bq"], abk=inputs["L0_attn_a_bk"],
        abv=inputs["L0_attn_a_bv"], abo=inputs["L0_attn_a_bo"],
        adecay=inputs["L0_attn_a_decay_params"], amix=inputs["L0_attn_a_mix_weights"],
        bwq=inputs["L0_attn_b_w_qs"], bwk=inputs["L0_attn_b_w_ks"],
        bwv=inputs["L0_attn_b_w_vs"], bwrel=inputs["L0_attn_b_w_rel_pos_k"],
        breb=np.asarray(inputs["L0_attn_b_rel_embed_bias"]).reshape(H2, DH2),
        brpb=np.asarray(inputs["L0_attn_b_rel_pos_bias"]).reshape(H2, DH2),
        bwout=inputs["L0_attn_b_w_out"], bbout=inputs["L0_attn_b_b_out"],
        fwf=inputs["L0_ffn_wf"], fuf=inputs["L0_ffn_uf"], fbf=inputs["L0_ffn_bf"],
        fwi=inputs["L0_ffn_wi"], fui=inputs["L0_ffn_ui"], fbi=inputs["L0_ffn_bi"],
        fwc=inputs["L0_ffn_wc"], fuc=inputs["L0_ffn_uc"], fbc=inputs["L0_ffn_bc"],
        fwo=inputs["L0_ffn_wo"], fuo=inputs["L0_ffn_uo"], fbo=inputs["L0_ffn_bo"],
        finc=inputs["L0_ffn_in_c"],
        posT=c["posT"], absd=c["absd"],
    )
    base = {k: np.ascontiguousarray(np.asarray(v, dtype=np.float32))
            for k, v in base.items()}
    base["eye"] = c["eye"]
    maps = []
    for core in range(NCORES):
        m = dict(base)
        m["x0"] = np.ascontiguousarray(x[2 * core:2 * core + 2].reshape(BC * S, D))
        if extra:
            m.update(extra(core))
        maps.append(m)
    return maps


def kernel(**inputs):
    nc = MK().build()
    maps = _in_maps(inputs)
    res = run_bass_kernel_spmd(nc, maps, core_ids=list(range(NCORES)))
    out = np.empty((B, S, D), dtype=np.float32)
    for core in range(NCORES):
        yv = res.results[core]["y"].reshape(TS, SL, D)
        out[:, core * SL:(core + 1) * SL, :] = yv
    return out
